# version 03
# baseline (speedup 1.0000x reference)
import os as _os
import numpy as np
from contextlib import ExitStack, contextmanager

import concourse.bass as bass
import concourse.mybir as mybir
from concourse.bass_utils import run_bass_kernel_spmd

BF16 = mybir.dt.bfloat16
F32 = mybir.dt.float32
AF = mybir.ActivationFunctionType
ALU = mybir.AluOpType
AX = mybir.AxisListType
ET = mybir.EngineType

NCORES = 8
L = 2048
D = 2048
NT = L // 128
GT = 4
NG = NT // GT
KD = D // 128


class Sem:

    def __init__(self, prog, name):
        self.name = name
        self.h = prog.es.enter_context(prog.nc.semaphore(name))
        self.c = 0

    def inc(self, inst, n=1):
        inst.then_inc(self.h, n)
        self.c += n
        return inst

    def cur(self, extra=0):
        return self.c + extra

    def wait(self, eng, extra=0):
        v = self.c + extra
        if v > 0:
            eng.wait_ge(self.h, v)


class Prog:

    CHUNK = 3500

    def __init__(self, nc):
        self.nc = nc
        self.es = ExitStack()
        self.sems = []
        self._chunk_n = 0
        self._next_break = self.CHUNK

    def sem(self, name):
        s = Sem(self, name)
        self.sems.append(s)
        return s

    def maybe_chunk(self):
        n = self.nc.n_instructions()
        if n >= self._next_break:
            self._chunk_n += 1
            nm = f"chunk{self._chunk_n}"
            self.nc.br(nm)
            self.nc.switch_bb(nm)
            self._next_break = self.nc.n_instructions() + self.CHUNK


class Arena:

    def __init__(self, nc, nbytes):
        self.nc = nc
        t = nc.alloc_sbuf_tensor(f"arena{nbytes}", [128, nbytes // 2],
                                 BF16).ap()
        self.base = nc.lookup_mloc(t.tensor).addr
        self.nbytes = nbytes
        self.cur = 0
        self.pfx = ""

    def scope(self, pfx):
        self.cur = 0
        self.pfx = pfx
        return self

    def __call__(self, name, shape, dt):
        esz = 2 if dt == BF16 else 4
        nb = 1
        for d in shape[1:]:
            nb *= d
        nb *= esz
        off = self.base + self.cur
        self.cur += (nb + 127) // 128 * 128
        assert self.cur <= self.nbytes, f"arena overflow at {name}: {self.cur}"
        return self.nc.alloc_sbuf_tensor_at(self.pfx + name, shape, dt,
                                            offset=off).ap()


def emit_sublayer_barrier(P, E):
    nc = P.nc
    engs = [nc.tensor, nc.vector, nc.scalar, nc.sync, nc.gpsimd]
    for k in ["pe", "dve", "act", "lds"]:
        for eng in engs:
            E[k].wait(eng)


def reg_consts(nc, vals, dt=F32):
    for i, v in enumerate(vals):
        t = nc.alloc_sbuf_tensor(f"cstv_{i}", [128, 1], dt).ap()
        nc.gpsimd.memset(t[:], float(v))
        nc.const_aps.register(dt, float(v), t[:])


def emit_constants(P, c):
    nc = P.nc
    g = nc.gpsimd
    if "ident" in c:
        g.memset(c["ident"][:], 0.0)
        g.affine_select(out=c["ident"][:], in_=c["ident"][:],
                        compare_op=ALU.not_equal, fill=1.0, base=0,
                        pattern=[[-1, 128]], channel_multiplier=1)
    if "ones1" in c:
        g.memset(c["ones1"][:], 1.0)
    if "triu" in c:
        g.memset(c["triu"][:], 0.0)
        g.affine_select(out=c["triu"][:], in_=c["triu"][:], compare_op=ALU.is_gt,
                        fill=1.0, base=0, pattern=[[-1, 128]], channel_multiplier=1)
    if "maskneg" in c:
        g.memset(c["maskneg"][:], 0.0)
        g.affine_select(out=c["maskneg"][:], in_=c["maskneg"][:],
                        compare_op=ALU.is_ge, fill=-1e30, base=0,
                        pattern=[[1, 128]], channel_multiplier=-1)
    if "e0mat" in c:
        g.memset(c["e0mat"][:], 0.0)
        g.affine_select(out=c["e0mat"][:], in_=c["e0mat"][:],
                        compare_op=ALU.not_equal, fill=1.0, base=0,
                        pattern=[[0, 128]], channel_multiplier=1)
    if "lmask8" in c:
        g.memset(c["lmask8"][:], 0.0)
        g.affine_select(out=c["lmask8"][:], in_=c["lmask8"][:],
                        compare_op=ALU.is_ge, fill=1.0, base=0,
                        pattern=[[-1, 8], [1, 8]], channel_multiplier=0)
    if "e127" in c:
        g.memset(c["e127"][:], 0.0)
        g.affine_select(out=c["e127"][:], in_=c["e127"][:],
                        compare_op=ALU.not_equal, fill=1.0, base=-127,
                        pattern=[[0, 128]], channel_multiplier=1)
    if "e127f" in c:
        g.memset(c["e127f"][:], 0.0)
        g.affine_select(out=c["e127f"][:], in_=c["e127f"][:],
                        compare_op=ALU.not_equal, fill=1.0, base=-127,
                        pattern=[[0, 128]], channel_multiplier=1)
    if "identf" in c:
        g.memset(c["identf"][:], 0.0)
        g.affine_select(out=c["identf"][:], in_=c["identf"][:],
                        compare_op=ALU.not_equal, fill=1.0, base=0,
                        pattern=[[-1, 128]], channel_multiplier=1)
    if "onesf" in c:
        g.memset(c["onesf"][:], 1.0)


def make_wscratch(nc, name, din, dout, oc=512):
    n_k, n_oc = din // 128, (dout + oc - 1) // oc
    t = nc.dram_tensor(name, [n_oc, 128, n_k, oc], BF16, kind="Internal")
    return t.ap()


def emit_precast(P, E, jobs, buf):
    nc = P.nc
    CB = buf.shape[2]
    lds, sts = E["pl_ld"], E["pl_st"]
    work = []
    for src, dst in jobs:
        din, dout = src.shape
        n_oc, _, n_k, OC = dst.shape
        assert n_k * 128 == din
        for r in range(n_k):
            for c0 in range(0, dout, CB):
                cw = min(CB, dout - c0)
                work.append((src, dst, r, c0, cw, OC))
    for idx, (src, dst, r, c0, cw, OC) in enumerate(work):
        slot = idx % 2
        if idx >= 2:
            nc.gpsimd.wait_ge(sts.h, sts.cur(-16))
        lds.inc(nc.gpsimd.dma_start(
            out=buf[:, slot, 0:cw],
            in_=src[r * 128:(r + 1) * 128, c0:c0 + cw]), 16)
        lds.wait(nc.sync)
        o0 = c0 // OC
        if cw % OC == 0:
            dv = dst[o0:o0 + cw // OC, :, r, :].rearrange("o p i -> p o i")
            bv = buf[:, slot, 0:cw].rearrange("p (o i) -> p o i", i=OC)
        else:
            dv = dst[o0:o0 + 1, :, r, 0:cw].rearrange("o p i -> p o i")
            bv = buf[:, slot, 0:cw].rearrange("p (o i) -> p o i", i=cw)
        sts.inc(nc.sync.dma_start(out=dv, in_=bv), 16)
    sts.wait(nc.sync)


def emit_ln_tile(P, E, x_src, xn_bf, t, eps=1e-5, xn_war=()):
    nc = P.nc
    dve, act = E["dve"], E["act"]
    n = x_src.shape[-1]
    dve.inc(nc.vector.reduce_sum(t["sum"][:], x_src, axis=AX.X))
    dve.wait(nc.vector)
    dve.inc(nc.vector.tensor_scalar_mul(t["mean"][:], t["sum"][:], 1.0 / n))
    dve.wait(nc.vector)
    dve.inc(nc.vector.tensor_scalar(t["m2"][:], t["mean"][:], t["mean"][:],
                                    None, ALU.mult))
    dve.wait(nc.scalar)
    act.inc(nc.scalar.activation(t["junk"][:, 0:n], x_src, AF.Square,
                                 accum_out=t["sq"][:]))
    act.wait(nc.vector)
    dve.wait(nc.vector)
    dve.inc(nc.vector.tensor_scalar(t["var"][:], t["sq"][:], 1.0 / n,
                                    t["m2"][:], ALU.mult, ALU.subtract))
    dve.wait(nc.scalar)
    act.inc(nc.scalar.activation(t["sd"][:], t["var"][:], AF.Sqrt, bias=eps))
    act.wait(nc.vector)
    dve.inc(nc.vector.reciprocal(t["rstd"][:], t["sd"][:]))
    dve.wait(nc.vector)
    for s in xn_war:
        s.wait(nc.vector)
    dve.inc(nc.vector.tensor_scalar(xn_bf[:, 0:n], x_src, t["mean"][:],
                                    t["rstd"][:], ALU.subtract, ALU.mult))


def emit_transposes(P, E, src_bf, n, ptile, ident, dst_fn=None,
                    dst_batch_fn=None, src_off=0):
    nc = P.nc
    pe, dve = E["pe"], E["dve"]
    S = ptile.shape[1]
    j = 0
    while j < n:
        m = min(S, n - j)
        dve.wait(nc.tensor)
        for k in range(m):
            ins = nc.tensor.transpose(
                ptile[:, k, :],
                src_bf[:, src_off + (j + k) * 128: src_off + (j + k + 1) * 128],
                ident[:])
        pe.inc(ins)
        pe.wait(nc.vector)
        if dst_batch_fn is not None:
            ins = nc.vector.tensor_copy(dst_batch_fn(j, m), ptile[:, 0:m, :])
        else:
            for k in range(m):
                ins = nc.vector.tensor_copy(dst_fn(j + k), ptile[:, k, :])
        dve.inc(ins)
        j += m


def emit_sparsify(P, E, h, habs, hc, hr, W=512, out_bf=None):
    nc = P.nc
    dve, act = E["dve"], E["act"]
    dve.wait(nc.scalar)
    act.inc(nc.scalar.activation(habs[:, 0:W], h[:, 0:W], AF.Abs))
    act.wait(nc.vector)
    q = W // 4
    av = [habs[:, r:W:4] for r in range(4)]
    c01, c02, c03 = hc[:, 0, 0:q], hc[:, 1, 0:q], hc[:, 2, 0:q]
    c12, c13, c23 = hc[:, 3, 0:q], hc[:, 4, 0:q], hc[:, 5, 0:q]
    nc.vector.tensor_tensor(c01, av[0], av[1], ALU.is_ge)
    nc.vector.tensor_tensor(c02, av[0], av[2], ALU.is_ge)
    nc.vector.tensor_tensor(c03, av[0], av[3], ALU.is_ge)
    nc.vector.tensor_tensor(c12, av[1], av[2], ALU.is_ge)
    nc.vector.tensor_tensor(c13, av[1], av[3], ALU.is_ge)
    nc.vector.tensor_tensor(c23, av[2], av[3], ALU.is_ge)
    t0, t1, t2, t3 = (hr[:, i, 0:q] for i in range(4))
    nc.vector.tensor_add(t0, c01, c02)
    nc.vector.tensor_add(t0, t0, c03)
    nc.vector.tensor_add(t1, c12, c13)
    nc.vector.tensor_sub(t1, t1, c01)
    nc.vector.tensor_sub(t2, c23, c02)
    nc.vector.tensor_sub(t2, t2, c12)
    nc.vector.tensor_add(t3, c03, c13)
    nc.vector.tensor_add(t3, t3, c23)
    hv = [h[:, r:W:4] for r in range(4)]
    ov = [(out_bf[:, r:W:4] if out_bf is not None else hv[r]) for r in range(4)]
    nc.vector.scalar_tensor_tensor(ov[0], t0, 1.5, hv[0], ALU.is_ge, ALU.mult)
    nc.vector.scalar_tensor_tensor(ov[1], t1, 0.5, hv[1], ALU.is_ge, ALU.mult)
    nc.vector.scalar_tensor_tensor(ov[2], t2, -0.5, hv[2], ALU.is_ge, ALU.mult)
    dve.inc(nc.vector.scalar_tensor_tensor(ov[3], t3, 1.5, hv[3],
                                           ALU.is_le, ALU.mult))


def emit_l0ffn_seq(P, E, C, B, PS, wg_s, wu_s, wd_s, xsb, io=None):
    nc = P.nc
    pe, dve, act, lds = E["pe"], E["dve"], E["act"], E["lds"]
    NOC = 16
    p4, pt = PS["p4"], PS["pt"]
    ident = C["ident"]
    lnt = B["lnt"]
    for g in range(NG):
        if io is not None:
            x_in, _ = io
            for tt in range(GT):
                dve.wait(nc.sync)
                lds.inc(nc.sync.dma_start(out=B["xstage"][:],
                                          in_=x_in[g * GT + tt]), 16)
                lds.wait(nc.vector)
                dve.inc(nc.vector.tensor_copy(xsb[:, g * GT + tt, :],
                                              B["xstage"][:]))
        for tt in range(GT):
            emit_ln_tile(P, E, xsb[:, g * GT + tt, :], B["xn"], lnt,
                         xn_war=(pe,))

            emit_transposes(
                P, E, B["xn"], KD, pt[:, tt % 2], ident,
                dst_batch_fn=lambda j, m, tt=tt: B["xT"][:, tt, j:j + m, :])
        for oc in range(NOC):
            P.maybe_chunk()
            pe.wait(nc.sync)
            lds.inc(nc.sync.dma_start(out=B["wg"][:], in_=wg_s[oc]), 16)
            lds.wait(nc.sync)
            lds.inc(nc.sync.dma_start(out=B["wu"][:], in_=wu_s[oc]), 16)
            lds.wait(nc.sync)
            if "dbg_wg" in B and g == 0 and oc == 0:
                lds.wait(nc.sync)
                E["lds"].inc(nc.sync.dma_start(out=B["dbg_wg"],
                                               in_=B["wg"][:]), 16)
                E["lds"].inc(nc.sync.dma_start(out=B["dbg_xT"],
                                               in_=B["xT"][:, 0, :, :]), 16)
                lds.wait(nc.sync)
            for tt in range(GT):
                pg = p4[:, tt % 2, :]
                pu = p4[:, 2 + tt % 2, :]
                act.wait(nc.tensor)
                dve.wait(nc.tensor)
                lds.wait(nc.tensor)
                for k in range(KD):
                    ins = nc.tensor.matmul(pg, B["xT"][:, tt, k, :],
                                           B["wg"][:, k, :],
                                           start=(k == 0), stop=(k == KD - 1))
                pe.inc(ins)
                for k in range(KD):
                    ins = nc.tensor.matmul(pu, B["xT"][:, tt, k, :],
                                           B["wu"][:, k, :],
                                           start=(k == 0), stop=(k == KD - 1))
                pe.inc(ins)
                pe.wait(nc.scalar, -1)
                act.inc(nc.scalar.activation(B["sg"][:, 0, :], pg, AF.Silu))
                pe.wait(nc.vector)
                act.wait(nc.vector)
                dve.inc(nc.vector.scalar_tensor_tensor(
                    B["hstf"][:, 0, :], pu, 1.0, B["sg"][:, 0, :],
                    ALU.mult, ALU.mult))
                if "dbg_hstf" in B and g == 0 and tt == 0 and oc == 0:
                    dve.wait(nc.sync)
                    E["lds"].inc(nc.sync.dma_start(out=B["dbg_hstf"],
                                                   in_=B["hstf"][:, 0, :]), 16)
                    E["lds"].wait(nc.vector)
                emit_sparsify(P, E, B["hstf"][:, 0], B["habs"][:, 0],
                              B["hc"], B["hr"], out_bf=B["hst"][:, tt % 2])
                if "dbg_hst" in B and g == 0 and tt == 0 and oc == 0:
                    dve.wait(nc.sync)
                    E["lds"].inc(nc.sync.dma_start(out=B["dbg_hst"],
                                                   in_=B["hst"][:, tt % 2, :]), 16)
                    E["lds"].wait(nc.vector)
                dve.wait(nc.tensor)
                ptb = pt[:, tt % 2]
                for s4 in range(4):
                    ins = nc.tensor.transpose(
                        ptb[:, s4, :],
                        B["hst"][:, tt % 2, s4 * 128:(s4 + 1) * 128], ident[:])
                pe.inc(ins)
                pe.wait(nc.vector)
                dve.inc(nc.vector.tensor_copy(
                    B["hsT"][:, tt, oc * 4:oc * 4 + 4, :], ptb[:, 0:4, :]))
        for o2 in range(4):
            P.maybe_chunk()
            for piece in range(8):
                pe.wait(nc.sync)
                if piece == 0:
                    act.wait(nc.tensor)
                    dve.wait(nc.tensor)
                lds.inc(nc.sync.dma_start(
                    out=B["wd"][:],
                    in_=wd_s[o2, :, piece * 8:(piece + 1) * 8, :]), 16)
                lds.wait(nc.sync)
                lds.wait(nc.tensor)
                for tt in range(GT):
                    for k in range(8):
                        ins = nc.tensor.matmul(
                            p4[:, tt, :], B["hsT"][:, tt, piece * 8 + k, :],
                            B["wd"][:, k, :],
                            start=(piece == 0 and k == 0),
                            stop=(piece == 7 and k == 7))
                    if tt == GT - 1:
                        pe.inc(ins)
            for tt in range(GT):
                pe.wait(nc.vector)
                xap = xsb[:, g * GT + tt, o2 * 512:(o2 + 1) * 512]
                dve.inc(nc.vector.scalar_tensor_tensor(
                    xap, p4[:, tt, :], 0.5, xap, ALU.mult, ALU.add))
        if io is not None:
            for tt in range(GT):
                lds.wait(nc.vector)
                dve.inc(nc.vector.tensor_copy(B["xstage"][:],
                                              xsb[:, g * GT + tt, :]))
                dve.wait(nc.sync)
                lds.inc(nc.sync.dma_start(out=io[1][g * GT + tt],
                                          in_=B["xstage"][:]), 16)
            lds.wait(nc.sync)


def emit_attn_seq(P, E, C, B, PS, wq_s, wkv_s, wo_s, xsb, io=None):
    nc = P.nc
    pe, dve, act, lds = E["pe"], E["dve"], E["act"], E["lds"]
    p4, pt, pav = PS["p4"], PS["pt"], PS["pav"]
    ident = C["ident"]
    lnt = B["lnt"]

    if io is not None:
        x_in, _ = io
        for t in range(NT):
            dve.wait(nc.sync)
            lds.inc(nc.sync.dma_start(out=B["xstage"][:, 0:D], in_=x_in[t]), 16)
            lds.wait(nc.vector)
            dve.inc(nc.vector.tensor_copy(xsb[:, t, :], B["xstage"][:, 0:D]))

    dve.wait(nc.vector)
    dve.inc(nc.vector.tensor_mul(B["scl"][:], B["nsc"][:], B["ssc"][:]))
    pe.wait(nc.sync)
    dve.wait(nc.sync)
    lds.inc(nc.sync.dma_start(out=B["wkv"][:], in_=wkv_s[0]), 16)
    lds.wait(nc.sync)

    for g in range(NG):
        P.maybe_chunk()
        for tt in range(GT):
            t_idx = g * GT + tt
            emit_ln_tile(P, E, xsb[:, t_idx, :], B["xn"], lnt, xn_war=(pe,))
            dve.wait(nc.scalar)
            act.inc(nc.scalar.activation(lnt["junk"][:, :], B["xn"][:],
                                         AF.Square, accum_out=lnt["sq"][:]))
            dve.wait(nc.scalar)
            act.inc(nc.scalar.activation(lnt["sd"][:], lnt["sq"][:], AF.Sqrt,
                                         scale=1.0 / D))
            act.wait(nc.vector)
            dve.inc(nc.vector.tensor_scalar_add(lnt["sd"][:], lnt["sd"][:], 1e-8))
            dve.wait(nc.vector)
            dve.inc(nc.vector.reciprocal(lnt["rstd"][:], lnt["sd"][:]))
            dve.wait(nc.vector)
            pe.wait(nc.vector)
            dve.inc(nc.vector.tensor_scalar(B["un"][:, 0:D], B["xn"][:],
                                            lnt["rstd"][:], None, ALU.mult))
            for half in range(2):
                dve.wait(nc.tensor)
                for k in range(8):
                    kk = half * 8 + k
                    ins = nc.tensor.transpose(
                        pt[:, tt % 2, k, :],
                        B["un"][:, kk * 128:(kk + 1) * 128], ident[:])
                pe.inc(ins)
                pe.wait(nc.vector)
                for k in range(8):
                    kk = half * 8 + k
                    ins = nc.vector.tensor_scalar(
                        B["uTall"][:, kk, t_idx * 128:(t_idx + 1) * 128],
                        pt[:, tt % 2, k, :], B["scl"][:, kk:kk + 1],
                        None, ALU.mult)
                dve.inc(ins)
        gc = slice(g * 512, (g + 1) * 512)
        act.wait(nc.tensor)
        dve.wait(nc.tensor)
        lds.wait(nc.tensor)
        for k in range(KD):
            ins = nc.tensor.matmul(p4[:, 0, :], B["wkv"][:, k, 0:128],
                                   B["uTall"][:, k, gc],
                                   start=(k == 0), stop=(k == KD - 1))
        pe.inc(ins)
        pe.wait(nc.vector)
        lds.wait(nc.vector)
        dve.inc(nc.vector.tensor_copy(B["ktmp"][:], p4[:, 0, :]))
        dve.wait(nc.vector)
        nc.vector.tensor_copy(B["kTboth"][0:64, 0, gc], B["ktmp"][0:64, :])
        dve.inc(nc.vector.tensor_copy(B["kTboth"][64:128, 1, gc],
                                      B["ktmp"][64:128, :]))
        dve.wait(nc.sync)
        lds.inc(nc.sync.dma_start(out=B["kTboth"][64:128, 0, gc],
                                  in_=B["ktmp"][0:64, :]), 16)
        lds.inc(nc.sync.dma_start(out=B["kTboth"][0:64, 1, gc],
                                  in_=B["ktmp"][64:128, :]), 16)
        for tt in range(GT):
            t_idx = g * GT + tt
            dve.wait(nc.tensor)
            for k in range(KD):
                ins = nc.tensor.matmul(
                    p4[:, 1, 0:128], B["uTall"][:, k, t_idx * 128:(t_idx + 1) * 128],
                    B["wkv"][:, k, 128:256], start=(k == 0), stop=(k == KD - 1))
            pe.inc(ins)
            pe.wait(nc.vector)
            dve.inc(nc.vector.tensor_copy(B["vv"][:, t_idx, :], p4[:, 1, 0:128]))

    for g in range(NG):
        gc = slice(g * 512, (g + 1) * 512)
        for ocq in range(4):
            P.maybe_chunk()
            pe.wait(nc.sync)
            lds.inc(nc.sync.dma_start(out=B["wqb"][:], in_=wq_s[ocq]), 16)
            lds.wait(nc.sync)
            lds.wait(nc.tensor)
            for sub in range(4):
                qfc = ocq * 4 + sub
                dve.wait(nc.tensor)
                for k in range(KD):
                    ins = nc.tensor.matmul(
                        p4[:, sub, :], B["wqb"][:, k, sub * 128:(sub + 1) * 128],
                        B["uTall"][:, k, gc], start=(k == 0), stop=(k == KD - 1))
                pe.inc(ins)
                pe.wait(nc.vector)
                dve.inc(nc.vector.tensor_copy(B["qTg"][:, qfc, :], p4[:, sub, :]))
        for st in range(GT):
            P.maybe_chunk()
            t_idx = g * GT + st
            for h in range(32):
                b = (h % 2) * 64
                qfc = h // 2
                hg = h // 16
                act.wait(nc.tensor)
                dve.wait(nc.tensor)
                for tc in range(4):
                    ins = nc.tensor.matmul(
                        p4[b:b + 64, tc, :] if False else p4[:, tc, :],
                        B["qTg"][b:b + 64, qfc, st * 128:(st + 1) * 128],
                        B["kTboth"][b:b + 64, hg, tc * 512:(tc + 1) * 512],
                        start=True, stop=True)
                pe.inc(ins)
                pe.wait(nc.vector)
                dve.inc(nc.vector.reduce_max(B["mc"][:, 0:4],
                                             p4[:, :, :], axis=AX.X))
                dve.wait(nc.vector)
                dve.inc(nc.vector.reduce_max(B["mc"][:, 4:5], B["mc"][:, 0:4],
                                             axis=AX.X))
                dve.wait(nc.vector)
                dve.inc(nc.vector.tensor_scalar_mul(B["mc"][:, 5:6],
                                                    B["mc"][:, 4:5], -0.125))
                dve.wait(nc.scalar)
                for tc in range(4):
                    ins = nc.scalar.activation(B["pb"][:, tc, :], p4[:, tc, :],
                                               AF.Exp, scale=0.125,
                                               bias=B["mc"][:, 5:6])
                act.inc(ins)
                act.wait(nc.vector)
                dve.inc(nc.vector.reduce_sum(B["mc"][:, 6:7],
                                             B["pb"][:, :, :], axis=AX.XY))
                dve.wait(nc.vector)
                dve.inc(nc.vector.reciprocal(B["mc"][:, 7:8], B["mc"][:, 6:7]))
                dve.wait(nc.tensor)
                act.wait(nc.tensor)
                for tk in range(16):
                    ins = nc.tensor.transpose(
                        pt[:, tk // 8, tk % 8, :],
                        B["pb"][:, :, :].rearrange("p a b -> p (a b)")
                        [:, tk * 128:(tk + 1) * 128], ident[:])
                    if tk == 7 or tk == 15:
                        pe.inc(ins)
                pe.wait(nc.vector, -1)
                nc.vector.tensor_copy(B["PT"][:, 0:8, :], pt[:, 0, :, :])
                pe.wait(nc.vector)
                dve.inc(nc.vector.tensor_copy(B["PT"][:, 8:16, :], pt[:, 1, :, :]))
                dve.wait(nc.tensor)
                for tk in range(16):
                    ins = nc.tensor.matmul(
                        pav[:, 0:64], B["PT"][:, tk, :],
                        B["vv"][:, tk, hg * 64:(hg + 1) * 64],
                        start=(tk == 0), stop=(tk == 15))
                pe.inc(ins)
                pe.wait(nc.vector)
                dve.wait(nc.vector)
                dve.inc(nc.vector.tensor_scalar(
                    B["ao"][:, h * 64:(h + 1) * 64], pav[:, 0:64],
                    B["mc"][:, 7:8], None, ALU.mult))
            for half in range(2):
                dve.wait(nc.tensor)
                for k in range(8):
                    kk = half * 8 + k
                    ins = nc.tensor.transpose(
                        pt[:, st % 2, k, :],
                        B["ao"][:, kk * 128:(kk + 1) * 128], ident[:])
                pe.inc(ins)
                pe.wait(nc.vector)
                for k in range(8):
                    kk = half * 8 + k
                    ins = nc.vector.tensor_copy(B["aoTg"][:, kk, st, :],
                                                pt[:, st % 2, k, :])
                dve.inc(ins)
        for tt in range(GT):
            t_idx = g * GT + tt
            emit_ln_tile(P, E, xsb[:, t_idx, :], B["xn"], lnt, xn_war=(pe,))
            dve.wait(nc.vector)
            dve.inc(nc.vector.tensor_add(xsb[:, t_idx, :], xsb[:, t_idx, :],
                                         B["xn"][:]))
        for oc in range(4):
            P.maybe_chunk()
            pe.wait(nc.sync)
            lds.inc(nc.sync.dma_start(out=B["wqb"][:], in_=wo_s[oc]), 16)
            lds.wait(nc.sync)
            lds.wait(nc.tensor)
            for tt in range(GT):
                t_idx = g * GT + tt
                act.wait(nc.tensor)
                dve.wait(nc.tensor)
                for k in range(KD):
                    ins = nc.tensor.matmul(
                        p4[:, tt, :], B["aoTg"][:, k, tt, :], B["wqb"][:, k, :],
                        start=(k == 0), stop=(k == KD - 1))
                pe.inc(ins)
                pe.wait(nc.vector)
                xap = xsb[:, t_idx, oc * 512:(oc + 1) * 512]
                dve.inc(nc.vector.scalar_tensor_tensor(
                    xap, p4[:, tt, :], 1.0, xap, ALU.mult, ALU.add))

    if io is not None:
        for t in range(NT):
            lds.wait(nc.vector)
            dve.inc(nc.vector.tensor_copy(B["xstage"][:, 0:D], xsb[:, t, :]))
            if _os.environ.get("K_MOE_SLOWST"):
                for _ in range(10):
                    dve.inc(nc.vector.tensor_copy(B["gstage"][:, :],
                                                  B["gstage"][:, :]))
            dve.wait(nc.sync)
            lds.inc(nc.sync.dma_start(out=io[1][t], in_=B["xstage"][:, 0:D]), 16)
        lds.wait(nc.sync)


def alloc_attn_buffers(nc, B, al=None):
    if al is None:
        al = lambda n, sh, dt: nc.alloc_sbuf_tensor(n, sh, dt).ap()
    B["uTall"] = al("uTall", [128, KD, L], BF16)
    B["kTboth"] = al("kTboth", [128, 2, L], BF16)
    B["vv"] = al("vv", [128, NT, 128], BF16)
    B["wqb"] = al("wqb", [128, 16, 512], BF16)
    B["qTg"] = al("qTg", [128, 16, 512], BF16)
    B["PT"] = al("PT", [128, 16, 128], BF16)
    B["aoTg"] = al("aoTg", [128, KD, GT, 128], BF16)
    B["mc"] = al("mc", [128, 8], F32)
    B["scl"] = al("scl", [128, 16], F32)
    B["nsc"] = al("nsc_sb", [128, 16], F32)
    B["ssc"] = al("ssc_sb", [128, 16], F32)
    B["wkv"] = nc.alloc_sbuf_tensor_at(
        "wkvb", [128, 16, 256], BF16,
        offset=nc.lookup_mloc(B["qTg"].tensor).addr).ap()
    B["ktmp"] = nc.alloc_sbuf_tensor_at(
        "ktmp", [128, 512], BF16,
        offset=nc.lookup_mloc(B["PT"].tensor).addr).ap()
    B["pb"] = nc.alloc_sbuf_tensor_at(
        "pb", [128, 4, 512], BF16,
        offset=nc.lookup_mloc(B["xstage"].tensor).addr).ap()
    B["ao"] = nc.alloc_sbuf_tensor_at(
        "ao", [128, D], BF16,
        offset=nc.lookup_mloc(B["xstage"].tensor).addr + 4096).ap()
    B["un"] = nc.alloc_sbuf_tensor_at(
        "un", [128, D], BF16,
        offset=nc.lookup_mloc(B["xstage"].tensor).addr + 4096).ap()
    return B


def build_attn_standalone(nseq=1):
    nc = bass.Bass("TRN2", target_bir_lowering=False)
    P = Prog(nc)
    x = nc.dram_tensor("x", [nseq, L, D], F32, kind="ExternalInput").ap()
    wq_f = nc.dram_tensor("wqT", [D, D], F32, kind="ExternalInput").ap()
    wkv_f = nc.dram_tensor("wkvT", [D, 256], F32, kind="ExternalInput").ap()
    wo_f = nc.dram_tensor("woT", [D, D], F32, kind="ExternalInput").ap()
    nsc_f = nc.dram_tensor("nsc", [128, 16], F32, kind="ExternalInput").ap()
    ssc_f = nc.dram_tensor("ssc", [128, 16], F32, kind="ExternalInput").ap()
    y = nc.dram_tensor("y", [nseq, L, D], F32, kind="ExternalOutput").ap()
    wq_s = make_wscratch(nc, "wq_s", D, D)
    wkv_s = make_wscratch(nc, "wkv_s", D, 256, oc=256)
    wo_s = make_wscratch(nc, "wo_s", D, D)

    C = {}
    C["ident"] = nc.alloc_sbuf_tensor("ident", [128, 128], BF16).ap()
    C["ones1"] = nc.alloc_sbuf_tensor("ones1", [1, 128], BF16).ap()
    B = {}
    B["xsb"] = nc.alloc_sbuf_tensor("xsb", [128, NT, D], BF16).ap()
    B["xn"] = nc.alloc_sbuf_tensor("xn", [128, D], BF16).ap()
    B["xstage"] = nc.alloc_sbuf_tensor("xstage", [128, D], F32).ap()
    lnt = {}
    for nm in ["sum", "mean", "m2", "sq", "var", "sd", "rstd"]:
        lnt[nm] = nc.alloc_sbuf_tensor("ln_" + nm, [128, 1], F32).ap()
    lnt["junk"] = B["xstage"]
    B["lnt"] = lnt
    alloc_attn_buffers(nc, B)
    xst_mloc = nc.lookup_mloc(B["xstage"].tensor)
    pcb = nc.alloc_sbuf_tensor_at("pcbuf", [128, 2, 1024], BF16,
                                  offset=xst_mloc.addr).ap()

    PS = {}
    PS["p4"] = nc.alloc_psum_tensor("p4", [128, 4, 512], F32).ap()
    PS["pav"] = nc.alloc_psum_tensor("pav", [128, 512], F32).ap()
    PS["pt"] = nc.alloc_psum_tensor("pt", [128, 2, 8, 128], BF16).ap()

    E = {k: P.sem(k) for k in ["pe", "dve", "act", "lds", "pl_ld", "pl_st",
                               "gps"]}
    reg_consts(nc, [1e-5])
    emit_constants(P, C)
    E["gps"].inc(nc.gpsimd.memset(C["ones1"][:], 1.0))
    for eng in [nc.tensor, nc.vector, nc.scalar, nc.sync]:
        E["gps"].wait(eng)
    E["lds"].inc(nc.sync.dma_start(out=B["nsc"][:], in_=nsc_f[:, :]), 16)
    E["lds"].inc(nc.sync.dma_start(out=B["ssc"][:], in_=ssc_f[:, :]), 16)
    emit_precast(P, E, [(wq_f, wq_s), (wkv_f, wkv_s), (wo_f, wo_s)], pcb)

    x4 = x.rearrange("s (t p) d -> s t p d", p=128)
    y4 = y.rearrange("s (t p) d -> s t p d", p=128)
    for sq in range(nseq):
        emit_attn_seq(P, E, C, B, PS, wq_s, wkv_s, wo_s, B["xsb"],
                      io=(x4[sq], y4[sq]))
    E["lds"].wait(nc.sync)
    return nc


def emit_moe_seq(P, E, C, B, PS, W, xsb, io=None):
    import os as _os
    _BAR = _os.environ.get("K_MOE_BARRIER", "")
    def bar(tag):
        if tag in _BAR or _BAR == "all":
            P.nc.all_engine_barrier()
    """L1 MoE FFN, one sequence: x += 0.5 * moe(LN(x)).

    W: scratch APs wg [1,128,17,16], wup [12,128,17,512], wsw [12,128,9,512],
    wdn [24,128,8,512], wsu [4,128,17,512], wsa [2,128,9,512],
    wsd [4,128,17,512]; bias rows are the last k-chunk (row 0; selected by
    the constant e0mat stationary). bdn handled by the W6 x bdn matmul.
    """
    nc = P.nc
    pe, dve, act, lds = E["pe"], E["dve"], E["act"], E["lds"]
    p4, pt = PS["p4"], PS["pt"]
    pt_f32 = PS.get("pav2")
    ident, e0 = C["ident"], C["e0mat"]
    lnt = B["lnt"]
    bdn = B["bdn_sb"]

    if io is not None:
        x_in, _ = io
        for t in range(NT):
            dve.wait(nc.sync)
            lds.inc(nc.sync.dma_start(out=B["xstage"][:, 0:D], in_=x_in[t]), 16)
            lds.wait(nc.vector)
            dve.inc(nc.vector.tensor_copy(xsb[:, t, :], B["xstage"][:, 0:D]))

    pe.wait(nc.sync)
    dve.wait(nc.sync)
    lds.inc(nc.sync.dma_start(out=B["wgb"][:], in_=W["wg"][0]), 16)
    lds.wait(nc.sync)

    def tp8(src, dst, par):
        dve.wait(nc.tensor)
        for k in range(8):
            ins = nc.tensor.transpose(pt[:, par, k, :],
                                      src[:, k * 128:(k + 1) * 128], ident[:])
        pe.inc(ins)
        pe.wait(nc.vector)
        dve.inc(nc.vector.tensor_copy(dst, pt[:, par, 0:8, :]))

    for g in range(NG):
        for tt in range(GT):
            emit_ln_tile(P, E, xsb[:, g * GT + tt, :], B["xn"], lnt,
                         xn_war=(pe,))
            dve.wait(nc.vector)
            dve.inc(nc.vector.tensor_copy(B["resb"][:, tt, :], B["xn"][:]))
        n_iters = 1 if _os.environ.get("K_MOE_ONEIT") else 2
        for it in range(n_iters):
            P.maybe_chunk()
            bar("g")
            if _os.environ.get("K_MOE_ONEIT"):
                it = 1
                xt = B["resb"]
            else:
                xt = B["resb"] if it == 0 else B["xt1"]
            for tt in range(GT):
                for half in range(2):
                    tp8(xt[:, tt, half * 1024:(half + 1) * 1024],
                        B["xtT"][:, tt, half * 8:half * 8 + 8, :], tt % 2)
            bar("a")
            if "dbg_xtT2" in B and g == 0 and it == 1:
                dve.wait(nc.sync)
                lds.inc(nc.sync.dma_start(out=B["dbg_xtT2"],
                                          in_=B["xtT"][:, 0, :, :]), 16)
                lds.wait(nc.vector)
                lds.wait(nc.tensor)
            if "dbg_xtT" in B and g == 0 and it == 0:
                dve.wait(nc.sync)
                lds.inc(nc.sync.dma_start(out=B["dbg_resb"],
                                          in_=B["resb"][:, 0, :]), 16)
                lds.inc(nc.sync.dma_start(out=B["dbg_xtT"],
                                          in_=B["xtT"][:, 0, :, :]), 16)
                lds.inc(nc.sync.dma_start(out=B["dbg_wgb"],
                                          in_=B["wgb"][:, :, :]), 16)
                lds.wait(nc.vector)
                lds.wait(nc.tensor)
            for tt in range(GT):
                act.wait(nc.tensor)
                dve.wait(nc.tensor)
                for k in range(KD):
                    ins = nc.tensor.matmul(
                        p4[:, 0, 0:8], B["xtT"][:, tt, k, :],
                        B["wgb"][:, k, it * 8:(it + 1) * 8],
                        start=(k == 0), stop=False)
                ins = nc.tensor.matmul(p4[:, 0, 0:8], e0[:],
                                       B["wgb"][:, 16, it * 8:(it + 1) * 8],
                                       start=False, stop=True)
                pe.inc(ins)
                pe.wait(nc.vector)
                dve.inc(nc.vector.tensor_copy(B["gcol"][:], p4[:, 0, 0:8]))
                dve.wait(nc.vector)
                for i in range(8):
                    nc.vector.tensor_scalar(B["cm"][:, i, :], B["gcol"][:],
                                            B["gcol"][:, i:i + 1], None, ALU.is_le)
                    ins = nc.vector.tensor_scalar(B["eqm"][:, i, :], B["gcol"][:],
                                                  B["gcol"][:, i:i + 1], None,
                                                  ALU.is_equal)
                dve.inc(ins)
                dve.wait(nc.vector)
                nc.vector.tensor_mul(B["eqm"][:, :, :], B["eqm"][:, :, :],
                                     C["lmask8"][:, :, :])
                dve.inc(nc.vector.reduce_sum(B["rs8"][:, 0:8], B["cm"][:, :, :],
                                             axis=AX.X))
                dve.wait(nc.vector)
                dve.inc(nc.vector.reduce_sum(B["rs8"][:, 8:16],
                                             B["eqm"][:, :, :], axis=AX.X))
                dve.wait(nc.vector)
                dve.inc(nc.vector.tensor_sub(B["rs8"][:, 0:8], B["rs8"][:, 0:8],
                                             B["rs8"][:, 8:16]))
                dve.wait(nc.vector)
                dve.inc(nc.vector.tensor_scalar(B["kept"][:], B["rs8"][:, 0:8],
                                                2.5, None, ALU.is_ge))
                dve.wait(nc.vector)
                dve.inc(nc.vector.tensor_scalar(B["rs8"][:, 8:16], B["kept"][:],
                                                1e30, 1e30, ALU.mult,
                                                ALU.subtract))
                dve.inc(nc.vector.tensor_mul(B["gm8"][:], B["gcol"][:],
                                             B["kept"][:]))
                dve.wait(nc.vector)
                dve.inc(nc.vector.tensor_add(B["gm8"][:], B["gm8"][:],
                                             B["rs8"][:, 8:16]))
                dve.wait(nc.vector)
                dve.inc(nc.vector.reduce_max(B["mc"][:, 0:1], B["gm8"][:],
                                             axis=AX.X))
                dve.wait(nc.vector)
                dve.inc(nc.vector.tensor_scalar_mul(B["mc"][:, 1:2],
                                                    B["mc"][:, 0:1], -1.0))
                dve.wait(nc.scalar)
                act.inc(nc.scalar.activation(B["e8"][:], B["gm8"][:], AF.Exp,
                                             bias=B["mc"][:, 1:2]))
                act.wait(nc.vector)
                dve.inc(nc.vector.reduce_sum(B["mc"][:, 2:3], B["e8"][:],
                                             axis=AX.X))
                dve.wait(nc.vector)
                dve.inc(nc.vector.reciprocal(B["mc"][:, 3:4], B["mc"][:, 2:3]))
                dve.wait(nc.vector)
                dve.inc(nc.vector.tensor_scalar(B["wcol"][:, tt, :], B["e8"][:],
                                                B["mc"][:, 3:4], None, ALU.mult))
                if "dbg_wcol" in B and g == 0 and it == B.get("dbg_it", 0):
                    dve.wait(nc.sync)
                    lds.inc(nc.sync.dma_start(out=B["dbg_wcol"][tt],
                                              in_=B["wcol"][:, tt, :]), 16)
                    lds.wait(nc.vector)
                    lds.inc(nc.sync.dma_start(out=B["dbg_gcol"][tt],
                                              in_=B["gcol"][:, :]), 16)
                    lds.wait(nc.vector)
                    if tt == 0:
                        lds.inc(nc.sync.dma_start(out=B["dbg_kept"][0],
                                                  in_=B["kept"][:, :]), 16)
                        lds.inc(nc.sync.dma_start(out=B["dbg_kept"][1],
                                                  in_=B["rs8"][:, 0:8]), 16)
                        lds.inc(nc.sync.dma_start(out=B["dbg_kept"][2],
                                                  in_=C["lmask8"][:, 0, :]), 16)
                        lds.inc(nc.sync.dma_start(out=B["dbg_kept"][3],
                                                  in_=C["lmask8"][:, 5, :]), 16)
                        lds.inc(nc.sync.dma_start(out=B["dbg_kept"][4],
                                                  in_=B["gm8"][:, :]), 16)
                        lds.inc(nc.sync.dma_start(out=B["dbg_kept"][5],
                                                  in_=B["e8"][:, :]), 16)
                        lds.wait(nc.vector)
                dve.wait(nc.vector)
                dve.inc(nc.vector.tensor_copy(B["w8b"][:], B["wcol"][:, tt, :]))
                dve.wait(nc.tensor)
                ins = nc.tensor.transpose(pt[0:8, 0, 0, :], B["w8b"][:], ident[:])
                pe.inc(ins)
                pe.wait(nc.vector)
                dve.inc(nc.vector.tensor_copy(B["W6"][0:8, tt, :],
                                              pt[0:8, 0, 0, :]))
            bar("b")
            for oc in range(12):
                P.maybe_chunk()
                n, sub = oc // 2, oc % 2
                pe.wait(nc.sync)
                lds.inc(nc.sync.dma_start(out=B["wub"][:], in_=W["wup"][oc]), 16)
                lds.wait(nc.sync)
                lds.wait(nc.tensor)
                for tt in range(GT):
                    pban = p4[:, tt % 2, :]
                    act.wait(nc.tensor)
                    dve.wait(nc.tensor)
                    for k in range(KD):
                        ins = nc.tensor.matmul(pban, B["xtT"][:, tt, k, :],
                                               B["wub"][:, k, :],
                                               start=(k == 0), stop=False)
                    ins = nc.tensor.matmul(pban, e0[:], B["wub"][:, 16, :],
                                           start=False, stop=True)
                    pe.inc(ins)
                    pe.wait(nc.vector)
                    dve.inc(nc.vector.tensor_copy(B["hall"][:, n, tt, sub, :],
                                                  pban))
                    if "dbg_h" in B and g == 0 and it == B.get("dbg_it", 0) and oc < 2 and tt == 0:
                        dve.wait(nc.sync)
                        lds.inc(nc.sync.dma_start(
                            out=B["dbg_h"][sub], in_=B["hall"][:, 0, 0, sub, :]), 16)
                        lds.wait(nc.vector)
            bar("c")
            for n in range(6):
                P.maybe_chunk()
                pe.wait(nc.sync)
                lds.inc(nc.sync.dma_start(out=B["wub"][:, 0:9, :],
                                          in_=W["wsw"][2 * n]), 16)
                lds.wait(nc.sync)
                lds.inc(nc.sync.dma_start(out=B["wub"][:, 9:18, :][:, 0:8, :]
                                          if False else B["wub"][:, 9:17, :],
                                          in_=W["wsw"][2 * n + 1][:, 0:8, :]), 16)
                lds.wait(nc.sync)
                lds.inc(nc.sync.dma_start(out=B["wgb2"][:, 0:1, :],
                                          in_=W["wsw"][2 * n + 1][:, 8:9, :]), 16)
                lds.wait(nc.sync)
                lds.wait(nc.tensor)
                for tt in range(GT):
                    tp8(B["hall"][:, n, tt, :, :].rearrange("p a b -> p (a b)"),
                        B["hT"][:, 0:8, :], tt % 2)
                    for so in range(2):
                        pban = p4[:, 2 + so, :]
                        act.wait(nc.tensor)
                        dve.wait(nc.tensor)
                        koff = 0 if so == 0 else 9
                        for k in range(8):
                            ins = nc.tensor.matmul(pban, B["hT"][:, k, :],
                                                   B["wub"][:, koff + k, :],
                                                   start=(k == 0), stop=False)
                        bias_src = B["wub"][:, 8, :] if so == 0 else B["wgb2"][:, 0, :]
                        ins = nc.tensor.matmul(pban, e0[:], bias_src,
                                               start=False, stop=True)
                        pe.inc(ins)
                        pe.wait(nc.scalar)
                        dve.wait(nc.scalar)
                        act.inc(nc.scalar.activation(B["sgb"][:, 0:512], pban,
                                                     AF.Silu))
                        act.wait(nc.vector)
                        pe.wait(nc.vector)
                        dve.inc(nc.vector.scalar_tensor_tensor(
                            B["gstage"][:, so * 512:(so + 1) * 512],
                            B["sgb"][:, 0:512], B["wcol"][:, tt, n:n + 1],
                            B["hall"][:, n, tt, so, :], ALU.mult, ALU.mult))
                    if "dbg_gst" in B and g == 0 and it == B.get("dbg_it", 0) and n == 0 and tt == 0:
                        dve.wait(nc.sync)
                        lds.inc(nc.sync.dma_start(out=B["dbg_gst"],
                                                  in_=B["gstage"][:, :]), 16)
                        lds.wait(nc.vector)
                    tp8(B["gstage"][:, :],
                        B["gT"][:, n, tt, :, :], tt % 2)
                    if ("dbg_gT" in B and g == 0 and n == 0 and tt == 0
                            and it == B.get("dbg_it", 0)):
                        dve.wait(nc.vector)
                        dve.inc(nc.vector.tensor_copy(B["dbg_pt_sb"],
                                                      pt[:, 0, 0:8, :]))
                        dve.wait(nc.sync)
                        lds.inc(nc.sync.dma_start(out=B["dbg_gT"],
                                                  in_=B["gT"][:, 0, 0, :, :]), 16)
                        lds.inc(nc.sync.dma_start(out=B["dbg_pt"],
                                                  in_=B["dbg_pt_sb"]), 16)
                        lds.wait(nc.vector)
                        lds.wait(nc.tensor)
            bar("d")
            for oc in range(4):
                P.maybe_chunk()
                pe.wait(nc.sync)
                lds.inc(nc.sync.dma_start(out=B["wub"][:], in_=W["wsu"][oc]), 16)
                lds.wait(nc.sync)
                lds.wait(nc.tensor)
                for tt in range(GT):
                    pban = p4[:, tt % 2, :]
                    act.wait(nc.tensor)
                    dve.wait(nc.tensor)
                    for k in range(KD):
                        ins = nc.tensor.matmul(pban, B["xtT"][:, tt, k, :],
                                               B["wub"][:, k, :],
                                               start=(k == 0), stop=False)
                    ins = nc.tensor.matmul(pban, e0[:], B["wub"][:, 16, :],
                                           start=False, stop=True)
                    pe.inc(ins)
                    pe.wait(nc.vector)
                    dve.inc(nc.vector.tensor_copy(B["shb"][:, tt, oc, :], pban))
            bar("e")
            pe.wait(nc.sync)
            lds.inc(nc.sync.dma_start(out=B["wub"][:, 0:9, :], in_=W["wsa"][0]), 16)
            lds.wait(nc.sync)
            lds.inc(nc.sync.dma_start(out=B["wub"][:, 9:17, :],
                                      in_=W["wsa"][1][:, 0:8, :]), 16)
            lds.wait(nc.sync)
            lds.inc(nc.sync.dma_start(out=B["wgb2"][:, 0:1, :],
                                      in_=W["wsa"][1][:, 8:9, :]), 16)
            lds.wait(nc.sync)
            lds.wait(nc.tensor)
            for tt in range(GT):
                for half in range(2):
                    tp8(B["shb"][:, tt, :, :].rearrange("p a b -> p (a b)")
                        [:, half * 1024:(half + 1) * 1024],
                        B["xtT"][:, tt, half * 8:half * 8 + 8, :], tt % 2)
                for hf in range(2):
                    for so in range(2):
                        pban = p4[:, 2 + so, :]
                        act.wait(nc.tensor)
                        dve.wait(nc.tensor)
                        koff = 0 if so == 0 else 9
                        for k in range(8):
                            ins = nc.tensor.matmul(
                                pban, B["xtT"][:, tt, hf * 8 + k, :],
                                B["wub"][:, koff + k, :],
                                start=(k == 0), stop=False)
                        bias_src = (B["wub"][:, 8, :] if so == 0
                                    else B["wgb2"][:, 0, :])
                        ins = nc.tensor.matmul(pban, e0[:], bias_src,
                                               start=False, stop=True)
                        pe.inc(ins)
                        pe.wait(nc.scalar)
                        dve.wait(nc.scalar)
                        act.inc(nc.scalar.activation(B["sgb"][:, 0:512], pban,
                                                     AF.Silu))
                        act.wait(nc.vector)
                        pe.wait(nc.vector)
                        dve.inc(nc.vector.scalar_tensor_tensor(
                            B["gstage"][:, so * 512:(so + 1) * 512],
                            B["sgb"][:, 0:512], 1.0,
                            B["shb"][:, tt, 2 * hf + so, :], ALU.mult, ALU.mult))
                    if "dbg_sh2" in B and g == 0 and it == B.get("dbg_it", 0) \
                            and tt == 0 and hf == 0:
                        dve.wait(nc.sync)
                        lds.inc(nc.sync.dma_start(out=B["dbg_sh2"],
                                                  in_=B["gstage"][:, :]), 16)
                        lds.wait(nc.vector)
                        lds.wait(nc.tensor)
                    tp8(B["gstage"][:, :],
                        B["xtT"][:, tt, hf * 8:hf * 8 + 8, :], tt % 2)
            bar("f")
            if "dbg_sh2T" in B and g == 0 and it == B.get("dbg_it", 0):
                dve.wait(nc.sync)
                lds.inc(nc.sync.dma_start(out=B["dbg_sh2T"],
                                          in_=B["xtT"][:, 0, :, :]), 16)
                lds.wait(nc.vector)
                lds.wait(nc.tensor)
            if it == 1:
                if "dbg_fin" in B and g == 0:
                    dve.wait(nc.sync)
                    lds.inc(nc.sync.dma_start(out=B["dbg_fin"][0],
                                              in_=B["resb"][:, 0, :]), 16)
                    lds.wait(nc.vector)
                for tt in range(GT):
                    dve.wait(nc.vector)
                    xap = xsb[:, g * GT + tt, :]
                    dve.inc(nc.vector.scalar_tensor_tensor(
                        xap, B["resb"][:, tt, :], 0.5, xap, ALU.mult, ALU.add))
                if "dbg_fin" in B and g == 0:
                    dve.wait(nc.sync)
                    lds.inc(nc.sync.dma_start(out=B["dbg_fin"][1],
                                              in_=xsb[:, 0, :]), 16)
                    lds.wait(nc.vector)
            for oc in range(4):
                P.maybe_chunk()
                pe.wait(nc.sync)
                lds.inc(nc.sync.dma_start(out=B["wub"][:], in_=W["wsd"][oc]), 16)
                lds.wait(nc.sync)
                act.wait(nc.tensor)
                dve.wait(nc.tensor)
                for n in range(6):
                    pe.wait(nc.sync)
                    lds.inc(nc.sync.dma_start(out=B["wdnb"][:],
                                              in_=W["wdn"][n * 4 + oc]), 16)
                    lds.wait(nc.sync)
                    lds.wait(nc.tensor)
                    for tt in range(GT):
                        for k in range(8):
                            ins = nc.tensor.matmul(
                                p4[:, tt, :], B["gT"][:, n, tt, k, :],
                                B["wdnb"][:, k, :],
                                start=(n == 0 and k == 0), stop=False,
                                skip_group_check=True)
                    pe.inc(ins)
                import os as _os
                for tt in range(GT):
                    pban = p4[:, tt, :]
                    if not _os.environ.get("K_MOE_NOBDN"):
                        ins = nc.tensor.matmul(
                            pban, B["W6"][0:8, tt, :],
                            bdn[:, oc * 512:(oc + 1) * 512],
                            start=False, stop=False, skip_group_check=True)
                    if not _os.environ.get("K_MOE_NOSH"):
                        _kr = (range(0, 8) if _os.environ.get("K_MOE_SH_H0")
                               else range(8, 16) if _os.environ.get("K_MOE_SH_H1")
                               else range(KD))
                        for k in _kr:
                            ins = nc.tensor.matmul(pban, B["xtT"][:, tt, k, :],
                                                   B["wub"][:, k, :],
                                                   start=False, stop=False,
                                                   skip_group_check=True)
                    ins = nc.tensor.matmul(pban, e0[:], B["wub"][:, 16, :],
                                           start=False, stop=True,
                                           skip_group_check=True)
                    pe.inc(ins)
                    pe.wait(nc.vector)
                    if ("dbg_psum" in B and g == 0 and tt == 0 and oc == 0
                            and it == B.get("dbg_it", 0)):
                        dve.inc(nc.vector.tensor_copy(B["dbg_ps_sb"], pban))
                        dve.wait(nc.sync)
                        lds.inc(nc.sync.dma_start(out=B["dbg_psum"],
                                                  in_=B["dbg_ps_sb"]), 16)
                        lds.wait(nc.vector)
                    if it == 0:
                        dve.inc(nc.vector.scalar_tensor_tensor(
                            B["xt1"][:, tt, oc * 512:(oc + 1) * 512], pban, 1.0,
                            B["resb"][:, tt, oc * 512:(oc + 1) * 512],
                            ALU.mult, ALU.add))
                        if "dbg_xt1" in B and g == 0 and tt == 0:
                            dve.wait(nc.sync)
                            lds.inc(nc.sync.dma_start(
                                out=B["dbg_xt1"][oc],
                                in_=B["xt1"][:, 0, oc * 512:(oc + 1) * 512]), 16)
                            lds.wait(nc.vector)
                    else:
                        xap = xsb[:, g * GT + tt, oc * 512:(oc + 1) * 512]
                        dve.inc(nc.vector.scalar_tensor_tensor(
                            xap, pban, 0.5, xap, ALU.mult, ALU.add))

    if io is not None:
        for t in range(NT):
            lds.wait(nc.vector)
            dve.inc(nc.vector.tensor_copy(B["xstage"][:, 0:D], xsb[:, t, :]))
            dve.wait(nc.sync)
            lds.inc(nc.sync.dma_start(out=io[1][t], in_=B["xstage"][:, 0:D]), 16)
        lds.wait(nc.sync)


def alloc_moe_buffers(nc, B, al=None):
    if al is None:
        al = lambda n, sh, dt: nc.alloc_sbuf_tensor(n, sh, dt).ap()
    B["resb"] = al("resb", [128, GT, D], BF16)
    B["xt1"] = al("xt1", [128, GT, D], BF16)
    B["xtT"] = al("xtT", [128, GT, 16, 128], BF16)
    B["hall"] = al("hall", [128, 6, GT, 2, 512], BF16)
    B["wub"] = al("wub", [128, 17, 512], BF16)
    B["wdnb"] = al("wdnb", [128, 8, 512], BF16)
    B["wgb"] = al("wgb", [128, 17, 16], BF16)
    B["wgb2"] = al("wgb2", [128, 1, 512], BF16)
    B["bdn_sb"] = al("bdn_sb", [8, D], BF16)
    B["W6"] = al("W6", [8, GT, 128], BF16)
    B["gcol"] = al("gcol", [128, 8], F32)
    B["cm"] = al("cmx", [128, 8, 8], F32)
    B["eqm"] = al("eqm", [128, 8, 8], F32)
    B["rs8"] = al("rs8", [128, 16], F32)
    B["kept"] = al("kept", [128, 8], F32)
    B["gm8"] = al("gm8", [128, 8], F32)
    B["e8"] = al("e8", [128, 8], F32)
    B["w8b"] = al("w8b", [128, 8], BF16)
    B["wcol"] = al("wcolb", [128, GT, 8], F32)
    B["mc"] = al("mcb", [128, 8], F32)
    xst = nc.lookup_mloc(B["xstage"].tensor).addr
    B["gstage"] = nc.alloc_sbuf_tensor_at("gstage", [128, 1024], BF16,
                                          offset=xst).ap()
    B["hT"] = nc.alloc_sbuf_tensor_at("hTb", [128, 8, 128], BF16,
                                      offset=xst + 2048).ap()
    B["sgb"] = nc.alloc_sbuf_tensor_at("sgb", [128, 512], F32,
                                       offset=xst + 4096).ap()
    B["shb"] = nc.alloc_sbuf_tensor_at(
        "shb", [128, GT, 4, 512], BF16,
        offset=nc.lookup_mloc(B["xt1"].tensor).addr).ap()
    B["gT"] = nc.alloc_sbuf_tensor_at(
        "gTb", [128, 6, GT, 8, 128], BF16,
        offset=nc.lookup_mloc(B["hall"].tensor).addr).ap()
    return B


def make_moe_scratch(nc):
    W = {}
    W["wg"] = make_wscratch(nc, "wg_s", 17 * 128, 16, oc=16)
    W["wup"] = make_wscratch(nc, "wup_s", 17 * 128, 12 * 512)
    W["wsw"] = make_wscratch(nc, "wsw_s", 9 * 128, 12 * 512)
    W["wdn"] = make_wscratch(nc, "wdn_s", 8 * 128, 24 * 512)
    W["wsu"] = make_wscratch(nc, "wsu_s", 17 * 128, 4 * 512)
    W["wsa"] = make_wscratch(nc, "wsa_s", 9 * 128, 2 * 512)
    W["wsd"] = make_wscratch(nc, "wsd_s", 17 * 128, 4 * 512)
    return W


def moe_host_weights(Wg, bg, Wup, bup, Wsw, bsw, Wdn, bdn, Wsu, bsu,
                     Wsa, bsa, Wsd, bsd):
    z = np.zeros
    def with_bias(w2d, brow):
        pad = z((128, w2d.shape[1]), np.float32)
        pad[0] = brow
        return np.ascontiguousarray(np.concatenate([w2d, pad], 0))
    out = {}
    out["wg_f"] = with_bias(np.concatenate([Wg[0], Wg[1]], 1),
                            np.concatenate([bg[0], bg[1]], 0))
    out["wup_f"] = with_bias(np.transpose(Wup, (1, 0, 2)).reshape(2048, 6144),
                             np.asarray(bup).reshape(-1))
    out["wsw_f"] = with_bias(np.transpose(Wsw, (1, 0, 2)).reshape(1024, 6144),
                             np.asarray(bsw).reshape(-1))
    out["wdn_f"] = np.ascontiguousarray(
        np.transpose(Wdn, (1, 0, 2)).reshape(1024, 12288))
    out["wsu_f"] = with_bias(np.asarray(Wsu), np.asarray(bsu))
    out["wsa_f"] = with_bias(np.asarray(Wsa), np.asarray(bsa))
    out["wsd_f"] = with_bias(np.asarray(Wsd), np.asarray(bsd))
    bdn8 = z((8, 2048), np.float32)
    bdn8[0:6] = bdn
    out["bdn_f"] = bdn8
    return out


def build_moe_standalone(nseq=1):
    nc = bass.Bass("TRN2", target_bir_lowering=False)
    P = Prog(nc)
    x = nc.dram_tensor("x", [nseq, L, D], F32, kind="ExternalInput").ap()
    names = ["wg_f", "wup_f", "wsw_f", "wdn_f", "wsu_f", "wsa_f", "wsd_f"]
    shapes = [(17 * 128, 16), (17 * 128, 6144), (9 * 128, 6144),
              (8 * 128, 12288), (17 * 128, 2048), (9 * 128, 1024),
              (17 * 128, 2048)]
    fins = {nm: nc.dram_tensor(nm, list(sh), F32, kind="ExternalInput").ap()
            for nm, sh in zip(names, shapes)}
    bdn_f = nc.dram_tensor("bdn_f", [8, D], F32, kind="ExternalInput").ap()
    y = nc.dram_tensor("y", [nseq, L, D], F32, kind="ExternalOutput").ap()
    W = make_moe_scratch(nc)

    C = {}
    C["ident"] = nc.alloc_sbuf_tensor("ident", [128, 128], BF16).ap()
    C["ones1"] = nc.alloc_sbuf_tensor("ones1", [1, 128], BF16).ap()
    C["e0mat"] = nc.alloc_sbuf_tensor("e0mat", [128, 128], BF16).ap()
    C["lmask8"] = nc.alloc_sbuf_tensor("lmask8", [128, 8, 8], F32).ap()
    B = {}
    B["xsb"] = nc.alloc_sbuf_tensor("xsb", [128, NT, D], BF16).ap()
    B["xn"] = nc.alloc_sbuf_tensor("xn", [128, D], BF16).ap()
    B["xstage"] = nc.alloc_sbuf_tensor("xstage", [128, D], F32).ap()
    lnt = {}
    for nm in ["sum", "mean", "m2", "sq", "var", "sd", "rstd"]:
        lnt[nm] = nc.alloc_sbuf_tensor("ln_" + nm, [128, 1], F32).ap()
    lnt["junk"] = B["xstage"]
    B["lnt"] = lnt
    alloc_moe_buffers(nc, B)
    pcb = nc.alloc_sbuf_tensor_at(
        "pcbuf", [128, 2, 1024], BF16,
        offset=nc.lookup_mloc(B["xstage"].tensor).addr + 0).ap()

    PS = {}
    PS["p4"] = nc.alloc_psum_tensor("p4", [128, 4, 512], F32).ap()
    PS["pt"] = nc.alloc_psum_tensor("pt", [128, 2, 8, 128], BF16).ap()
    PS["pav2"] = nc.alloc_psum_tensor("pav2", [128, 2, 512], F32).ap()

    E = {k: P.sem(k) for k in ["pe", "dve", "act", "lds", "pl_ld", "pl_st",
                               "gps"]}
    reg_consts(nc, [1e-5])
    emit_constants(P, C)
    E["gps"].inc(nc.gpsimd.memset(C["ones1"][:], 1.0))
    for eng in [nc.tensor, nc.vector, nc.scalar, nc.sync]:
        E["gps"].wait(eng)
    E["pl_ld"].inc(nc.gpsimd.dma_start(out=B["bdn_sb"][:], in_=bdn_f[:, :]), 16)
    E["pl_ld"].wait(nc.tensor)
    emit_precast(P, E, [(fins["wg_f"], W["wg"][0] if False else W["wg"]),
                        (fins["wup_f"], W["wup"]), (fins["wsw_f"], W["wsw"]),
                        (fins["wdn_f"], W["wdn"]), (fins["wsu_f"], W["wsu"]),
                        (fins["wsa_f"], W["wsa"]), (fins["wsd_f"], W["wsd"])],
                 pcb)

    import os
    if os.environ.get("K_DEBUG"):
        B["dbg_wcol"] = nc.dram_tensor("dbg_wcol", [GT, 128, 8], F32,
                                       kind="ExternalOutput").ap()
        B["dbg_gcol"] = nc.dram_tensor("dbg_gcol", [GT, 128, 8], F32,
                                       kind="ExternalOutput").ap()
        B["dbg_h"] = nc.dram_tensor("dbg_h", [2, 128, 512], BF16,
                                    kind="ExternalOutput").ap()
        B["dbg_gst"] = nc.dram_tensor("dbg_gst", [128, 1024], BF16,
                                      kind="ExternalOutput").ap()[:, :]
        B["dbg_xt1"] = nc.dram_tensor("dbg_xt1", [4, 128, 512], BF16,
                                      kind="ExternalOutput").ap()
        B["dbg_xtT"] = nc.dram_tensor("dbg_xtT", [128, 16, 128], BF16,
                                      kind="ExternalOutput").ap()[:, :, :]
        B["dbg_kept"] = nc.dram_tensor("dbg_kept", [6, 128, 8], F32,
                                       kind="ExternalOutput").ap()
        B["dbg_resb"] = nc.dram_tensor("dbg_resb", [128, D], BF16,
                                       kind="ExternalOutput").ap()[:, :]
        B["dbg_gT"] = nc.dram_tensor("dbg_gT", [128, 8, 128], BF16,
                                     kind="ExternalOutput").ap()[:, :, :]
        B["dbg_xtT2"] = nc.dram_tensor("dbg_xtT2", [128, 16, 128], BF16,
                                       kind="ExternalOutput").ap()[:, :, :]
        B["dbg_it"] = int(os.environ.get("K_DBG_IT", "0"))
        B["dbg_fin"] = nc.dram_tensor("dbg_fin", [2, 128, D], BF16,
                                      kind="ExternalOutput").ap()
        B["dbg_shb"] = nc.dram_tensor("dbg_shb", [128, 4, 512], BF16,
                                      kind="ExternalOutput").ap()[:, :, :]
        B["dbg_sh2"] = nc.dram_tensor("dbg_sh2", [128, 1024], BF16,
                                      kind="ExternalOutput").ap()[:, :]
        B["dbg_sh2T"] = nc.dram_tensor("dbg_sh2T", [128, 16, 128], BF16,
                                       kind="ExternalOutput").ap()[:, :, :]
        B["dbg_psum"] = nc.dram_tensor("dbg_psum", [128, 512], F32,
                                       kind="ExternalOutput").ap()[:, :]
        B["dbg_pt"] = nc.dram_tensor("dbg_pt", [128, 8, 128], BF16,
                                     kind="ExternalOutput").ap()[:, :, :]
        B["dbg_pt_sb"] = nc.alloc_sbuf_tensor_at(
            "dbg_pt_sb", [128, 8, 128], BF16,
            offset=nc.lookup_mloc(B["xstage"].tensor).addr + 6144).ap()[:, :, :]
        B["dbg_ps_sb"] = nc.alloc_sbuf_tensor_at(
            "dbg_ps_sb", [128, 512], F32,
            offset=nc.lookup_mloc(B["xstage"].tensor).addr + 6144).ap()[:, :]
        B["dbg_wgb"] = nc.dram_tensor("dbg_wgb", [128, 17, 16], BF16,
                                      kind="ExternalOutput").ap()[:, :, :]
    x4 = x.rearrange("s (t p) d -> s t p d", p=128)
    y4 = y.rearrange("s (t p) d -> s t p d", p=128)
    for sq in range(nseq):
        emit_moe_seq(P, E, C, B, PS, W, B["xsb"], io=(x4[sq], y4[sq]))
    E["lds"].wait(nc.sync)
    return nc


MB_NG = 8


def emit_mamba_setup(P, E, C, B, PS, conv_w, a_log, d_in, norm_w):
    nc = P.nc
    pe, dve, act, lds = E["pe"], E["dve"], E["act"], E["lds"]
    p4 = PS["p4"]
    lds.inc(nc.sync.dma_start(
        out=B["cw"][:, :, :],
        in_=conv_w.rearrange("(c p) k -> p c k", p=128)), 16)
    lds.inc(nc.sync.dma_start(
        out=B["nw"][:, :], in_=norm_w.rearrange("(k p) -> p k", p=128),
        allow_slow_non_contiguous=True), 16)
    lds.inc(nc.sync.dma_start(
        out=B["ad"][0:1, 0:64], in_=a_log.rearrange("(a h) -> a h", a=1)), 16)
    lds.inc(nc.sync.dma_start(
        out=B["ad"][0:1, 64:128], in_=d_in.rearrange("(a h) -> a h", a=1)), 16)
    lds.wait(nc.tensor)
    lds.wait(nc.vector)
    lds.wait(nc.scalar)
    ins = nc.tensor.matmul(p4[:, 0, 0:128], C["onesf"][0:1, 0:128],
                           B["ad"][0:1, 0:128], start=True, stop=True)
    pe.inc(ins)
    pe.wait(nc.scalar)
    act.inc(nc.scalar.activation(B["abc"][:, :], p4[:, 0, 0:64], AF.Exp))
    pe.wait(nc.vector)
    dve.inc(nc.vector.tensor_copy(B["db"][:, :], p4[:, 0, 64:128]))
    act.wait(nc.vector)
    dve.inc(nc.vector.tensor_scalar_mul(B["abc"][:, :], B["abc"][:, :], -1.0))


def emit_mamba_seq(P, E, C, B, PS, win_s, wout_s, xsb, io=None):
    nc = P.nc
    pe, dve, act, lds = E["pe"], E["dve"], E["act"], E["lds"]
    p4, pt = PS["p4"], PS["pt"]
    ident = C["ident"]
    e0 = C["e0mat"]
    lnt = B["lnt"]
    HID = 4096

    if io is not None:
        x_in, _ = io
        for t in range(NT):
            dve.wait(nc.sync)
            lds.inc(nc.sync.dma_start(out=B["xstage"][:, 0:D], in_=x_in[t]), 16)
            lds.wait(nc.vector)
            dve.inc(nc.vector.tensor_copy(xsb[:, t, :], B["xstage"][:, 0:D]))

    pe.wait(nc.vector)
    nc.vector.memset(B["S"][:, :, :], 0.0)
    dve.inc(nc.vector.memset(B["xBCT"][:, :, 0:3], 0.0))

    for g in range(MB_NG):
        P.maybe_chunk()
        for tt in range(2):
            emit_ln_tile(P, E, xsb[:, g * 2 + tt, :], B["xn"], lnt,
                         xn_war=(pe,))
            for half in range(2):
                dve.wait(nc.tensor)
                for k in range(8):
                    kk = half * 8 + k
                    ins = nc.tensor.transpose(
                        pt[:, tt % 2, k, :],
                        B["xn"][:, kk * 128:(kk + 1) * 128], ident[:])
                pe.inc(ins)
                pe.wait(nc.vector)
                dve.inc(nc.vector.tensor_copy(
                    B["xTg"][:, tt, half * 8:half * 8 + 8, :],
                    pt[:, tt % 2, 0:8, :]))
        for oc in range(17):
            nw = 320 if oc == 16 else 512
            for half in range(2):
                pe.wait(nc.sync)
                if half == 0:
                    lds.inc(nc.sync.dma_start(out=B["winb"][:, 0:8, :],
                                              in_=win_s[oc][:, 0:8, :]), 16)
                else:
                    lds.inc(nc.sync.dma_start(out=B["winb"][:, 0:9, :],
                                              in_=win_s[oc][:, 8:17, :]), 16)
                lds.wait(nc.tensor)
                if half == 0:
                    dve.wait(nc.tensor)
                    act.wait(nc.tensor)
                for tt in range(2):
                    pban = p4[:, tt % 2, 0:nw]
                    for k in range(8):
                        ins = nc.tensor.matmul(
                            pban, B["xTg"][:, tt, half * 8 + k, :],
                            B["winb"][:, k, 0:nw], start=(half == 0 and k == 0),
                            stop=False, skip_group_check=True)
                    if half == 1:
                        ins = nc.tensor.matmul(
                            pban, e0[:], B["winb"][:, 8, 0:nw],
                            start=False, stop=True, skip_group_check=True)
                pe.inc(ins)
            pe.wait(nc.vector)
            for tt in range(2):
                pban = p4[:, tt % 2, 0:nw]
                if oc < 8:
                    ins = nc.vector.tensor_copy(
                        B["zg"][:, tt, oc * 512:(oc + 1) * 512], pban)
                    dve.inc(ins)
                else:
                    nbc = 256 if oc == 16 else 512
                    dve.inc(nc.vector.tensor_copy(B["bstage"][:, 0:nbc],
                                                  pban[:, 0:nbc]))
                    if oc == 16:
                        dve.inc(nc.vector.tensor_copy(B["dtg"][:, tt, :],
                                                      pban[:, 256:320]))
                    nsub = nbc // 128
                    dve.wait(nc.tensor)
                    for s4 in range(nsub):
                        ins = nc.tensor.transpose(
                            pt[:, tt % 2, s4, :],
                            B["bstage"][:, s4 * 128:(s4 + 1) * 128], ident[:])
                    pe.inc(ins)
                    pe.wait(nc.vector)
                    for s4 in range(nsub):
                        ch = (oc - 8) * 4 + s4
                        ins = nc.vector.tensor_copy(
                            B["xBCT"][:, ch, 3 + tt * 128:3 + (tt + 1) * 128],
                            pt[:, tt % 2, s4, :])
                    dve.inc(ins)
        if g == 0 and "dbgm_zg" in B:
            dbg = E["dbg"]
            dve.wait(nc.sync); act.wait(nc.sync); pe.wait(nc.sync)
            for nm, src in [("dbgm_zg", B["zg"][:, :, :]),
                            ("dbgm_xBCT", B["xBCT"][:, :, :]),
                            ("dbgm_dtg", B["dtg"][:, :, :]),
                            ("dbgm_xTg", B["xTg"][:, :, :, :])]:
                dbg.inc(nc.sync.dma_start(out=B[nm], in_=src), 16)
            for eng in [nc.vector, nc.scalar, nc.tensor]:
                dbg.wait(eng)
        for cc in range(2):
            P.maybe_chunk()
            t0 = cc * 128
            for hc in range(2):
                act.wait(nc.vector)
                for ch17 in range(17):
                    ch = hc * 17 + ch17
                    nc.vector.tensor_scalar(
                        B["xBCf"][:, ch17, :], B["xBCT"][:, ch, t0:t0 + 128],
                        B["cw"][:, ch, 0:1], None, ALU.mult)
                for k in range(1, 4):
                    for ch17 in range(17):
                        ch = hc * 17 + ch17
                        ins = nc.vector.scalar_tensor_tensor(
                            B["xBCf"][:, ch17, :],
                            B["xBCT"][:, ch, t0 + k:t0 + k + 128],
                            B["cw"][:, ch, k:k + 1], B["xBCf"][:, ch17, :],
                            ALU.mult, ALU.add)
                dve.inc(ins)
                dve.wait(nc.scalar)
                if hc == 0:
                    pe.wait(nc.scalar)
                act.inc(nc.scalar.activation(
                    B["xBCc"][:, hc * 17:hc * 17 + 17, :]
                    .rearrange("p a b -> p (a b)"),
                    B["xBCf"][:, :, :].rearrange("p a b -> p (a b)"), AF.Silu))
            act.wait(nc.tensor)
            for half4 in range(4):
                dve.wait(nc.tensor)
                for k in range(8):
                    ch = half4 * 8 + k
                    ins = nc.tensor.transpose(pt[:, half4 % 2, k, :],
                                              B["xBCc"][:, ch, :], ident[:])
                pe.inc(ins)
                pe.wait(nc.vector)
                dve.inc(nc.vector.tensor_copy(
                    B["xs"][:, half4 * 8:half4 * 8 + 8, :],
                    pt[:, half4 % 2, 0:8, :]))
            dve.wait(nc.tensor)
            ins = nc.tensor.matmul(p4[:, 2, 0:128], B["xBCc"][:, 32, :],
                                   B["xBCc"][:, 33, :], start=True, stop=True)
            pe.inc(ins)
            ins = nc.tensor.transpose(pt[:, 0, 0, :], B["xBCc"][:, 32, :],
                                      ident[:])
            pe.inc(ins)
            pe.wait(nc.vector)
            dve.inc(nc.vector.tensor_copy(B["CBT"][:, :], p4[:, 2, 0:128]))
            dve.inc(nc.vector.tensor_copy(B["Btm"][:, :], pt[:, 0, 0, :]))
            dve.wait(nc.scalar)
            act.inc(nc.scalar.activation(B["diff"][:, 0:64],
                                         B["dtg"][:, cc, :], AF.Abs))
            act.wait(nc.scalar)
            act.inc(nc.scalar.activation(B["diff"][:, 64:128],
                                         B["diff"][:, 0:64], AF.Exp,
                                         scale=-1.0))
            act.wait(nc.scalar)
            act.inc(nc.scalar.activation(B["diff"][:, 0:64],
                                         B["diff"][:, 64:128], AF.Ln,
                                         bias=1.0))
            act.wait(nc.vector)
            dve.inc(nc.vector.tensor_scalar_max(B["dt"][:, :],
                                                B["dtg"][:, cc, :], 0.0))
            dve.wait(nc.vector)
            dve.inc(nc.vector.tensor_add(B["dt"][:, :], B["dt"][:, :],
                                         B["diff"][:, 0:64]))
            dve.wait(nc.vector)
            dve.inc(nc.vector.tensor_mul(B["av"][:, :], B["dt"][:, :],
                                         B["abc"][:, :]))
            dve.wait(nc.tensor)
            act.wait(nc.tensor)
            ins = nc.tensor.matmul(p4[:, 2, 0:64], C["triu"][:, :],
                                   B["av"][:, :], start=True, stop=True)
            pe.inc(ins)
            pe.wait(nc.vector)
            dve.inc(nc.vector.tensor_copy(B["cum"][:, :], p4[:, 2, 0:64]))
            dve.wait(nc.vector)
            dve.inc(nc.vector.tensor_scalar_mul(B["ncum"][:, :],
                                                B["cum"][:, :], -1.0))
            dve.wait(nc.scalar)
            act.inc(nc.scalar.activation(B["ecum"][:, :], B["cum"][:, :],
                                         AF.Exp))
            dve.wait(nc.tensor)
            act.wait(nc.tensor)
            ins = nc.tensor.matmul(p4[:, 2, 0:64], C["e127f"][:, :],
                                   B["cum"][:, :], start=True, stop=True)
            pe.inc(ins)
            pe.wait(nc.scalar)
            act.inc(nc.scalar.activation(B["decB"][:, :], p4[:, 2, 0:64],
                                         AF.Exp))
            act.wait(nc.vector)
            if g == 0 and cc == 0 and "dbgm_zg" in B:
                dbg = E["dbg"]
                dve.wait(nc.sync); act.wait(nc.sync); pe.wait(nc.sync)
                for nm, src in [("dbgm_xBCc", B["xBCc"][:, :, :]),
                                ("dbgm_xs", B["xs"][:, :, :]),
                                ("dbgm_CBT", B["CBT"][:, :]),
                                ("dbgm_Btm", B["Btm"][:, :]),
                                ("dbgm_dt", B["dt"][:, :]),
                                ("dbgm_cum", B["cum"][:, :]),
                                ("dbgm_ecum", B["ecum"][:, :]),
                                ("dbgm_decB", B["decB"][:, :])]:
                    dbg.inc(nc.sync.dma_start(out=B[nm], in_=src), 16)
                for eng in [nc.vector, nc.scalar, nc.tensor]:
                    dbg.wait(eng)
            for h in range(64):
                bb = h % 2
                dve.inc(nc.vector.tensor_scalar(
                    B["bb1"][:, :], C["onesf"][:, :], B["cum"][:, h:h + 1],
                    None, ALU.mult))
                dve.wait(nc.tensor)
                act.wait(nc.tensor)
                ins = nc.tensor.matmul(p4[:, 2 + bb, 0:128], B["bb1"][:, :],
                                       C["identf"][:, :], start=True, stop=True)
                pe.inc(ins)
                pe.wait(nc.vector)
                dve.inc(nc.vector.scalar_tensor_tensor(
                    B["diff"][:, :], p4[:, 2 + bb, 0:128],
                    B["ncum"][:, h:h + 1], C["maskneg"][:, :],
                    ALU.add, ALU.add))
                dve.wait(nc.scalar)
                act.inc(nc.scalar.activation(B["Lh"][:, :], B["diff"][:, :],
                                             AF.Exp))
                act.wait(nc.vector)
                dve.inc(nc.vector.scalar_tensor_tensor(
                    B["MT"][:, :], B["Lh"][:, :], B["dt"][:, h:h + 1],
                    B["CBT"][:, :], ALU.mult, ALU.mult))
                if g == 0 and cc == 0 and h == 0 and "dbgm_zg" in B:
                    dbg = E["dbg"]
                    dve.wait(nc.sync); act.wait(nc.sync); pe.wait(nc.sync)
                    dbg.inc(nc.sync.dma_start(out=B["dbgm_Lh"],
                                              in_=B["Lh"][:, :]), 16)
                    dbg.inc(nc.sync.dma_start(out=B["dbgm_MT"],
                                              in_=B["MT"][:, :]), 16)
                    for eng in [nc.vector, nc.scalar, nc.tensor]:
                        dbg.wait(eng)
                dve.wait(nc.vector)
                dve.inc(nc.vector.tensor_scalar(
                    B["wdc"][:, 0:1], B["Lh"][:, 127:128], B["dt"][:, h:h + 1],
                    None, ALU.mult))
                dve.wait(nc.vector)
                dve.inc(nc.vector.tensor_scalar(
                    B["xw"][:, :], B["xs"][:, :, :]
                    .rearrange("p a b -> p (a b)")[:, h * 64:(h + 1) * 64],
                    B["wdc"][:, 0:1], None, ALU.mult))
                dve.inc(nc.vector.tensor_copy(B["shbf"][:, :], B["S"][:, h, :]))
                dve.wait(nc.tensor)
                act.wait(nc.tensor)
                ins = nc.tensor.matmul(
                    p4[:, bb, 0:64], B["MT"][:, :],
                    B["xs"][:, :, :].rearrange("p a b -> p (a b)")
                    [:, h * 64:(h + 1) * 64], start=True, stop=True)
                pe.inc(ins)
                ins = nc.tensor.matmul(p4[:, 2 + bb, 0:64],
                                       B["xBCc"][:, 33, :], B["shbf"][:, :],
                                       start=True, stop=True)
                pe.inc(ins)
                ins = nc.tensor.matmul(p4[:, 2 + bb, 64:128], B["Btm"][:, :],
                                       B["xw"][:, :], start=True, stop=True)
                pe.inc(ins)
                pe.wait(nc.vector)
                dve.inc(nc.vector.tensor_copy(B["ys"][:, :], p4[:, bb, 0:64]))
                dve.wait(nc.vector)
                dve.inc(nc.vector.scalar_tensor_tensor(
                    B["ys"][:, :], p4[:, 2 + bb, 0:64], B["ecum"][:, h:h + 1],
                    B["ys"][:, :], ALU.mult, ALU.add))
                dve.wait(nc.vector)
                dve.inc(nc.vector.scalar_tensor_tensor(
                    B["yc"][:, h * 64:(h + 1) * 64],
                    B["xs"][:, :, :].rearrange("p a b -> p (a b)")
                    [:, h * 64:(h + 1) * 64], B["db"][:, h:h + 1],
                    B["ys"][:, :], ALU.mult, ALU.add))
                dve.inc(nc.vector.scalar_tensor_tensor(
                    B["S"][:, h, :], B["S"][:, h, :], B["decB"][:, h:h + 1],
                    p4[:, 2 + bb, 64:128], ALU.mult, ALU.add))
            if g == 0 and cc == 0 and "dbgm_zg" in B:
                dbg = E["dbg"]
                dve.wait(nc.sync); act.wait(nc.sync); pe.wait(nc.sync)
                dbg.inc(nc.sync.dma_start(out=B["dbgm_yc"],
                                          in_=B["yc"][:, :]), 16)
                dbg.inc(nc.sync.dma_start(out=B["dbgm_S"],
                                          in_=B["S"][:, :, :]), 16)
                for eng in [nc.vector, nc.scalar, nc.tensor]:
                    dbg.wait(eng)
            dve.wait(nc.scalar)
            pe.wait(nc.scalar)
            act.inc(nc.scalar.activation(
                B["zs"][:, :], B["zg"][:, cc, :], AF.Silu))
            act.wait(nc.vector)
            dve.inc(nc.vector.tensor_mul(B["yc"][:, :], B["yc"][:, :],
                                         B["zs"][:, :]))
            dve.wait(nc.scalar)
            act.inc(nc.scalar.activation(B["mjunk"][:, 0:HID], B["yc"][:, :],
                                         AF.Square, accum_out=lnt["sq"][:]))
            act.wait(nc.scalar)
            act.inc(nc.scalar.activation(lnt["sd"][:], lnt["sq"][:], AF.Sqrt,
                                         scale=1.0 / HID, bias=1e-5))
            act.wait(nc.vector)
            dve.inc(nc.vector.reciprocal(lnt["rstd"][:], lnt["sd"][:]))
            dve.wait(nc.vector)
            dve.inc(nc.vector.tensor_scalar(B["yc"][:, :], B["yc"][:, :],
                                            lnt["rstd"][:], None, ALU.mult))
            for q8 in range(4):
                dve.wait(nc.tensor)
                for k in range(8):
                    kk = q8 * 8 + k
                    ins = nc.tensor.transpose(
                        pt[:, q8 % 2, k, :],
                        B["yc"][:, kk * 128:(kk + 1) * 128], ident[:])
                pe.inc(ins)
                pe.wait(nc.vector)
                for k in range(8):
                    kk = q8 * 8 + k
                    ins = nc.vector.tensor_scalar(
                        B["yznT"][:, kk, cc, :], pt[:, q8 % 2, k, :],
                        B["nw"][:, kk:kk + 1], None, ALU.mult)
                dve.inc(ins)
        if g == 0 and "dbgm_zg" in B:
            dbg = E["dbg"]
            dve.wait(nc.sync); act.wait(nc.sync); pe.wait(nc.sync)
            dbg.inc(nc.sync.dma_start(out=B["dbgm_yznT"],
                                      in_=B["yznT"][:, :, :, :]), 16)
            for eng in [nc.vector, nc.scalar, nc.tensor]:
                dbg.wait(eng)
        for oc in range(4):
            P.maybe_chunk()
            for q in range(4):
                pe.wait(nc.sync)
                act.wait(nc.sync)
                lds.inc(nc.sync.dma_start(
                    out=B["winb"][:, 0:8, :],
                    in_=wout_s[oc][:, q * 8:(q + 1) * 8, :]), 16)
                lds.wait(nc.tensor)
                if q == 0:
                    act.wait(nc.tensor)
                    dve.wait(nc.tensor)
                for cc in range(2):
                    pban = p4[:, cc, :]
                    for k in range(8):
                        ins = nc.tensor.matmul(
                            pban, B["yznT"][:, q * 8 + k, cc, :],
                            B["winb"][:, k, :], start=(q == 0 and k == 0),
                            stop=(q == 3 and k == 7), skip_group_check=True)
                pe.inc(ins)
            pe.wait(nc.vector)
            for cc in range(2):
                xap = xsb[:, g * 2 + cc, oc * 512:(oc + 1) * 512]
                dve.inc(nc.vector.scalar_tensor_tensor(
                    xap, p4[:, cc, :], 1.0, xap, ALU.mult, ALU.add))
        dve.inc(nc.vector.tensor_copy(B["xBCT"][:, :, 0:3],
                                      B["xBCT"][:, :, 256:259]))

    if io is not None:
        for t in range(NT):
            lds.wait(nc.vector)
            dve.inc(nc.vector.tensor_copy(B["xstage"][:, 0:D], xsb[:, t, :]))
            dve.wait(nc.sync)
            lds.inc(nc.sync.dma_start(out=io[1][t], in_=B["xstage"][:, 0:D]), 16)
        lds.wait(nc.sync)


def alloc_mamba_buffers(nc, B, al=None, compact=False):
    if al is None:
        al = lambda n, sh, dt: nc.alloc_sbuf_tensor(n, sh, dt).ap()
    a = lambda n, sh, dt: al(n, sh, dt)
    B["zg"] = a("zg", [128, 2, 4096], BF16)
    B["xBCT"] = a("xBCT", [128, 34, 259], BF16)
    B["dtg"] = a("dtg", [128, 2, 64], F32)
    B["winb"] = a("winb", [128, 9, 512], BF16)
    B["xBCc"] = a("xBCc", [128, 34, 128], BF16)
    B["xBCf"] = a("xBCf", [128, 17, 128], F32)
    B["xs"] = a("xs", [128, 32, 128], BF16)
    B["yc"] = a("yc", [128, 4096], BF16)
    B["S"] = a("S", [128, 64, 64], F32)
    B["yznT"] = a("yznT", [128, 32, 2, 128], BF16)
    if compact:
        B["mjunk"] = nc.alloc_sbuf_tensor_at(
            "mjunk", [128, 4096], F32,
            offset=nc.lookup_mloc(B["winb"].tensor).addr).ap()
    else:
        B["mjunk"] = B["xstage"]
    B["xTg"] = nc.alloc_sbuf_tensor_at(
        "xTg", [128, 2, 16, 128], BF16,
        offset=nc.lookup_mloc(B["yznT"].tensor).addr).ap()
    B["zs"] = nc.alloc_sbuf_tensor_at(
        "zs", [128, 4096], BF16,
        offset=nc.lookup_mloc(B["xBCf"].tensor).addr).ap()
    B["bstage"] = nc.alloc_sbuf_tensor_at(
        "bstage", [128, 512], BF16,
        offset=nc.lookup_mloc(B["xn"].tensor).addr).ap()
    for nm, sh, dt in [("CBT", [128, 128], BF16), ("Btm", [128, 128], BF16),
                       ("Lh", [128, 128], BF16), ("MT", [128, 128], BF16),
                       ("diff", [128, 128], F32), ("bb1", [128, 128], F32),
                       ("dt", [128, 64], F32), ("av", [128, 64], F32),
                       ("cum", [128, 64], F32), ("ncum", [128, 64], F32),
                       ("ecum", [128, 64], F32), ("decB", [128, 64], F32),
                       ("wdc", [128, 1], F32), ("xw", [128, 64], BF16),
                       ("ys", [128, 64], F32), ("shbf", [128, 64], BF16),
                       ("nw", [128, 32], F32), ("cw", [128, 34, 4], F32),
                       ("ad", [1, 128], F32), ("abc", [128, 64], F32),
                       ("db", [128, 64], F32)]:
        B[nm] = a("mb_" + nm, sh, dt)


def make_mamba_scratch(nc):
    return (make_wscratch(nc, "win_s", 2176, 8512),
            make_wscratch(nc, "wout_s", 4096, 2048))


def build_mamba_standalone(nseq=1):
    nc = bass.Bass("TRN2", target_bir_lowering=False)
    P = Prog(nc)
    x = nc.dram_tensor("x", [nseq, L, D], F32, kind="ExternalInput").ap()
    win_f = nc.dram_tensor("win_f", [2176, 8512], F32,
                           kind="ExternalInput").ap()
    wout_f = nc.dram_tensor("wout_f", [4096, 2048], F32,
                            kind="ExternalInput").ap()
    conv_w = nc.dram_tensor("conv_w", [4352, 4], F32,
                            kind="ExternalInput").ap()
    a_log = nc.dram_tensor("a_log", [64], F32, kind="ExternalInput").ap()
    d_in = nc.dram_tensor("d_in", [64], F32, kind="ExternalInput").ap()
    norm_w = nc.dram_tensor("norm_w", [4096], F32, kind="ExternalInput").ap()
    y = nc.dram_tensor("y", [nseq, L, D], F32, kind="ExternalOutput").ap()
    win_s, wout_s = make_mamba_scratch(nc)

    C = {}
    C["ident"] = nc.alloc_sbuf_tensor("ident", [128, 128], BF16).ap()
    C["identf"] = nc.alloc_sbuf_tensor("identf", [128, 128], F32).ap()
    C["onesf"] = nc.alloc_sbuf_tensor("onesf", [128, 128], F32).ap()
    C["triu"] = nc.alloc_sbuf_tensor("triu", [128, 128], F32).ap()
    C["maskneg"] = nc.alloc_sbuf_tensor("maskneg", [128, 128], F32).ap()
    C["e127f"] = nc.alloc_sbuf_tensor("e127f", [128, 128], F32).ap()
    C["e0mat"] = nc.alloc_sbuf_tensor("e0mat", [128, 128], BF16).ap()
    C["ones1"] = nc.alloc_sbuf_tensor("ones1", [1, 128], BF16).ap()
    B = {}
    B["xsb"] = nc.alloc_sbuf_tensor("xsb", [128, NT, D], BF16).ap()
    B["xn"] = nc.alloc_sbuf_tensor("xn", [128, D], BF16).ap()
    B["xstage"] = nc.alloc_sbuf_tensor("xstage", [128, 4096], F32).ap()
    lnt = {}
    for nm in ["sum", "mean", "m2", "sq", "var", "sd", "rstd"]:
        lnt[nm] = nc.alloc_sbuf_tensor("ln_" + nm, [128, 1], F32).ap()
    lnt["junk"] = B["xstage"]
    B["lnt"] = lnt
    alloc_mamba_buffers(nc, B)
    pcb = nc.alloc_sbuf_tensor_at(
        "pcbuf", [128, 2, 1024], BF16,
        offset=nc.lookup_mloc(B["xstage"].tensor).addr + 0).ap()

    PS = {}
    PS["p4"] = nc.alloc_psum_tensor("p4", [128, 4, 512], F32).ap()
    PS["pt"] = nc.alloc_psum_tensor("pt", [128, 2, 8, 128], BF16).ap()

    E = {k: P.sem(k) for k in ["pe", "dve", "act", "lds", "pl_ld", "pl_st",
                               "gps"]}
    reg_consts(nc, [1e-5, 1.0])
    emit_constants(P, C)
    E["gps"].inc(nc.gpsimd.memset(C["ones1"][:], 1.0))
    for eng in [nc.tensor, nc.vector, nc.scalar, nc.sync]:
        E["gps"].wait(eng)
    import os
    if os.environ.get("K_MB_DEBUG"):
        E["dbg"] = P.sem("dbg")
        for nm, sh, dt in [("dbgm_zg", [128, 2, 4096], BF16),
                           ("dbgm_xBCT", [128, 34, 259], BF16),
                           ("dbgm_dtg", [128, 2, 64], F32),
                           ("dbgm_xTg", [128, 2, 16, 128], BF16),
                           ("dbgm_xBCc", [128, 34, 128], BF16),
                           ("dbgm_xs", [128, 32, 128], BF16),
                           ("dbgm_CBT", [128, 128], BF16),
                           ("dbgm_Btm", [128, 128], BF16),
                           ("dbgm_dt", [128, 64], F32),
                           ("dbgm_cum", [128, 64], F32),
                           ("dbgm_ecum", [128, 64], F32),
                           ("dbgm_decB", [128, 64], F32),
                           ("dbgm_Lh", [128, 128], BF16),
                           ("dbgm_MT", [128, 128], BF16),
                           ("dbgm_yc", [128, 4096], BF16),
                           ("dbgm_S", [128, 64, 64], F32),
                           ("dbgm_yznT", [128, 32, 2, 128], BF16)]:
            ap = nc.dram_tensor(nm, sh, dt, kind="ExternalOutput").ap()
            B[nm] = ap[tuple(slice(None) for _ in sh)]
    emit_precast(P, E, [(win_f, win_s), (wout_f, wout_s)], pcb)
    emit_mamba_setup(P, E, C, B, PS, conv_w, a_log, d_in, norm_w)
    x4 = x.rearrange("s (t p) d -> s t p d", p=128)
    y4 = y.rearrange("s (t p) d -> s t p d", p=128)
    for sq in range(nseq):
        emit_mamba_seq(P, E, C, B, PS, win_s, wout_s, B["xsb"],
                       io=(x4[sq], y4[sq]))
    E["lds"].wait(nc.sync)
    return nc


def alloc_l0ffn_buffers(nc, B=None, al=None, compact=False):
    if al is None:
        al = lambda n, sh, dt: nc.alloc_sbuf_tensor(n, sh, dt).ap()
    if B is None:
        B = {}
        B["xsb"] = nc.alloc_sbuf_tensor("xsb", [128, NT, D], BF16).ap()
        B["xn"] = nc.alloc_sbuf_tensor("xn", [128, D], BF16).ap()
        B["xstage"] = nc.alloc_sbuf_tensor("xstage", [128, D], F32).ap()
        lnt = {}
        for nm in ["sum", "mean", "m2", "sq", "var", "sd", "rstd"]:
            lnt[nm] = nc.alloc_sbuf_tensor("ln_" + nm, [128, 1], F32).ap()
        lnt["junk"] = B["xstage"]
        B["lnt"] = lnt
    B["xT"] = al("xT", [128, GT, KD, 128], BF16)
    B["hsT"] = al("hsT", [128, GT, 64, 128], BF16)
    B["wg"] = al("f_wgb", [128, 16, 512], BF16)
    B["wu"] = al("f_wub", [128, 16, 512], BF16)
    if compact:
        B["wd"] = nc.alloc_sbuf_tensor_at(
            "f_wdb", [128, 8, 512], BF16,
            offset=nc.lookup_mloc(B["wg"].tensor).addr).ap()
    else:
        B["wd"] = al("f_wdb", [128, 8, 512], BF16)
    B["sg"] = al("sg", [128, 1, 512], F32)
    B["hst"] = al("hst", [128, 2, 512], BF16)
    B["hstf"] = al("hstf", [128, 1, 512], F32)
    B["habs"] = al("habs", [128, 1, 512], F32)
    B["hc"] = al("hc", [128, 6, 128], BF16)
    B["hr"] = al("hr", [128, 4, 128], BF16)
    return B


def build_l0ffn_standalone(nseq=2):
    nc = bass.Bass("TRN2", target_bir_lowering=False)
    P = Prog(nc)
    x = nc.dram_tensor("x", [nseq, L, D], F32, kind="ExternalInput").ap()
    wg_f = nc.dram_tensor("w_gate", [D, 8192], F32, kind="ExternalInput").ap()
    wu_f = nc.dram_tensor("w_up", [D, 8192], F32, kind="ExternalInput").ap()
    wd_f = nc.dram_tensor("w_down", [8192, D], F32, kind="ExternalInput").ap()
    y = nc.dram_tensor("y", [nseq, L, D], F32, kind="ExternalOutput").ap()
    wg_s = make_wscratch(nc, "wg_s", D, 8192)
    wu_s = make_wscratch(nc, "wu_s", D, 8192)
    wd_s = make_wscratch(nc, "wd_s", 8192, D)

    C = {}
    C["ident"] = nc.alloc_sbuf_tensor("ident", [128, 128], BF16).ap()
    C["ones1"] = nc.alloc_sbuf_tensor("ones1", [1, 128], BF16).ap()
    B = alloc_l0ffn_buffers(nc)
    xst_mloc = nc.lookup_mloc(B["xstage"].tensor)
    pcb = nc.alloc_sbuf_tensor_at("pcbuf", [128, 2, 1024], BF16,
                                  offset=xst_mloc.addr).ap()

    PS = {}
    PS["p4"] = nc.alloc_psum_tensor("p4", [128, 4, 512], F32).ap()
    PS["pt"] = nc.alloc_psum_tensor("pt", [128, 2, 8, 128], BF16).ap()

    E = {k: P.sem(k) for k in ["pe", "dve", "act", "lds", "pl_ld", "pl_st",
                               "gps"]}
    reg_consts(nc, [1e-5])
    emit_constants(P, C)
    E["gps"].inc(nc.gpsimd.memset(C["ones1"][:], 1.0))
    for eng in [nc.tensor, nc.vector, nc.scalar, nc.sync]:
        E["gps"].wait(eng)
    emit_precast(P, E, [(wg_f, wg_s), (wu_f, wu_s), (wd_f, wd_s)], pcb)

    import os
    if os.environ.get("K_DEBUG"):
        dbg_wg = nc.dram_tensor("dbg_wg", [128, 16, 512], BF16, kind="ExternalOutput").ap()
        dbg_xT = nc.dram_tensor("dbg_xT", [128, 16, 128], BF16, kind="ExternalOutput").ap()
        B["dbg_wg"] = dbg_wg[:, :, :]
        B["dbg_xT"] = dbg_xT[:, :, :]
        dbg_hstf = nc.dram_tensor("dbg_hstf", [128, 512], F32, kind="ExternalOutput").ap()
        dbg_hst = nc.dram_tensor("dbg_hst", [128, 512], BF16, kind="ExternalOutput").ap()
        B["dbg_hstf"] = dbg_hstf[:, :]
        B["dbg_hst"] = dbg_hst[:, :]
    x4 = x.rearrange("s (t p) d -> s t p d", p=128)
    y4 = y.rearrange("s (t p) d -> s t p d", p=128)
    for seq in range(nseq):
        emit_l0ffn_seq(P, E, C, B, PS, wg_s, wu_s, wd_s, B["xsb"],
                       io=(x4[seq], y4[seq]))
    E["lds"].wait(nc.sync)
    return nc


def emit_xsb_load(P, E, B, x4):
    nc = P.nc
    dve, lds = E["dve"], E["lds"]
    for t in range(NT):
        dve.wait(nc.sync)
        lds.inc(nc.sync.dma_start(out=B["xstage"][:, 0:D], in_=x4[t]), 16)
        lds.wait(nc.vector)
        dve.inc(nc.vector.tensor_copy(B["xsb"][:, t, :], B["xstage"][:, 0:D]))


def emit_xsb_store(P, E, B, y4):
    nc = P.nc
    dve, lds = E["dve"], E["lds"]
    for t in range(NT):
        lds.wait(nc.vector)
        dve.inc(nc.vector.tensor_copy(B["xstage"][:, 0:D], B["xsb"][:, t, :]))
        dve.wait(nc.sync)
        lds.inc(nc.sync.dma_start(out=y4[t], in_=B["xstage"][:, 0:D]), 16)
    lds.wait(nc.sync)


def build_fused(nseq=2):
    nc = bass.Bass("TRN2", target_bir_lowering=False)
    P = Prog(nc)
    x = nc.dram_tensor("x", [nseq, L, D], F32, kind="ExternalInput").ap()
    y = nc.dram_tensor("y", [nseq, L, D], F32, kind="ExternalOutput").ap()
    wg_f = nc.dram_tensor("w_gate", [D, 8192], F32, kind="ExternalInput").ap()
    wu_f = nc.dram_tensor("w_up", [D, 8192], F32, kind="ExternalInput").ap()
    wd_f = nc.dram_tensor("w_down", [8192, D], F32, kind="ExternalInput").ap()
    win_f = nc.dram_tensor("win_f", [2176, 8512], F32,
                           kind="ExternalInput").ap()
    wout_f = nc.dram_tensor("wout_f", [4096, 2048], F32,
                            kind="ExternalInput").ap()
    conv_w = nc.dram_tensor("conv_w", [4352, 4], F32,
                            kind="ExternalInput").ap()
    a_log = nc.dram_tensor("a_log", [64], F32, kind="ExternalInput").ap()
    d_in = nc.dram_tensor("d_in", [64], F32, kind="ExternalInput").ap()
    norm_w = nc.dram_tensor("norm_w", [4096], F32, kind="ExternalInput").ap()
    wq_f = nc.dram_tensor("wqT", [D, D], F32, kind="ExternalInput").ap()
    wkv_f = nc.dram_tensor("wkvT", [D, 256], F32, kind="ExternalInput").ap()
    wo_f = nc.dram_tensor("woT", [D, D], F32, kind="ExternalInput").ap()
    nsc_f = nc.dram_tensor("nsc", [128, 16], F32, kind="ExternalInput").ap()
    ssc_f = nc.dram_tensor("ssc", [128, 16], F32, kind="ExternalInput").ap()
    mnames = ["wg_f", "wup_f", "wsw_f", "wdn_f", "wsu_f", "wsa_f", "wsd_f"]
    mshapes = [(17 * 128, 16), (17 * 128, 6144), (9 * 128, 6144),
               (8 * 128, 12288), (17 * 128, 2048), (9 * 128, 1024),
               (17 * 128, 2048)]
    fins = {nm: nc.dram_tensor(nm, list(sh), F32, kind="ExternalInput").ap()
            for nm, sh in zip(mnames, mshapes)}
    bdn_f = nc.dram_tensor("bdn_f", [8, D], F32, kind="ExternalInput").ap()

    wg_s = make_wscratch(nc, "fwg_s", D, 8192)
    wu_s = make_wscratch(nc, "fwu_s", D, 8192)
    wd_s = make_wscratch(nc, "fwd_s", 8192, D)
    win_s, wout_s = make_mamba_scratch(nc)
    wq_s = make_wscratch(nc, "wq_s", D, D)
    wkv_s = make_wscratch(nc, "wkv_s", D, 256, oc=256)
    wo_s = make_wscratch(nc, "wo_s", D, D)
    W = make_moe_scratch(nc)

    C = {}
    C["ident"] = nc.alloc_sbuf_tensor("ident", [128, 128], BF16).ap()
    C["ones1"] = nc.alloc_sbuf_tensor("ones1", [1, 128], BF16).ap()
    C["e0mat"] = nc.alloc_sbuf_tensor("e0mat", [128, 128], BF16).ap()
    C["lmask8"] = nc.alloc_sbuf_tensor("lmask8", [128, 8, 8], F32).ap()
    B = {}
    B["xsb"] = nc.alloc_sbuf_tensor("xsb", [128, NT, D], BF16).ap()
    B["xn"] = nc.alloc_sbuf_tensor("xn", [128, D], BF16).ap()
    B["xstage"] = nc.alloc_sbuf_tensor("xstage", [128, D], F32).ap()
    lnt = {}
    for nm in ["sum", "mean", "m2", "sq", "var", "sd", "rstd"]:
        lnt[nm] = nc.alloc_sbuf_tensor("ln_" + nm, [128, 1], F32).ap()
    lnt["junk"] = B["xstage"]
    B["lnt"] = lnt
    pcb = nc.alloc_sbuf_tensor_at(
        "pcbuf", [128, 2, 1024], BF16,
        offset=nc.lookup_mloc(B["xstage"].tensor).addr).ap()
    E = {k: P.sem(k) for k in ["pe", "dve", "act", "lds", "pl_ld", "pl_st",
                               "gps"]}
    reg_consts(nc, [1e-5, 1.0])

    ar = None
    for kb in range(133, 100, -1):
        try:
            ar = Arena(nc, kb * 1024)
            break
        except Exception:
            continue
    assert ar is not None, "no room for arena"

    Bf = dict(B)
    alloc_l0ffn_buffers(nc, Bf, ar.scope("A"), compact=True)
    Ba = dict(B)
    alloc_attn_buffers(nc, Ba, ar.scope("B"))
    Bm = dict(B)
    alloc_moe_buffers(nc, Bm, ar.scope("C"))
    Bs = dict(B)
    alloc_mamba_buffers(nc, Bs, ar.scope("D"), compact=True)
    Cm = dict(C)
    for nm in ["identf", "onesf", "triu", "maskneg", "e127f"]:
        Cm[nm] = ar("c_" + nm, [128, 128], F32)

    PS = {}
    PS["p4"] = nc.alloc_psum_tensor("p4", [128, 4, 512], F32).ap()
    PS["pt"] = nc.alloc_psum_tensor("pt", [128, 2, 8, 128], BF16).ap()
    PS["pav"] = nc.alloc_psum_tensor("pav", [128, 512], F32).ap()

    emit_constants(P, C)
    E["gps"].inc(nc.gpsimd.memset(C["ones1"][:], 1.0))
    for eng in [nc.tensor, nc.vector, nc.scalar, nc.sync]:
        E["gps"].wait(eng)
    emit_precast(P, E, [(wg_f, wg_s), (wu_f, wu_s), (wd_f, wd_s),
                        (win_f, win_s), (wout_f, wout_s),
                        (wq_f, wq_s), (wkv_f, wkv_s), (wo_f, wo_s)] +
                 [(fins[nm], W[nm.split("_")[0]]) for nm in mnames], pcb)

    def mamba_pre():
        sub = {k: Cm[k] for k in ["identf", "onesf", "triu", "maskneg",
                                  "e127f"]}
        emit_constants(P, sub)
        E["gps"].inc(nc.gpsimd.memset(Cm["onesf"][0:1, 0:1], 1.0))
        for eng in [nc.tensor, nc.vector, nc.scalar]:
            E["gps"].wait(eng)
        emit_mamba_setup(P, E, Cm, Bs, PS, conv_w, a_log, d_in, norm_w)

    def moe_pre():
        E["pl_ld"].inc(nc.gpsimd.dma_start(out=Bm["bdn_sb"][:],
                                           in_=bdn_f[:, :]), 16)
        E["pl_ld"].wait(nc.tensor)
        E["pl_ld"].wait(nc.vector)

    def attn_pre():
        E["lds"].inc(nc.sync.dma_start(out=Ba["nsc"][:], in_=nsc_f[:, :]), 16)
        E["lds"].inc(nc.sync.dma_start(out=Ba["ssc"][:], in_=ssc_f[:, :]), 16)
        for eng in [nc.tensor, nc.vector, nc.scalar]:
            E["lds"].wait(eng)

    x4 = x.rearrange("s (t p) d -> s t p d", p=128)
    y4 = y.rearrange("s (t p) d -> s t p d", p=128)
    for sq in range(nseq):
        emit_xsb_load(P, E, B, x4[sq])
        emit_sublayer_barrier(P, E)
        emit_l0ffn_seq(P, E, C, Bf, PS, wg_s, wu_s, wd_s, B["xsb"])
        emit_sublayer_barrier(P, E)
        mamba_pre()
        emit_mamba_seq(P, E, Cm, Bs, PS, win_s, wout_s, B["xsb"])
        emit_sublayer_barrier(P, E)
        emit_l0ffn_seq(P, E, C, Bf, PS, wg_s, wu_s, wd_s, B["xsb"])
        emit_sublayer_barrier(P, E)
        moe_pre()
        emit_moe_seq(P, E, C, Bm, PS, W, B["xsb"])
        emit_sublayer_barrier(P, E)
        attn_pre()
        emit_attn_seq(P, E, C, Ba, PS, wq_s, wkv_s, wo_s, B["xsb"])
        emit_sublayer_barrier(P, E)
        moe_pre()
        emit_moe_seq(P, E, C, Bm, PS, W, B["xsb"])
        emit_sublayer_barrier(P, E)
        emit_xsb_store(P, E, B, y4[sq])
    E["lds"].wait(nc.sync)
    return nc


def kernel(**inputs):
    x = np.ascontiguousarray(inputs["L0_attn_x"], dtype=np.float32)

    w_in = inputs["L0_attn_W_in"]
    bias_row = np.zeros((1, 8512), np.float32)
    bias_row[0, 8448:8512] = inputs["L0_attn_dt_bias"]
    win_f = np.concatenate([np.ascontiguousarray(w_in.T), bias_row,
                            np.zeros((127, 8512), np.float32)], axis=0)
    wout_T = np.ascontiguousarray(inputs["L0_attn_W_out"].T)
    mamba_in = {"win_f": win_f, "wout_f": wout_T,
                "conv_w": np.ascontiguousarray(inputs["L0_attn_conv_w"]),
                "a_log": np.ascontiguousarray(inputs["L0_attn_A_log"]),
                "d_in": np.ascontiguousarray(inputs["L0_attn_D"]),
                "norm_w": np.ascontiguousarray(inputs["L0_attn_norm_w"])}
    ffn_in = {"w_gate": np.ascontiguousarray(inputs["L0_ffn_w_gate"]),
              "w_up": np.ascontiguousarray(inputs["L0_ffn_w_up"]),
              "w_down": np.ascontiguousarray(inputs["L0_ffn_w_down"])}
    attn_in = {
        "wqT": np.ascontiguousarray(inputs["L1_attn_Wq"].T),
        "wkvT": np.ascontiguousarray(np.concatenate(
            [inputs["L1_attn_Wk"].T, inputs["L1_attn_Wv"].T], axis=1)),
        "woT": np.ascontiguousarray(inputs["L1_attn_Wo"].T),
        "nsc": np.ascontiguousarray(
            inputs["L1_attn_norm_scale"].reshape(16, 128).T),
        "ssc": np.ascontiguousarray(
            inputs["L1_attn_split_scale"].reshape(16, 128).T)}
    hw = moe_host_weights(
        inputs["L1_ffn_Wg"], inputs["L1_ffn_bg"], inputs["L1_ffn_Wup"],
        inputs["L1_ffn_bup"], inputs["L1_ffn_Wsw"], inputs["L1_ffn_bsw"],
        inputs["L1_ffn_Wdn"], inputs["L1_ffn_bdn"], inputs["L1_ffn_Wsu"],
        inputs["L1_ffn_bsu"], inputs["L1_ffn_Wsa"], inputs["L1_ffn_bsa"],
        inputs["L1_ffn_Wsd"], inputs["L1_ffn_bsd"])
    moe_in = {nm: hw[nm] for nm in ["wg_f", "wup_f", "wsw_f", "wdn_f",
                                    "wsu_f", "wsa_f", "wsd_f", "bdn_f"]}

    prog = build_fused(2)
    wall = {}
    wall.update(ffn_in)
    wall.update(mamba_in)
    wall.update(attn_in)
    wall.update(moe_in)
    per_core = []
    for c in range(NCORES):
        d = {"x": x[2 * c:2 * c + 2]}
        d.update(wall)
        per_core.append(d)
    res = run_bass_kernel_spmd(prog, per_core, core_ids=list(range(NCORES)))
    out = np.concatenate(
        [np.asarray(res.results[c]["y"], dtype=np.float32)
         for c in range(NCORES)], axis=0)
    return out.reshape(16, L, D)


# version 04 (final)
# speedup vs baseline: 1.0055x; 1.0055x over previous
import os as _os
import numpy as np
from contextlib import ExitStack, contextmanager

import concourse.bass as bass
import concourse.mybir as mybir
from concourse.bass_utils import run_bass_kernel_spmd

BF16 = mybir.dt.bfloat16
F32 = mybir.dt.float32
AF = mybir.ActivationFunctionType
ALU = mybir.AluOpType
AX = mybir.AxisListType
ET = mybir.EngineType

NCORES = 8
L = 2048
D = 2048
NT = L // 128
GT = 4
NG = NT // GT
KD = D // 128


class Sem:

    def __init__(self, prog, name):
        self.name = name
        self.h = prog.es.enter_context(prog.nc.semaphore(name))
        self.c = 0

    def inc(self, inst, n=1):
        inst.then_inc(self.h, n)
        self.c += n
        return inst

    def cur(self, extra=0):
        return self.c + extra

    def wait(self, eng, extra=0):
        v = self.c + extra
        if v > 0:
            eng.wait_ge(self.h, v)


class Prog:

    CHUNK = 3500

    def __init__(self, nc):
        self.nc = nc
        self.es = ExitStack()
        self.sems = []
        self._chunk_n = 0
        self._next_break = self.CHUNK

    def sem(self, name):
        s = Sem(self, name)
        self.sems.append(s)
        return s

    def maybe_chunk(self):
        n = self.nc.n_instructions()
        if n >= self._next_break:
            self._chunk_n += 1
            nm = f"chunk{self._chunk_n}"
            self.nc.br(nm)
            self.nc.switch_bb(nm)
            self._next_break = self.nc.n_instructions() + self.CHUNK


class Arena:

    def __init__(self, nc, nbytes):
        self.nc = nc
        t = nc.alloc_sbuf_tensor(f"arena{nbytes}", [128, nbytes // 2],
                                 BF16).ap()
        self.base = nc.lookup_mloc(t.tensor).addr
        self.nbytes = nbytes
        self.cur = 0
        self.pfx = ""

    def scope(self, pfx):
        self.cur = 0
        self.pfx = pfx
        return self

    def __call__(self, name, shape, dt):
        esz = 2 if dt == BF16 else 4
        nb = 1
        for d in shape[1:]:
            nb *= d
        nb *= esz
        off = self.base + self.cur
        self.cur += (nb + 127) // 128 * 128
        assert self.cur <= self.nbytes, f"arena overflow at {name}: {self.cur}"
        return self.nc.alloc_sbuf_tensor_at(self.pfx + name, shape, dt,
                                            offset=off).ap()


def emit_sublayer_barrier(P, E):
    nc = P.nc
    engs = [nc.tensor, nc.vector, nc.scalar, nc.sync, nc.gpsimd]
    for k in ["pe", "dve", "act", "lds"]:
        for eng in engs:
            E[k].wait(eng)


def reg_consts(nc, vals, dt=F32):
    for i, v in enumerate(vals):
        t = nc.alloc_sbuf_tensor(f"cstv_{i}", [128, 1], dt).ap()
        nc.gpsimd.memset(t[:], float(v))
        nc.const_aps.register(dt, float(v), t[:])


def emit_constants(P, c):
    nc = P.nc
    g = nc.gpsimd
    if "ident" in c:
        g.memset(c["ident"][:], 0.0)
        g.affine_select(out=c["ident"][:], in_=c["ident"][:],
                        compare_op=ALU.not_equal, fill=1.0, base=0,
                        pattern=[[-1, 128]], channel_multiplier=1)
    if "ones1" in c:
        g.memset(c["ones1"][:], 1.0)
    if "triu" in c:
        g.memset(c["triu"][:], 0.0)
        g.affine_select(out=c["triu"][:], in_=c["triu"][:], compare_op=ALU.is_gt,
                        fill=1.0, base=0, pattern=[[-1, 128]], channel_multiplier=1)
    if "maskneg" in c:
        g.memset(c["maskneg"][:], 0.0)
        g.affine_select(out=c["maskneg"][:], in_=c["maskneg"][:],
                        compare_op=ALU.is_ge, fill=-1e30, base=0,
                        pattern=[[1, 128]], channel_multiplier=-1)
    if "e0mat" in c:
        g.memset(c["e0mat"][:], 0.0)
        g.affine_select(out=c["e0mat"][:], in_=c["e0mat"][:],
                        compare_op=ALU.not_equal, fill=1.0, base=0,
                        pattern=[[0, 128]], channel_multiplier=1)
    if "lmask8" in c:
        g.memset(c["lmask8"][:], 0.0)
        g.affine_select(out=c["lmask8"][:], in_=c["lmask8"][:],
                        compare_op=ALU.is_ge, fill=1.0, base=0,
                        pattern=[[-1, 8], [1, 8]], channel_multiplier=0)
    if "e127" in c:
        g.memset(c["e127"][:], 0.0)
        g.affine_select(out=c["e127"][:], in_=c["e127"][:],
                        compare_op=ALU.not_equal, fill=1.0, base=-127,
                        pattern=[[0, 128]], channel_multiplier=1)
    if "e127f" in c:
        g.memset(c["e127f"][:], 0.0)
        g.affine_select(out=c["e127f"][:], in_=c["e127f"][:],
                        compare_op=ALU.not_equal, fill=1.0, base=-127,
                        pattern=[[0, 128]], channel_multiplier=1)
    if "identf" in c:
        g.memset(c["identf"][:], 0.0)
        g.affine_select(out=c["identf"][:], in_=c["identf"][:],
                        compare_op=ALU.not_equal, fill=1.0, base=0,
                        pattern=[[-1, 128]], channel_multiplier=1)
    if "onesf" in c:
        g.memset(c["onesf"][:], 1.0)


def make_wscratch(nc, name, din, dout, oc=512):
    n_k, n_oc = din // 128, (dout + oc - 1) // oc
    t = nc.dram_tensor(name, [n_oc, 128, n_k, oc], BF16, kind="Internal")
    return t.ap()


def emit_precast(P, E, jobs, buf):
    nc = P.nc
    CB = buf.shape[2]
    lds, sts = E["pl_ld"], E["pl_st"]
    work = []
    for src, dst in jobs:
        din, dout = src.shape
        n_oc, _, n_k, OC = dst.shape
        assert n_k * 128 == din
        for r in range(n_k):
            for c0 in range(0, dout, CB):
                cw = min(CB, dout - c0)
                work.append((src, dst, r, c0, cw, OC))
    for idx, (src, dst, r, c0, cw, OC) in enumerate(work):
        slot = idx % 2
        if idx >= 2:
            nc.gpsimd.wait_ge(sts.h, sts.cur(-16))
        lds.inc(nc.gpsimd.dma_start(
            out=buf[:, slot, 0:cw],
            in_=src[r * 128:(r + 1) * 128, c0:c0 + cw]), 16)
        lds.wait(nc.sync)
        o0 = c0 // OC
        if cw % OC == 0:
            dv = dst[o0:o0 + cw // OC, :, r, :].rearrange("o p i -> p o i")
            bv = buf[:, slot, 0:cw].rearrange("p (o i) -> p o i", i=OC)
        else:
            dv = dst[o0:o0 + 1, :, r, 0:cw].rearrange("o p i -> p o i")
            bv = buf[:, slot, 0:cw].rearrange("p (o i) -> p o i", i=cw)
        sts.inc(nc.sync.dma_start(out=dv, in_=bv), 16)
    sts.wait(nc.sync)
    sts.wait(nc.gpsimd)


def emit_ln_tile(P, E, x_src, xn_bf, t, eps=1e-5, xn_war=()):
    nc = P.nc
    dve, act = E["dve"], E["act"]
    n = x_src.shape[-1]
    dve.inc(nc.vector.reduce_sum(t["sum"][:], x_src, axis=AX.X))
    dve.wait(nc.vector)
    dve.inc(nc.vector.tensor_scalar_mul(t["mean"][:], t["sum"][:], 1.0 / n))
    dve.wait(nc.vector)
    dve.inc(nc.vector.tensor_scalar(t["m2"][:], t["mean"][:], t["mean"][:],
                                    None, ALU.mult))
    dve.wait(nc.scalar)
    act.inc(nc.scalar.activation(t["junk"][:, 0:n], x_src, AF.Square,
                                 accum_out=t["sq"][:]))
    act.wait(nc.vector)
    dve.wait(nc.vector)
    dve.inc(nc.vector.tensor_scalar(t["var"][:], t["sq"][:], 1.0 / n,
                                    t["m2"][:], ALU.mult, ALU.subtract))
    dve.wait(nc.scalar)
    act.inc(nc.scalar.activation(t["sd"][:], t["var"][:], AF.Sqrt, bias=eps))
    act.wait(nc.vector)
    dve.inc(nc.vector.reciprocal(t["rstd"][:], t["sd"][:]))
    dve.wait(nc.vector)
    for s in xn_war:
        s.wait(nc.vector)
    dve.inc(nc.vector.tensor_scalar(xn_bf[:, 0:n], x_src, t["mean"][:],
                                    t["rstd"][:], ALU.subtract, ALU.mult))


def emit_transposes(P, E, src_bf, n, ptile, ident, dst_fn=None,
                    dst_batch_fn=None, src_off=0):
    nc = P.nc
    pe, dve = E["pe"], E["dve"]
    S = ptile.shape[1]
    j = 0
    while j < n:
        m = min(S, n - j)
        dve.wait(nc.tensor)
        for k in range(m):
            ins = nc.tensor.transpose(
                ptile[:, k, :],
                src_bf[:, src_off + (j + k) * 128: src_off + (j + k + 1) * 128],
                ident[:])
        pe.inc(ins)
        pe.wait(nc.vector)
        if dst_batch_fn is not None:
            ins = nc.vector.tensor_copy(dst_batch_fn(j, m), ptile[:, 0:m, :])
        else:
            for k in range(m):
                ins = nc.vector.tensor_copy(dst_fn(j + k), ptile[:, k, :])
        dve.inc(ins)
        j += m


def emit_sparsify(P, E, h, habs, hc, hr, W=512, out_bf=None):
    nc = P.nc
    dve, act = E["dve"], E["act"]
    dve.wait(nc.scalar)
    act.inc(nc.scalar.activation(habs[:, 0:W], h[:, 0:W], AF.Abs))
    act.wait(nc.vector)
    q = W // 4
    av = [habs[:, r:W:4] for r in range(4)]
    c01, c02, c03 = hc[:, 0, 0:q], hc[:, 1, 0:q], hc[:, 2, 0:q]
    c12, c13, c23 = hc[:, 3, 0:q], hc[:, 4, 0:q], hc[:, 5, 0:q]
    nc.vector.tensor_tensor(c01, av[0], av[1], ALU.is_ge)
    nc.vector.tensor_tensor(c02, av[0], av[2], ALU.is_ge)
    nc.vector.tensor_tensor(c03, av[0], av[3], ALU.is_ge)
    nc.vector.tensor_tensor(c12, av[1], av[2], ALU.is_ge)
    nc.vector.tensor_tensor(c13, av[1], av[3], ALU.is_ge)
    nc.vector.tensor_tensor(c23, av[2], av[3], ALU.is_ge)
    t0, t1, t2, t3 = (hr[:, i, 0:q] for i in range(4))
    nc.vector.tensor_add(t0, c01, c02)
    nc.vector.tensor_add(t0, t0, c03)
    nc.vector.tensor_add(t1, c12, c13)
    nc.vector.tensor_sub(t1, t1, c01)
    nc.vector.tensor_sub(t2, c23, c02)
    nc.vector.tensor_sub(t2, t2, c12)
    nc.vector.tensor_add(t3, c03, c13)
    nc.vector.tensor_add(t3, t3, c23)
    hv = [h[:, r:W:4] for r in range(4)]
    ov = [(out_bf[:, r:W:4] if out_bf is not None else hv[r]) for r in range(4)]
    nc.vector.scalar_tensor_tensor(ov[0], t0, 1.5, hv[0], ALU.is_ge, ALU.mult)
    nc.vector.scalar_tensor_tensor(ov[1], t1, 0.5, hv[1], ALU.is_ge, ALU.mult)
    nc.vector.scalar_tensor_tensor(ov[2], t2, -0.5, hv[2], ALU.is_ge, ALU.mult)
    dve.inc(nc.vector.scalar_tensor_tensor(ov[3], t3, 1.5, hv[3],
                                           ALU.is_le, ALU.mult))


def emit_l0ffn_seq(P, E, C, B, PS, wg_s, wu_s, wd_s, xsb, io=None):
    nc = P.nc
    pe, dve, act, lds = E["pe"], E["dve"], E["act"], E["lds"]
    gd = E["gd"]
    NOC = 16
    p4, pt = PS["p4"], PS["pt"]
    ident = C["ident"]
    lnt = B["lnt"]
    for g in range(NG):
        if io is not None:
            x_in, _ = io
            for tt in range(GT):
                dve.wait(nc.sync)
                lds.inc(nc.sync.dma_start(out=B["xstage"][:],
                                          in_=x_in[g * GT + tt]), 16)
                lds.wait(nc.vector)
                dve.inc(nc.vector.tensor_copy(xsb[:, g * GT + tt, :],
                                              B["xstage"][:]))
        for tt in range(GT):
            emit_ln_tile(P, E, xsb[:, g * GT + tt, :], B["xn"], lnt,
                         xn_war=(pe,))

            emit_transposes(
                P, E, B["xn"], KD, pt[:, tt % 2], ident,
                dst_batch_fn=lambda j, m, tt=tt: B["xT"][:, tt, j:j + m, :])
        for oc in range(NOC):
            P.maybe_chunk()
            pe.wait(nc.sync)
            pe.wait(nc.gpsimd)
            lds.inc(nc.sync.dma_start(out=B["wg"][:], in_=wg_s[oc]), 16)
            gd.inc(nc.gpsimd.dma_start(out=B["wu"][:], in_=wu_s[oc]), 16)
            if "dbg_wg" in B and g == 0 and oc == 0:
                lds.wait(nc.sync)
                E["lds"].inc(nc.sync.dma_start(out=B["dbg_wg"],
                                               in_=B["wg"][:]), 16)
                E["lds"].inc(nc.sync.dma_start(out=B["dbg_xT"],
                                               in_=B["xT"][:, 0, :, :]), 16)
                lds.wait(nc.sync)
            for tt in range(GT):
                pg = p4[:, tt % 2, :]
                pu = p4[:, 2 + tt % 2, :]
                act.wait(nc.tensor)
                dve.wait(nc.tensor)
                lds.wait(nc.tensor)
                gd.wait(nc.tensor)
                for k in range(KD):
                    ins = nc.tensor.matmul(pg, B["xT"][:, tt, k, :],
                                           B["wg"][:, k, :],
                                           start=(k == 0), stop=(k == KD - 1))
                pe.inc(ins)
                for k in range(KD):
                    ins = nc.tensor.matmul(pu, B["xT"][:, tt, k, :],
                                           B["wu"][:, k, :],
                                           start=(k == 0), stop=(k == KD - 1))
                pe.inc(ins)
                pe.wait(nc.scalar, -1)
                act.inc(nc.scalar.activation(B["sg"][:, 0, :], pg, AF.Silu))
                pe.wait(nc.vector)
                act.wait(nc.vector)
                dve.inc(nc.vector.scalar_tensor_tensor(
                    B["hstf"][:, 0, :], pu, 1.0, B["sg"][:, 0, :],
                    ALU.mult, ALU.mult))
                if "dbg_hstf" in B and g == 0 and tt == 0 and oc == 0:
                    dve.wait(nc.sync)
                    E["lds"].inc(nc.sync.dma_start(out=B["dbg_hstf"],
                                                   in_=B["hstf"][:, 0, :]), 16)
                    E["lds"].wait(nc.vector)
                emit_sparsify(P, E, B["hstf"][:, 0], B["habs"][:, 0],
                              B["hc"], B["hr"], out_bf=B["hst"][:, tt % 2])
                if "dbg_hst" in B and g == 0 and tt == 0 and oc == 0:
                    dve.wait(nc.sync)
                    E["lds"].inc(nc.sync.dma_start(out=B["dbg_hst"],
                                                   in_=B["hst"][:, tt % 2, :]), 16)
                    E["lds"].wait(nc.vector)
                dve.wait(nc.tensor)
                ptb = pt[:, tt % 2]
                for s4 in range(4):
                    ins = nc.tensor.transpose(
                        ptb[:, s4, :],
                        B["hst"][:, tt % 2, s4 * 128:(s4 + 1) * 128], ident[:])
                pe.inc(ins)
                pe.wait(nc.vector)
                dve.inc(nc.vector.tensor_copy(
                    B["hsT"][:, tt, oc * 4:oc * 4 + 4, :], ptb[:, 0:4, :]))
        for o2 in range(4):
            P.maybe_chunk()
            for piece in range(8):
                pe.wait(nc.sync)
                if piece == 0:
                    act.wait(nc.tensor)
                    dve.wait(nc.tensor)
                pe.wait(nc.gpsimd)
                lds.inc(nc.sync.dma_start(
                    out=B["wd"][:, 0:4, :],
                    in_=wd_s[o2, :, piece * 8:piece * 8 + 4, :]), 16)
                gd.inc(nc.gpsimd.dma_start(
                    out=B["wd"][:, 4:8, :],
                    in_=wd_s[o2, :, piece * 8 + 4:piece * 8 + 8, :]), 16)
                lds.wait(nc.tensor)
                gd.wait(nc.tensor)
                for tt in range(GT):
                    for k in range(8):
                        ins = nc.tensor.matmul(
                            p4[:, tt, :], B["hsT"][:, tt, piece * 8 + k, :],
                            B["wd"][:, k, :],
                            start=(piece == 0 and k == 0),
                            stop=(piece == 7 and k == 7))
                    if tt == GT - 1:
                        pe.inc(ins)
            for tt in range(GT):
                pe.wait(nc.vector)
                xap = xsb[:, g * GT + tt, o2 * 512:(o2 + 1) * 512]
                dve.inc(nc.vector.scalar_tensor_tensor(
                    xap, p4[:, tt, :], 0.5, xap, ALU.mult, ALU.add))
        if io is not None:
            for tt in range(GT):
                lds.wait(nc.vector)
                dve.inc(nc.vector.tensor_copy(B["xstage"][:],
                                              xsb[:, g * GT + tt, :]))
                dve.wait(nc.sync)
                lds.inc(nc.sync.dma_start(out=io[1][g * GT + tt],
                                          in_=B["xstage"][:]), 16)
            lds.wait(nc.sync)


def emit_attn_seq(P, E, C, B, PS, wq_s, wkv_s, wo_s, xsb, io=None):
    nc = P.nc
    pe, dve, act, lds = E["pe"], E["dve"], E["act"], E["lds"]
    gd = E["gd"]
    p4, pt, pav = PS["p4"], PS["pt"], PS["pav"]
    ident = C["ident"]
    lnt = B["lnt"]

    if io is not None:
        x_in, _ = io
        for t in range(NT):
            dve.wait(nc.sync)
            lds.inc(nc.sync.dma_start(out=B["xstage"][:, 0:D], in_=x_in[t]), 16)
            lds.wait(nc.vector)
            dve.inc(nc.vector.tensor_copy(xsb[:, t, :], B["xstage"][:, 0:D]))

    dve.wait(nc.vector)
    dve.inc(nc.vector.tensor_mul(B["scl"][:], B["nsc"][:], B["ssc"][:]))
    pe.wait(nc.sync)
    dve.wait(nc.sync)
    lds.inc(nc.sync.dma_start(out=B["wkv"][:], in_=wkv_s[0]), 16)
    lds.wait(nc.sync)

    for g in range(NG):
        P.maybe_chunk()
        for tt in range(GT):
            t_idx = g * GT + tt
            emit_ln_tile(P, E, xsb[:, t_idx, :], B["xn"], lnt, xn_war=(pe,))
            dve.wait(nc.scalar)
            act.inc(nc.scalar.activation(lnt["junk"][:, :], B["xn"][:],
                                         AF.Square, accum_out=lnt["sq"][:]))
            dve.wait(nc.scalar)
            act.inc(nc.scalar.activation(lnt["sd"][:], lnt["sq"][:], AF.Sqrt,
                                         scale=1.0 / D))
            act.wait(nc.vector)
            dve.inc(nc.vector.tensor_scalar_add(lnt["sd"][:], lnt["sd"][:], 1e-8))
            dve.wait(nc.vector)
            dve.inc(nc.vector.reciprocal(lnt["rstd"][:], lnt["sd"][:]))
            dve.wait(nc.vector)
            pe.wait(nc.vector)
            dve.inc(nc.vector.tensor_scalar(B["un"][:, 0:D], B["xn"][:],
                                            lnt["rstd"][:], None, ALU.mult))
            for half in range(2):
                dve.wait(nc.tensor)
                for k in range(8):
                    kk = half * 8 + k
                    ins = nc.tensor.transpose(
                        pt[:, tt % 2, k, :],
                        B["un"][:, kk * 128:(kk + 1) * 128], ident[:])
                pe.inc(ins)
                pe.wait(nc.vector)
                for k in range(8):
                    kk = half * 8 + k
                    ins = nc.vector.tensor_scalar(
                        B["uTall"][:, kk, t_idx * 128:(t_idx + 1) * 128],
                        pt[:, tt % 2, k, :], B["scl"][:, kk:kk + 1],
                        None, ALU.mult)
                dve.inc(ins)
        gc = slice(g * 512, (g + 1) * 512)
        act.wait(nc.tensor)
        dve.wait(nc.tensor)
        lds.wait(nc.tensor)
        for k in range(KD):
            ins = nc.tensor.matmul(p4[:, 0, :], B["wkv"][:, k, 0:128],
                                   B["uTall"][:, k, gc],
                                   start=(k == 0), stop=(k == KD - 1))
        pe.inc(ins)
        pe.wait(nc.vector)
        lds.wait(nc.vector)
        dve.inc(nc.vector.tensor_copy(B["ktmp"][:], p4[:, 0, :]))
        dve.wait(nc.vector)
        nc.vector.tensor_copy(B["kTboth"][0:64, 0, gc], B["ktmp"][0:64, :])
        dve.inc(nc.vector.tensor_copy(B["kTboth"][64:128, 1, gc],
                                      B["ktmp"][64:128, :]))
        dve.wait(nc.sync)
        lds.inc(nc.sync.dma_start(out=B["kTboth"][64:128, 0, gc],
                                  in_=B["ktmp"][0:64, :]), 16)
        lds.inc(nc.sync.dma_start(out=B["kTboth"][0:64, 1, gc],
                                  in_=B["ktmp"][64:128, :]), 16)
        for tt in range(GT):
            t_idx = g * GT + tt
            dve.wait(nc.tensor)
            for k in range(KD):
                ins = nc.tensor.matmul(
                    p4[:, 1, 0:128], B["uTall"][:, k, t_idx * 128:(t_idx + 1) * 128],
                    B["wkv"][:, k, 128:256], start=(k == 0), stop=(k == KD - 1))
            pe.inc(ins)
            pe.wait(nc.vector)
            dve.inc(nc.vector.tensor_copy(B["vv"][:, t_idx, :], p4[:, 1, 0:128]))

    for g in range(NG):
        gc = slice(g * 512, (g + 1) * 512)
        for ocq in range(4):
            P.maybe_chunk()
            pe.wait(nc.sync)
            lds.inc(nc.sync.dma_start(out=B["wqb"][:], in_=wq_s[ocq]), 16)
            lds.wait(nc.sync)
            lds.wait(nc.tensor)
            for sub in range(4):
                qfc = ocq * 4 + sub
                dve.wait(nc.tensor)
                for k in range(KD):
                    ins = nc.tensor.matmul(
                        p4[:, sub, :], B["wqb"][:, k, sub * 128:(sub + 1) * 128],
                        B["uTall"][:, k, gc], start=(k == 0), stop=(k == KD - 1))
                pe.inc(ins)
                pe.wait(nc.vector)
                dve.inc(nc.vector.tensor_copy(B["qTg"][:, qfc, :], p4[:, sub, :]))
        for st in range(GT):
            P.maybe_chunk()
            t_idx = g * GT + st
            for h in range(32):
                b = (h % 2) * 64
                qfc = h // 2
                hg = h // 16
                act.wait(nc.tensor)
                dve.wait(nc.tensor)
                for tc in range(4):
                    ins = nc.tensor.matmul(
                        p4[b:b + 64, tc, :] if False else p4[:, tc, :],
                        B["qTg"][b:b + 64, qfc, st * 128:(st + 1) * 128],
                        B["kTboth"][b:b + 64, hg, tc * 512:(tc + 1) * 512],
                        start=True, stop=True)
                pe.inc(ins)
                pe.wait(nc.vector)
                dve.inc(nc.vector.reduce_max(B["mc"][:, 0:4],
                                             p4[:, :, :], axis=AX.X))
                dve.wait(nc.vector)
                dve.inc(nc.vector.reduce_max(B["mc"][:, 4:5], B["mc"][:, 0:4],
                                             axis=AX.X))
                dve.wait(nc.vector)
                dve.inc(nc.vector.tensor_scalar_mul(B["mc"][:, 5:6],
                                                    B["mc"][:, 4:5], -0.125))
                dve.wait(nc.scalar)
                for tc in range(4):
                    ins = nc.scalar.activation(B["pb"][:, tc, :], p4[:, tc, :],
                                               AF.Exp, scale=0.125,
                                               bias=B["mc"][:, 5:6])
                act.inc(ins)
                act.wait(nc.vector)
                dve.inc(nc.vector.reduce_sum(B["mc"][:, 6:7],
                                             B["pb"][:, :, :], axis=AX.XY))
                dve.wait(nc.vector)
                dve.inc(nc.vector.reciprocal(B["mc"][:, 7:8], B["mc"][:, 6:7]))
                dve.wait(nc.tensor)
                act.wait(nc.tensor)
                for tk in range(16):
                    ins = nc.tensor.transpose(
                        pt[:, tk // 8, tk % 8, :],
                        B["pb"][:, :, :].rearrange("p a b -> p (a b)")
                        [:, tk * 128:(tk + 1) * 128], ident[:])
                    if tk == 7 or tk == 15:
                        pe.inc(ins)
                pe.wait(nc.vector, -1)
                nc.vector.tensor_copy(B["PT"][:, 0:8, :], pt[:, 0, :, :])
                pe.wait(nc.vector)
                dve.inc(nc.vector.tensor_copy(B["PT"][:, 8:16, :], pt[:, 1, :, :]))
                dve.wait(nc.tensor)
                for tk in range(16):
                    ins = nc.tensor.matmul(
                        pav[:, 0:64], B["PT"][:, tk, :],
                        B["vv"][:, tk, hg * 64:(hg + 1) * 64],
                        start=(tk == 0), stop=(tk == 15))
                pe.inc(ins)
                pe.wait(nc.vector)
                dve.wait(nc.vector)
                dve.inc(nc.vector.tensor_scalar(
                    B["ao"][:, h * 64:(h + 1) * 64], pav[:, 0:64],
                    B["mc"][:, 7:8], None, ALU.mult))
            for half in range(2):
                dve.wait(nc.tensor)
                for k in range(8):
                    kk = half * 8 + k
                    ins = nc.tensor.transpose(
                        pt[:, st % 2, k, :],
                        B["ao"][:, kk * 128:(kk + 1) * 128], ident[:])
                pe.inc(ins)
                pe.wait(nc.vector)
                for k in range(8):
                    kk = half * 8 + k
                    ins = nc.vector.tensor_copy(B["aoTg"][:, kk, st, :],
                                                pt[:, st % 2, k, :])
                dve.inc(ins)
        for tt in range(GT):
            t_idx = g * GT + tt
            emit_ln_tile(P, E, xsb[:, t_idx, :], B["xn"], lnt, xn_war=(pe,))
            dve.wait(nc.vector)
            dve.inc(nc.vector.tensor_add(xsb[:, t_idx, :], xsb[:, t_idx, :],
                                         B["xn"][:]))
        for oc in range(4):
            P.maybe_chunk()
            pe.wait(nc.sync)
            lds.inc(nc.sync.dma_start(out=B["wqb"][:], in_=wo_s[oc]), 16)
            lds.wait(nc.sync)
            lds.wait(nc.tensor)
            for tt in range(GT):
                t_idx = g * GT + tt
                act.wait(nc.tensor)
                dve.wait(nc.tensor)
                for k in range(KD):
                    ins = nc.tensor.matmul(
                        p4[:, tt, :], B["aoTg"][:, k, tt, :], B["wqb"][:, k, :],
                        start=(k == 0), stop=(k == KD - 1))
                pe.inc(ins)
                pe.wait(nc.vector)
                xap = xsb[:, t_idx, oc * 512:(oc + 1) * 512]
                dve.inc(nc.vector.scalar_tensor_tensor(
                    xap, p4[:, tt, :], 1.0, xap, ALU.mult, ALU.add))

    if io is not None:
        for t in range(NT):
            lds.wait(nc.vector)
            dve.inc(nc.vector.tensor_copy(B["xstage"][:, 0:D], xsb[:, t, :]))
            if _os.environ.get("K_MOE_SLOWST"):
                for _ in range(10):
                    dve.inc(nc.vector.tensor_copy(B["gstage"][:, :],
                                                  B["gstage"][:, :]))
            dve.wait(nc.sync)
            lds.inc(nc.sync.dma_start(out=io[1][t], in_=B["xstage"][:, 0:D]), 16)
        lds.wait(nc.sync)


def alloc_attn_buffers(nc, B, al=None):
    if al is None:
        al = lambda n, sh, dt: nc.alloc_sbuf_tensor(n, sh, dt).ap()
    B["uTall"] = al("uTall", [128, KD, L], BF16)
    B["kTboth"] = al("kTboth", [128, 2, L], BF16)
    B["vv"] = al("vv", [128, NT, 128], BF16)
    B["wqb"] = al("wqb", [128, 16, 512], BF16)
    B["qTg"] = al("qTg", [128, 16, 512], BF16)
    B["PT"] = al("PT", [128, 16, 128], BF16)
    B["aoTg"] = al("aoTg", [128, KD, GT, 128], BF16)
    B["mc"] = al("mc", [128, 8], F32)
    B["scl"] = al("scl", [128, 16], F32)
    B["nsc"] = al("nsc_sb", [128, 16], F32)
    B["ssc"] = al("ssc_sb", [128, 16], F32)
    B["wkv"] = nc.alloc_sbuf_tensor_at(
        "wkvb", [128, 16, 256], BF16,
        offset=nc.lookup_mloc(B["qTg"].tensor).addr).ap()
    B["ktmp"] = nc.alloc_sbuf_tensor_at(
        "ktmp", [128, 512], BF16,
        offset=nc.lookup_mloc(B["PT"].tensor).addr).ap()
    B["pb"] = nc.alloc_sbuf_tensor_at(
        "pb", [128, 4, 512], BF16,
        offset=nc.lookup_mloc(B["xstage"].tensor).addr).ap()
    B["ao"] = nc.alloc_sbuf_tensor_at(
        "ao", [128, D], BF16,
        offset=nc.lookup_mloc(B["xstage"].tensor).addr + 4096).ap()
    B["un"] = nc.alloc_sbuf_tensor_at(
        "un", [128, D], BF16,
        offset=nc.lookup_mloc(B["xstage"].tensor).addr + 4096).ap()
    return B


def build_attn_standalone(nseq=1):
    nc = bass.Bass("TRN2", target_bir_lowering=False)
    P = Prog(nc)
    x = nc.dram_tensor("x", [nseq, L, D], F32, kind="ExternalInput").ap()
    wq_f = nc.dram_tensor("wqT", [D, D], F32, kind="ExternalInput").ap()
    wkv_f = nc.dram_tensor("wkvT", [D, 256], F32, kind="ExternalInput").ap()
    wo_f = nc.dram_tensor("woT", [D, D], F32, kind="ExternalInput").ap()
    nsc_f = nc.dram_tensor("nsc", [128, 16], F32, kind="ExternalInput").ap()
    ssc_f = nc.dram_tensor("ssc", [128, 16], F32, kind="ExternalInput").ap()
    y = nc.dram_tensor("y", [nseq, L, D], F32, kind="ExternalOutput").ap()
    wq_s = make_wscratch(nc, "wq_s", D, D)
    wkv_s = make_wscratch(nc, "wkv_s", D, 256, oc=256)
    wo_s = make_wscratch(nc, "wo_s", D, D)

    C = {}
    C["ident"] = nc.alloc_sbuf_tensor("ident", [128, 128], BF16).ap()
    C["ones1"] = nc.alloc_sbuf_tensor("ones1", [1, 128], BF16).ap()
    B = {}
    B["xsb"] = nc.alloc_sbuf_tensor("xsb", [128, NT, D], BF16).ap()
    B["xn"] = nc.alloc_sbuf_tensor("xn", [128, D], BF16).ap()
    B["xstage"] = nc.alloc_sbuf_tensor("xstage", [128, D], F32).ap()
    lnt = {}
    for nm in ["sum", "mean", "m2", "sq", "var", "sd", "rstd"]:
        lnt[nm] = nc.alloc_sbuf_tensor("ln_" + nm, [128, 1], F32).ap()
    lnt["junk"] = B["xstage"]
    B["lnt"] = lnt
    alloc_attn_buffers(nc, B)
    xst_mloc = nc.lookup_mloc(B["xstage"].tensor)
    pcb = nc.alloc_sbuf_tensor_at("pcbuf", [128, 2, 1024], BF16,
                                  offset=xst_mloc.addr).ap()

    PS = {}
    PS["p4"] = nc.alloc_psum_tensor("p4", [128, 4, 512], F32).ap()
    PS["pav"] = nc.alloc_psum_tensor("pav", [128, 512], F32).ap()
    PS["pt"] = nc.alloc_psum_tensor("pt", [128, 2, 8, 128], BF16).ap()

    E = {k: P.sem(k) for k in ["pe", "dve", "act", "lds", "pl_ld", "pl_st",
                               "gps", "gd"]}
    reg_consts(nc, [1e-5])
    emit_constants(P, C)
    E["gps"].inc(nc.gpsimd.memset(C["ones1"][:], 1.0))
    for eng in [nc.tensor, nc.vector, nc.scalar, nc.sync]:
        E["gps"].wait(eng)
    E["lds"].inc(nc.sync.dma_start(out=B["nsc"][:], in_=nsc_f[:, :]), 16)
    E["lds"].inc(nc.sync.dma_start(out=B["ssc"][:], in_=ssc_f[:, :]), 16)
    emit_precast(P, E, [(wq_f, wq_s), (wkv_f, wkv_s), (wo_f, wo_s)], pcb)

    x4 = x.rearrange("s (t p) d -> s t p d", p=128)
    y4 = y.rearrange("s (t p) d -> s t p d", p=128)
    for sq in range(nseq):
        emit_attn_seq(P, E, C, B, PS, wq_s, wkv_s, wo_s, B["xsb"],
                      io=(x4[sq], y4[sq]))
    E["lds"].wait(nc.sync)
    return nc


def emit_moe_seq(P, E, C, B, PS, W, xsb, io=None):
    import os as _os
    _BAR = _os.environ.get("K_MOE_BARRIER", "")
    def bar(tag):
        if tag in _BAR or _BAR == "all":
            P.nc.all_engine_barrier()
    """L1 MoE FFN, one sequence: x += 0.5 * moe(LN(x)).

    W: scratch APs wg [1,128,17,16], wup [12,128,17,512], wsw [12,128,9,512],
    wdn [24,128,8,512], wsu [4,128,17,512], wsa [2,128,9,512],
    wsd [4,128,17,512]; bias rows are the last k-chunk (row 0; selected by
    the constant e0mat stationary). bdn handled by the W6 x bdn matmul.
    """
    nc = P.nc
    pe, dve, act, lds = E["pe"], E["dve"], E["act"], E["lds"]
    gd = E["gd"]
    p4, pt = PS["p4"], PS["pt"]
    pt_f32 = PS.get("pav2")
    ident, e0 = C["ident"], C["e0mat"]
    lnt = B["lnt"]
    bdn = B["bdn_sb"]

    if io is not None:
        x_in, _ = io
        for t in range(NT):
            dve.wait(nc.sync)
            lds.inc(nc.sync.dma_start(out=B["xstage"][:, 0:D], in_=x_in[t]), 16)
            lds.wait(nc.vector)
            dve.inc(nc.vector.tensor_copy(xsb[:, t, :], B["xstage"][:, 0:D]))

    pe.wait(nc.sync)
    dve.wait(nc.sync)
    lds.inc(nc.sync.dma_start(out=B["wgb"][:], in_=W["wg"][0]), 16)
    lds.wait(nc.sync)

    def tp8(src, dst, par):
        dve.wait(nc.tensor)
        for k in range(8):
            ins = nc.tensor.transpose(pt[:, par, k, :],
                                      src[:, k * 128:(k + 1) * 128], ident[:])
        pe.inc(ins)
        pe.wait(nc.vector)
        dve.inc(nc.vector.tensor_copy(dst, pt[:, par, 0:8, :]))

    for g in range(NG):
        for tt in range(GT):
            emit_ln_tile(P, E, xsb[:, g * GT + tt, :], B["xn"], lnt,
                         xn_war=(pe,))
            dve.wait(nc.vector)
            dve.inc(nc.vector.tensor_copy(B["resb"][:, tt, :], B["xn"][:]))
        n_iters = 1 if _os.environ.get("K_MOE_ONEIT") else 2
        for it in range(n_iters):
            P.maybe_chunk()
            bar("g")
            if _os.environ.get("K_MOE_ONEIT"):
                it = 1
                xt = B["resb"]
            else:
                xt = B["resb"] if it == 0 else B["xt1"]
            for tt in range(GT):
                for half in range(2):
                    tp8(xt[:, tt, half * 1024:(half + 1) * 1024],
                        B["xtT"][:, tt, half * 8:half * 8 + 8, :], tt % 2)
            bar("a")
            if "dbg_xtT2" in B and g == 0 and it == 1:
                dve.wait(nc.sync)
                lds.inc(nc.sync.dma_start(out=B["dbg_xtT2"],
                                          in_=B["xtT"][:, 0, :, :]), 16)
                lds.wait(nc.vector)
                lds.wait(nc.tensor)
            if "dbg_xtT" in B and g == 0 and it == 0:
                dve.wait(nc.sync)
                lds.inc(nc.sync.dma_start(out=B["dbg_resb"],
                                          in_=B["resb"][:, 0, :]), 16)
                lds.inc(nc.sync.dma_start(out=B["dbg_xtT"],
                                          in_=B["xtT"][:, 0, :, :]), 16)
                lds.inc(nc.sync.dma_start(out=B["dbg_wgb"],
                                          in_=B["wgb"][:, :, :]), 16)
                lds.wait(nc.vector)
                lds.wait(nc.tensor)
            for tt in range(GT):
                act.wait(nc.tensor)
                dve.wait(nc.tensor)
                for k in range(KD):
                    ins = nc.tensor.matmul(
                        p4[:, 0, 0:8], B["xtT"][:, tt, k, :],
                        B["wgb"][:, k, it * 8:(it + 1) * 8],
                        start=(k == 0), stop=False)
                ins = nc.tensor.matmul(p4[:, 0, 0:8], e0[:],
                                       B["wgb"][:, 16, it * 8:(it + 1) * 8],
                                       start=False, stop=True)
                pe.inc(ins)
                pe.wait(nc.vector)
                dve.inc(nc.vector.tensor_copy(B["gcol"][:], p4[:, 0, 0:8]))
                dve.wait(nc.vector)
                for i in range(8):
                    nc.vector.tensor_scalar(B["cm"][:, i, :], B["gcol"][:],
                                            B["gcol"][:, i:i + 1], None, ALU.is_le)
                    ins = nc.vector.tensor_scalar(B["eqm"][:, i, :], B["gcol"][:],
                                                  B["gcol"][:, i:i + 1], None,
                                                  ALU.is_equal)
                dve.inc(ins)
                dve.wait(nc.vector)
                nc.vector.tensor_mul(B["eqm"][:, :, :], B["eqm"][:, :, :],
                                     C["lmask8"][:, :, :])
                dve.inc(nc.vector.reduce_sum(B["rs8"][:, 0:8], B["cm"][:, :, :],
                                             axis=AX.X))
                dve.wait(nc.vector)
                dve.inc(nc.vector.reduce_sum(B["rs8"][:, 8:16],
                                             B["eqm"][:, :, :], axis=AX.X))
                dve.wait(nc.vector)
                dve.inc(nc.vector.tensor_sub(B["rs8"][:, 0:8], B["rs8"][:, 0:8],
                                             B["rs8"][:, 8:16]))
                dve.wait(nc.vector)
                dve.inc(nc.vector.tensor_scalar(B["kept"][:], B["rs8"][:, 0:8],
                                                2.5, None, ALU.is_ge))
                dve.wait(nc.vector)
                dve.inc(nc.vector.tensor_scalar(B["rs8"][:, 8:16], B["kept"][:],
                                                1e30, 1e30, ALU.mult,
                                                ALU.subtract))
                dve.inc(nc.vector.tensor_mul(B["gm8"][:], B["gcol"][:],
                                             B["kept"][:]))
                dve.wait(nc.vector)
                dve.inc(nc.vector.tensor_add(B["gm8"][:], B["gm8"][:],
                                             B["rs8"][:, 8:16]))
                dve.wait(nc.vector)
                dve.inc(nc.vector.reduce_max(B["mc"][:, 0:1], B["gm8"][:],
                                             axis=AX.X))
                dve.wait(nc.vector)
                dve.inc(nc.vector.tensor_scalar_mul(B["mc"][:, 1:2],
                                                    B["mc"][:, 0:1], -1.0))
                dve.wait(nc.scalar)
                act.inc(nc.scalar.activation(B["e8"][:], B["gm8"][:], AF.Exp,
                                             bias=B["mc"][:, 1:2]))
                act.wait(nc.vector)
                dve.inc(nc.vector.reduce_sum(B["mc"][:, 2:3], B["e8"][:],
                                             axis=AX.X))
                dve.wait(nc.vector)
                dve.inc(nc.vector.reciprocal(B["mc"][:, 3:4], B["mc"][:, 2:3]))
                dve.wait(nc.vector)
                dve.inc(nc.vector.tensor_scalar(B["wcol"][:, tt, :], B["e8"][:],
                                                B["mc"][:, 3:4], None, ALU.mult))
                if "dbg_wcol" in B and g == 0 and it == B.get("dbg_it", 0):
                    dve.wait(nc.sync)
                    lds.inc(nc.sync.dma_start(out=B["dbg_wcol"][tt],
                                              in_=B["wcol"][:, tt, :]), 16)
                    lds.wait(nc.vector)
                    lds.inc(nc.sync.dma_start(out=B["dbg_gcol"][tt],
                                              in_=B["gcol"][:, :]), 16)
                    lds.wait(nc.vector)
                    if tt == 0:
                        lds.inc(nc.sync.dma_start(out=B["dbg_kept"][0],
                                                  in_=B["kept"][:, :]), 16)
                        lds.inc(nc.sync.dma_start(out=B["dbg_kept"][1],
                                                  in_=B["rs8"][:, 0:8]), 16)
                        lds.inc(nc.sync.dma_start(out=B["dbg_kept"][2],
                                                  in_=C["lmask8"][:, 0, :]), 16)
                        lds.inc(nc.sync.dma_start(out=B["dbg_kept"][3],
                                                  in_=C["lmask8"][:, 5, :]), 16)
                        lds.inc(nc.sync.dma_start(out=B["dbg_kept"][4],
                                                  in_=B["gm8"][:, :]), 16)
                        lds.inc(nc.sync.dma_start(out=B["dbg_kept"][5],
                                                  in_=B["e8"][:, :]), 16)
                        lds.wait(nc.vector)
                dve.wait(nc.vector)
                dve.inc(nc.vector.tensor_copy(B["w8b"][:], B["wcol"][:, tt, :]))
                dve.wait(nc.tensor)
                ins = nc.tensor.transpose(pt[0:8, 0, 0, :], B["w8b"][:], ident[:])
                pe.inc(ins)
                pe.wait(nc.vector)
                dve.inc(nc.vector.tensor_copy(B["W6"][0:8, tt, :],
                                              pt[0:8, 0, 0, :]))
            bar("b")
            for oc in range(12):
                P.maybe_chunk()
                n, sub = oc // 2, oc % 2
                pe.wait(nc.sync)
                pe.wait(nc.gpsimd)
                lds.inc(nc.sync.dma_start(out=B["wub"][:, 0:9, :],
                                          in_=W["wup"][oc][:, 0:9, :]), 16)
                gd.inc(nc.gpsimd.dma_start(out=B["wub"][:, 9:17, :],
                                           in_=W["wup"][oc][:, 9:17, :]), 16)
                lds.wait(nc.tensor)
                gd.wait(nc.tensor)
                for tt in range(GT):
                    pban = p4[:, tt % 2, :]
                    act.wait(nc.tensor)
                    dve.wait(nc.tensor)
                    for k in range(KD):
                        ins = nc.tensor.matmul(pban, B["xtT"][:, tt, k, :],
                                               B["wub"][:, k, :],
                                               start=(k == 0), stop=False)
                    ins = nc.tensor.matmul(pban, e0[:], B["wub"][:, 16, :],
                                           start=False, stop=True)
                    pe.inc(ins)
                    pe.wait(nc.vector)
                    dve.inc(nc.vector.tensor_copy(B["hall"][:, n, tt, sub, :],
                                                  pban))
                    if "dbg_h" in B and g == 0 and it == B.get("dbg_it", 0) and oc < 2 and tt == 0:
                        dve.wait(nc.sync)
                        lds.inc(nc.sync.dma_start(
                            out=B["dbg_h"][sub], in_=B["hall"][:, 0, 0, sub, :]), 16)
                        lds.wait(nc.vector)
            bar("c")
            for n in range(6):
                P.maybe_chunk()
                pe.wait(nc.sync)
                pe.wait(nc.gpsimd)
                lds.inc(nc.sync.dma_start(out=B["wub"][:, 0:9, :],
                                          in_=W["wsw"][2 * n]), 16)
                gd.inc(nc.gpsimd.dma_start(out=B["wub"][:, 9:17, :],
                                           in_=W["wsw"][2 * n + 1][:, 0:8, :]),
                       16)
                lds.inc(nc.sync.dma_start(out=B["wgb2"][:, 0:1, :],
                                          in_=W["wsw"][2 * n + 1][:, 8:9, :]), 16)
                lds.wait(nc.tensor)
                gd.wait(nc.tensor)
                for tt in range(GT):
                    tp8(B["hall"][:, n, tt, :, :].rearrange("p a b -> p (a b)"),
                        B["hT"][:, 0:8, :], tt % 2)
                    for so in range(2):
                        pban = p4[:, 2 + so, :]
                        act.wait(nc.tensor)
                        dve.wait(nc.tensor)
                        koff = 0 if so == 0 else 9
                        for k in range(8):
                            ins = nc.tensor.matmul(pban, B["hT"][:, k, :],
                                                   B["wub"][:, koff + k, :],
                                                   start=(k == 0), stop=False)
                        bias_src = B["wub"][:, 8, :] if so == 0 else B["wgb2"][:, 0, :]
                        ins = nc.tensor.matmul(pban, e0[:], bias_src,
                                               start=False, stop=True)
                        pe.inc(ins)
                        pe.wait(nc.scalar)
                        dve.wait(nc.scalar)
                        act.inc(nc.scalar.activation(B["sgb"][:, 0:512], pban,
                                                     AF.Silu))
                        act.wait(nc.vector)
                        pe.wait(nc.vector)
                        dve.inc(nc.vector.scalar_tensor_tensor(
                            B["gstage"][:, so * 512:(so + 1) * 512],
                            B["sgb"][:, 0:512], B["wcol"][:, tt, n:n + 1],
                            B["hall"][:, n, tt, so, :], ALU.mult, ALU.mult))
                    if "dbg_gst" in B and g == 0 and it == B.get("dbg_it", 0) and n == 0 and tt == 0:
                        dve.wait(nc.sync)
                        lds.inc(nc.sync.dma_start(out=B["dbg_gst"],
                                                  in_=B["gstage"][:, :]), 16)
                        lds.wait(nc.vector)
                    tp8(B["gstage"][:, :],
                        B["gT"][:, n, tt, :, :], tt % 2)
                    if ("dbg_gT" in B and g == 0 and n == 0 and tt == 0
                            and it == B.get("dbg_it", 0)):
                        dve.wait(nc.vector)
                        dve.inc(nc.vector.tensor_copy(B["dbg_pt_sb"],
                                                      pt[:, 0, 0:8, :]))
                        dve.wait(nc.sync)
                        lds.inc(nc.sync.dma_start(out=B["dbg_gT"],
                                                  in_=B["gT"][:, 0, 0, :, :]), 16)
                        lds.inc(nc.sync.dma_start(out=B["dbg_pt"],
                                                  in_=B["dbg_pt_sb"]), 16)
                        lds.wait(nc.vector)
                        lds.wait(nc.tensor)
            bar("d")
            for oc in range(4):
                P.maybe_chunk()
                pe.wait(nc.sync)
                pe.wait(nc.gpsimd)
                lds.inc(nc.sync.dma_start(out=B["wub"][:, 0:9, :],
                                          in_=W["wsu"][oc][:, 0:9, :]), 16)
                gd.inc(nc.gpsimd.dma_start(out=B["wub"][:, 9:17, :],
                                           in_=W["wsu"][oc][:, 9:17, :]), 16)
                lds.wait(nc.tensor)
                gd.wait(nc.tensor)
                for tt in range(GT):
                    pban = p4[:, tt % 2, :]
                    act.wait(nc.tensor)
                    dve.wait(nc.tensor)
                    for k in range(KD):
                        ins = nc.tensor.matmul(pban, B["xtT"][:, tt, k, :],
                                               B["wub"][:, k, :],
                                               start=(k == 0), stop=False)
                    ins = nc.tensor.matmul(pban, e0[:], B["wub"][:, 16, :],
                                           start=False, stop=True)
                    pe.inc(ins)
                    pe.wait(nc.vector)
                    dve.inc(nc.vector.tensor_copy(B["shb"][:, tt, oc, :], pban))
            bar("e")
            pe.wait(nc.sync)
            pe.wait(nc.gpsimd)
            lds.inc(nc.sync.dma_start(out=B["wub"][:, 0:9, :], in_=W["wsa"][0]), 16)
            gd.inc(nc.gpsimd.dma_start(out=B["wub"][:, 9:17, :],
                                       in_=W["wsa"][1][:, 0:8, :]), 16)
            lds.inc(nc.sync.dma_start(out=B["wgb2"][:, 0:1, :],
                                      in_=W["wsa"][1][:, 8:9, :]), 16)
            lds.wait(nc.tensor)
            gd.wait(nc.tensor)
            for tt in range(GT):
                for half in range(2):
                    tp8(B["shb"][:, tt, :, :].rearrange("p a b -> p (a b)")
                        [:, half * 1024:(half + 1) * 1024],
                        B["xtT"][:, tt, half * 8:half * 8 + 8, :], tt % 2)
                for hf in range(2):
                    for so in range(2):
                        pban = p4[:, 2 + so, :]
                        act.wait(nc.tensor)
                        dve.wait(nc.tensor)
                        koff = 0 if so == 0 else 9
                        for k in range(8):
                            ins = nc.tensor.matmul(
                                pban, B["xtT"][:, tt, hf * 8 + k, :],
                                B["wub"][:, koff + k, :],
                                start=(k == 0), stop=False)
                        bias_src = (B["wub"][:, 8, :] if so == 0
                                    else B["wgb2"][:, 0, :])
                        ins = nc.tensor.matmul(pban, e0[:], bias_src,
                                               start=False, stop=True)
                        pe.inc(ins)
                        pe.wait(nc.scalar)
                        dve.wait(nc.scalar)
                        act.inc(nc.scalar.activation(B["sgb"][:, 0:512], pban,
                                                     AF.Silu))
                        act.wait(nc.vector)
                        pe.wait(nc.vector)
                        dve.inc(nc.vector.scalar_tensor_tensor(
                            B["gstage"][:, so * 512:(so + 1) * 512],
                            B["sgb"][:, 0:512], 1.0,
                            B["shb"][:, tt, 2 * hf + so, :], ALU.mult, ALU.mult))
                    if "dbg_sh2" in B and g == 0 and it == B.get("dbg_it", 0) \
                            and tt == 0 and hf == 0:
                        dve.wait(nc.sync)
                        lds.inc(nc.sync.dma_start(out=B["dbg_sh2"],
                                                  in_=B["gstage"][:, :]), 16)
                        lds.wait(nc.vector)
                        lds.wait(nc.tensor)
                    tp8(B["gstage"][:, :],
                        B["xtT"][:, tt, hf * 8:hf * 8 + 8, :], tt % 2)
            bar("f")
            if "dbg_sh2T" in B and g == 0 and it == B.get("dbg_it", 0):
                dve.wait(nc.sync)
                lds.inc(nc.sync.dma_start(out=B["dbg_sh2T"],
                                          in_=B["xtT"][:, 0, :, :]), 16)
                lds.wait(nc.vector)
                lds.wait(nc.tensor)
            if it == 1:
                if "dbg_fin" in B and g == 0:
                    dve.wait(nc.sync)
                    lds.inc(nc.sync.dma_start(out=B["dbg_fin"][0],
                                              in_=B["resb"][:, 0, :]), 16)
                    lds.wait(nc.vector)
                for tt in range(GT):
                    dve.wait(nc.vector)
                    xap = xsb[:, g * GT + tt, :]
                    dve.inc(nc.vector.scalar_tensor_tensor(
                        xap, B["resb"][:, tt, :], 0.5, xap, ALU.mult, ALU.add))
                if "dbg_fin" in B and g == 0:
                    dve.wait(nc.sync)
                    lds.inc(nc.sync.dma_start(out=B["dbg_fin"][1],
                                              in_=xsb[:, 0, :]), 16)
                    lds.wait(nc.vector)
            for oc in range(4):
                P.maybe_chunk()
                pe.wait(nc.sync)
                pe.wait(nc.gpsimd)
                lds.inc(nc.sync.dma_start(out=B["wub"][:, 0:9, :],
                                          in_=W["wsd"][oc][:, 0:9, :]), 16)
                gd.inc(nc.gpsimd.dma_start(out=B["wub"][:, 9:17, :],
                                           in_=W["wsd"][oc][:, 9:17, :]), 16)
                act.wait(nc.tensor)
                dve.wait(nc.tensor)
                for n in range(6):
                    pe.wait(nc.sync)
                    pe.wait(nc.gpsimd)
                    lds.inc(nc.sync.dma_start(out=B["wdnb"][:, 0:4, :],
                                              in_=W["wdn"][n * 4 + oc]
                                              [:, 0:4, :]), 16)
                    gd.inc(nc.gpsimd.dma_start(out=B["wdnb"][:, 4:8, :],
                                               in_=W["wdn"][n * 4 + oc]
                                               [:, 4:8, :]), 16)
                    lds.wait(nc.tensor)
                    gd.wait(nc.tensor)
                    for tt in range(GT):
                        for k in range(8):
                            ins = nc.tensor.matmul(
                                p4[:, tt, :], B["gT"][:, n, tt, k, :],
                                B["wdnb"][:, k, :],
                                start=(n == 0 and k == 0), stop=False,
                                skip_group_check=True)
                    pe.inc(ins)
                import os as _os
                for tt in range(GT):
                    pban = p4[:, tt, :]
                    if not _os.environ.get("K_MOE_NOBDN"):
                        ins = nc.tensor.matmul(
                            pban, B["W6"][0:8, tt, :],
                            bdn[:, oc * 512:(oc + 1) * 512],
                            start=False, stop=False, skip_group_check=True)
                    if not _os.environ.get("K_MOE_NOSH"):
                        _kr = (range(0, 8) if _os.environ.get("K_MOE_SH_H0")
                               else range(8, 16) if _os.environ.get("K_MOE_SH_H1")
                               else range(KD))
                        for k in _kr:
                            ins = nc.tensor.matmul(pban, B["xtT"][:, tt, k, :],
                                                   B["wub"][:, k, :],
                                                   start=False, stop=False,
                                                   skip_group_check=True)
                    ins = nc.tensor.matmul(pban, e0[:], B["wub"][:, 16, :],
                                           start=False, stop=True,
                                           skip_group_check=True)
                    pe.inc(ins)
                    pe.wait(nc.vector)
                    if ("dbg_psum" in B and g == 0 and tt == 0 and oc == 0
                            and it == B.get("dbg_it", 0)):
                        dve.inc(nc.vector.tensor_copy(B["dbg_ps_sb"], pban))
                        dve.wait(nc.sync)
                        lds.inc(nc.sync.dma_start(out=B["dbg_psum"],
                                                  in_=B["dbg_ps_sb"]), 16)
                        lds.wait(nc.vector)
                    if it == 0:
                        dve.inc(nc.vector.scalar_tensor_tensor(
                            B["xt1"][:, tt, oc * 512:(oc + 1) * 512], pban, 1.0,
                            B["resb"][:, tt, oc * 512:(oc + 1) * 512],
                            ALU.mult, ALU.add))
                        if "dbg_xt1" in B and g == 0 and tt == 0:
                            dve.wait(nc.sync)
                            lds.inc(nc.sync.dma_start(
                                out=B["dbg_xt1"][oc],
                                in_=B["xt1"][:, 0, oc * 512:(oc + 1) * 512]), 16)
                            lds.wait(nc.vector)
                    else:
                        xap = xsb[:, g * GT + tt, oc * 512:(oc + 1) * 512]
                        dve.inc(nc.vector.scalar_tensor_tensor(
                            xap, pban, 0.5, xap, ALU.mult, ALU.add))

    if io is not None:
        for t in range(NT):
            lds.wait(nc.vector)
            dve.inc(nc.vector.tensor_copy(B["xstage"][:, 0:D], xsb[:, t, :]))
            dve.wait(nc.sync)
            lds.inc(nc.sync.dma_start(out=io[1][t], in_=B["xstage"][:, 0:D]), 16)
        lds.wait(nc.sync)


def alloc_moe_buffers(nc, B, al=None):
    if al is None:
        al = lambda n, sh, dt: nc.alloc_sbuf_tensor(n, sh, dt).ap()
    B["resb"] = al("resb", [128, GT, D], BF16)
    B["xt1"] = al("xt1", [128, GT, D], BF16)
    B["xtT"] = al("xtT", [128, GT, 16, 128], BF16)
    B["hall"] = al("hall", [128, 6, GT, 2, 512], BF16)
    B["wub"] = al("wub", [128, 17, 512], BF16)
    B["wdnb"] = al("wdnb", [128, 8, 512], BF16)
    B["wgb"] = al("wgb", [128, 17, 16], BF16)
    B["wgb2"] = al("wgb2", [128, 1, 512], BF16)
    B["bdn_sb"] = al("bdn_sb", [8, D], BF16)
    B["W6"] = al("W6", [8, GT, 128], BF16)
    B["gcol"] = al("gcol", [128, 8], F32)
    B["cm"] = al("cmx", [128, 8, 8], F32)
    B["eqm"] = al("eqm", [128, 8, 8], F32)
    B["rs8"] = al("rs8", [128, 16], F32)
    B["kept"] = al("kept", [128, 8], F32)
    B["gm8"] = al("gm8", [128, 8], F32)
    B["e8"] = al("e8", [128, 8], F32)
    B["w8b"] = al("w8b", [128, 8], BF16)
    B["wcol"] = al("wcolb", [128, GT, 8], F32)
    B["mc"] = al("mcb", [128, 8], F32)
    xst = nc.lookup_mloc(B["xstage"].tensor).addr
    B["gstage"] = nc.alloc_sbuf_tensor_at("gstage", [128, 1024], BF16,
                                          offset=xst).ap()
    B["hT"] = nc.alloc_sbuf_tensor_at("hTb", [128, 8, 128], BF16,
                                      offset=xst + 2048).ap()
    B["sgb"] = nc.alloc_sbuf_tensor_at("sgb", [128, 512], F32,
                                       offset=xst + 4096).ap()
    B["shb"] = nc.alloc_sbuf_tensor_at(
        "shb", [128, GT, 4, 512], BF16,
        offset=nc.lookup_mloc(B["xt1"].tensor).addr).ap()
    B["gT"] = nc.alloc_sbuf_tensor_at(
        "gTb", [128, 6, GT, 8, 128], BF16,
        offset=nc.lookup_mloc(B["hall"].tensor).addr).ap()
    return B


def make_moe_scratch(nc):
    W = {}
    W["wg"] = make_wscratch(nc, "wg_s", 17 * 128, 16, oc=16)
    W["wup"] = make_wscratch(nc, "wup_s", 17 * 128, 12 * 512)
    W["wsw"] = make_wscratch(nc, "wsw_s", 9 * 128, 12 * 512)
    W["wdn"] = make_wscratch(nc, "wdn_s", 8 * 128, 24 * 512)
    W["wsu"] = make_wscratch(nc, "wsu_s", 17 * 128, 4 * 512)
    W["wsa"] = make_wscratch(nc, "wsa_s", 9 * 128, 2 * 512)
    W["wsd"] = make_wscratch(nc, "wsd_s", 17 * 128, 4 * 512)
    return W


def moe_host_weights(Wg, bg, Wup, bup, Wsw, bsw, Wdn, bdn, Wsu, bsu,
                     Wsa, bsa, Wsd, bsd):
    z = np.zeros
    def with_bias(w2d, brow):
        pad = z((128, w2d.shape[1]), np.float32)
        pad[0] = brow
        return np.ascontiguousarray(np.concatenate([w2d, pad], 0))
    out = {}
    out["wg_f"] = with_bias(np.concatenate([Wg[0], Wg[1]], 1),
                            np.concatenate([bg[0], bg[1]], 0))
    out["wup_f"] = with_bias(np.transpose(Wup, (1, 0, 2)).reshape(2048, 6144),
                             np.asarray(bup).reshape(-1))
    out["wsw_f"] = with_bias(np.transpose(Wsw, (1, 0, 2)).reshape(1024, 6144),
                             np.asarray(bsw).reshape(-1))
    out["wdn_f"] = np.ascontiguousarray(
        np.transpose(Wdn, (1, 0, 2)).reshape(1024, 12288))
    out["wsu_f"] = with_bias(np.asarray(Wsu), np.asarray(bsu))
    out["wsa_f"] = with_bias(np.asarray(Wsa), np.asarray(bsa))
    out["wsd_f"] = with_bias(np.asarray(Wsd), np.asarray(bsd))
    bdn8 = z((8, 2048), np.float32)
    bdn8[0:6] = bdn
    out["bdn_f"] = bdn8
    return out


def build_moe_standalone(nseq=1):
    nc = bass.Bass("TRN2", target_bir_lowering=False)
    P = Prog(nc)
    x = nc.dram_tensor("x", [nseq, L, D], F32, kind="ExternalInput").ap()
    names = ["wg_f", "wup_f", "wsw_f", "wdn_f", "wsu_f", "wsa_f", "wsd_f"]
    shapes = [(17 * 128, 16), (17 * 128, 6144), (9 * 128, 6144),
              (8 * 128, 12288), (17 * 128, 2048), (9 * 128, 1024),
              (17 * 128, 2048)]
    fins = {nm: nc.dram_tensor(nm, list(sh), F32, kind="ExternalInput").ap()
            for nm, sh in zip(names, shapes)}
    bdn_f = nc.dram_tensor("bdn_f", [8, D], F32, kind="ExternalInput").ap()
    y = nc.dram_tensor("y", [nseq, L, D], F32, kind="ExternalOutput").ap()
    W = make_moe_scratch(nc)

    C = {}
    C["ident"] = nc.alloc_sbuf_tensor("ident", [128, 128], BF16).ap()
    C["ones1"] = nc.alloc_sbuf_tensor("ones1", [1, 128], BF16).ap()
    C["e0mat"] = nc.alloc_sbuf_tensor("e0mat", [128, 128], BF16).ap()
    C["lmask8"] = nc.alloc_sbuf_tensor("lmask8", [128, 8, 8], F32).ap()
    B = {}
    B["xsb"] = nc.alloc_sbuf_tensor("xsb", [128, NT, D], BF16).ap()
    B["xn"] = nc.alloc_sbuf_tensor("xn", [128, D], BF16).ap()
    B["xstage"] = nc.alloc_sbuf_tensor("xstage", [128, D], F32).ap()
    lnt = {}
    for nm in ["sum", "mean", "m2", "sq", "var", "sd", "rstd"]:
        lnt[nm] = nc.alloc_sbuf_tensor("ln_" + nm, [128, 1], F32).ap()
    lnt["junk"] = B["xstage"]
    B["lnt"] = lnt
    alloc_moe_buffers(nc, B)
    pcb = nc.alloc_sbuf_tensor_at(
        "pcbuf", [128, 2, 1024], BF16,
        offset=nc.lookup_mloc(B["xstage"].tensor).addr + 0).ap()

    PS = {}
    PS["p4"] = nc.alloc_psum_tensor("p4", [128, 4, 512], F32).ap()
    PS["pt"] = nc.alloc_psum_tensor("pt", [128, 2, 8, 128], BF16).ap()
    PS["pav2"] = nc.alloc_psum_tensor("pav2", [128, 2, 512], F32).ap()

    E = {k: P.sem(k) for k in ["pe", "dve", "act", "lds", "pl_ld", "pl_st",
                               "gps", "gd"]}
    reg_consts(nc, [1e-5])
    emit_constants(P, C)
    E["gps"].inc(nc.gpsimd.memset(C["ones1"][:], 1.0))
    for eng in [nc.tensor, nc.vector, nc.scalar, nc.sync]:
        E["gps"].wait(eng)
    E["pl_ld"].inc(nc.gpsimd.dma_start(out=B["bdn_sb"][:], in_=bdn_f[:, :]), 16)
    E["pl_ld"].wait(nc.tensor)
    emit_precast(P, E, [(fins["wg_f"], W["wg"][0] if False else W["wg"]),
                        (fins["wup_f"], W["wup"]), (fins["wsw_f"], W["wsw"]),
                        (fins["wdn_f"], W["wdn"]), (fins["wsu_f"], W["wsu"]),
                        (fins["wsa_f"], W["wsa"]), (fins["wsd_f"], W["wsd"])],
                 pcb)

    import os
    if os.environ.get("K_DEBUG"):
        B["dbg_wcol"] = nc.dram_tensor("dbg_wcol", [GT, 128, 8], F32,
                                       kind="ExternalOutput").ap()
        B["dbg_gcol"] = nc.dram_tensor("dbg_gcol", [GT, 128, 8], F32,
                                       kind="ExternalOutput").ap()
        B["dbg_h"] = nc.dram_tensor("dbg_h", [2, 128, 512], BF16,
                                    kind="ExternalOutput").ap()
        B["dbg_gst"] = nc.dram_tensor("dbg_gst", [128, 1024], BF16,
                                      kind="ExternalOutput").ap()[:, :]
        B["dbg_xt1"] = nc.dram_tensor("dbg_xt1", [4, 128, 512], BF16,
                                      kind="ExternalOutput").ap()
        B["dbg_xtT"] = nc.dram_tensor("dbg_xtT", [128, 16, 128], BF16,
                                      kind="ExternalOutput").ap()[:, :, :]
        B["dbg_kept"] = nc.dram_tensor("dbg_kept", [6, 128, 8], F32,
                                       kind="ExternalOutput").ap()
        B["dbg_resb"] = nc.dram_tensor("dbg_resb", [128, D], BF16,
                                       kind="ExternalOutput").ap()[:, :]
        B["dbg_gT"] = nc.dram_tensor("dbg_gT", [128, 8, 128], BF16,
                                     kind="ExternalOutput").ap()[:, :, :]
        B["dbg_xtT2"] = nc.dram_tensor("dbg_xtT2", [128, 16, 128], BF16,
                                       kind="ExternalOutput").ap()[:, :, :]
        B["dbg_it"] = int(os.environ.get("K_DBG_IT", "0"))
        B["dbg_fin"] = nc.dram_tensor("dbg_fin", [2, 128, D], BF16,
                                      kind="ExternalOutput").ap()
        B["dbg_shb"] = nc.dram_tensor("dbg_shb", [128, 4, 512], BF16,
                                      kind="ExternalOutput").ap()[:, :, :]
        B["dbg_sh2"] = nc.dram_tensor("dbg_sh2", [128, 1024], BF16,
                                      kind="ExternalOutput").ap()[:, :]
        B["dbg_sh2T"] = nc.dram_tensor("dbg_sh2T", [128, 16, 128], BF16,
                                       kind="ExternalOutput").ap()[:, :, :]
        B["dbg_psum"] = nc.dram_tensor("dbg_psum", [128, 512], F32,
                                       kind="ExternalOutput").ap()[:, :]
        B["dbg_pt"] = nc.dram_tensor("dbg_pt", [128, 8, 128], BF16,
                                     kind="ExternalOutput").ap()[:, :, :]
        B["dbg_pt_sb"] = nc.alloc_sbuf_tensor_at(
            "dbg_pt_sb", [128, 8, 128], BF16,
            offset=nc.lookup_mloc(B["xstage"].tensor).addr + 6144).ap()[:, :, :]
        B["dbg_ps_sb"] = nc.alloc_sbuf_tensor_at(
            "dbg_ps_sb", [128, 512], F32,
            offset=nc.lookup_mloc(B["xstage"].tensor).addr + 6144).ap()[:, :]
        B["dbg_wgb"] = nc.dram_tensor("dbg_wgb", [128, 17, 16], BF16,
                                      kind="ExternalOutput").ap()[:, :, :]
    x4 = x.rearrange("s (t p) d -> s t p d", p=128)
    y4 = y.rearrange("s (t p) d -> s t p d", p=128)
    for sq in range(nseq):
        emit_moe_seq(P, E, C, B, PS, W, B["xsb"], io=(x4[sq], y4[sq]))
    E["lds"].wait(nc.sync)
    return nc


MB_NG = 8


def emit_mamba_setup(P, E, C, B, PS, conv_w, a_log, d_in, norm_w):
    nc = P.nc
    pe, dve, act, lds = E["pe"], E["dve"], E["act"], E["lds"]
    gd = E["gd"]
    p4 = PS["p4"]
    lds.inc(nc.sync.dma_start(
        out=B["cw"][:, :, :],
        in_=conv_w.rearrange("(c p) k -> p c k", p=128)), 16)
    lds.inc(nc.sync.dma_start(
        out=B["nw"][:, :], in_=norm_w.rearrange("(k p) -> p k", p=128),
        allow_slow_non_contiguous=True), 16)
    lds.inc(nc.sync.dma_start(
        out=B["ad"][0:1, 0:64], in_=a_log.rearrange("(a h) -> a h", a=1)), 16)
    lds.inc(nc.sync.dma_start(
        out=B["ad"][0:1, 64:128], in_=d_in.rearrange("(a h) -> a h", a=1)), 16)
    lds.wait(nc.tensor)
    lds.wait(nc.vector)
    lds.wait(nc.scalar)
    ins = nc.tensor.matmul(p4[:, 0, 0:128], C["onesf"][0:1, 0:128],
                           B["ad"][0:1, 0:128], start=True, stop=True)
    pe.inc(ins)
    pe.wait(nc.scalar)
    act.inc(nc.scalar.activation(B["abc"][:, :], p4[:, 0, 0:64], AF.Exp))
    pe.wait(nc.vector)
    dve.inc(nc.vector.tensor_copy(B["db"][:, :], p4[:, 0, 64:128]))
    act.wait(nc.vector)
    dve.inc(nc.vector.tensor_scalar_mul(B["abc"][:, :], B["abc"][:, :], -1.0))


def emit_mamba_seq(P, E, C, B, PS, win_s, wout_s, xsb, io=None):
    nc = P.nc
    pe, dve, act, lds = E["pe"], E["dve"], E["act"], E["lds"]
    gd = E["gd"]
    p4, pt = PS["p4"], PS["pt"]
    ident = C["ident"]
    e0 = C["e0mat"]
    lnt = B["lnt"]
    HID = 4096

    if io is not None:
        x_in, _ = io
        for t in range(NT):
            dve.wait(nc.sync)
            lds.inc(nc.sync.dma_start(out=B["xstage"][:, 0:D], in_=x_in[t]), 16)
            lds.wait(nc.vector)
            dve.inc(nc.vector.tensor_copy(xsb[:, t, :], B["xstage"][:, 0:D]))

    pe.wait(nc.vector)
    nc.vector.memset(B["S"][:, :, :], 0.0)
    dve.inc(nc.vector.memset(B["xBCT"][:, :, 0:3], 0.0))

    for g in range(MB_NG):
        P.maybe_chunk()
        for tt in range(2):
            emit_ln_tile(P, E, xsb[:, g * 2 + tt, :], B["xn"], lnt,
                         xn_war=(pe,))
            for half in range(2):
                dve.wait(nc.tensor)
                for k in range(8):
                    kk = half * 8 + k
                    ins = nc.tensor.transpose(
                        pt[:, tt % 2, k, :],
                        B["xn"][:, kk * 128:(kk + 1) * 128], ident[:])
                pe.inc(ins)
                pe.wait(nc.vector)
                dve.inc(nc.vector.tensor_copy(
                    B["xTg"][:, tt, half * 8:half * 8 + 8, :],
                    pt[:, tt % 2, 0:8, :]))
        for oc in range(17):
            nw = 320 if oc == 16 else 512
            for half in range(2):
                pe.wait(nc.sync)
                pe.wait(nc.gpsimd)
                if half == 0:
                    lds.inc(nc.sync.dma_start(out=B["winb"][:, 0:4, :],
                                              in_=win_s[oc][:, 0:4, :]), 16)
                    gd.inc(nc.gpsimd.dma_start(out=B["winb"][:, 4:8, :],
                                               in_=win_s[oc][:, 4:8, :]), 16)
                else:
                    lds.inc(nc.sync.dma_start(out=B["winb"][:, 0:5, :],
                                              in_=win_s[oc][:, 8:13, :]), 16)
                    gd.inc(nc.gpsimd.dma_start(out=B["winb"][:, 5:9, :],
                                               in_=win_s[oc][:, 13:17, :]), 16)
                lds.wait(nc.tensor)
                gd.wait(nc.tensor)
                if half == 0:
                    dve.wait(nc.tensor)
                    act.wait(nc.tensor)
                for tt in range(2):
                    pban = p4[:, tt % 2, 0:nw]
                    for k in range(8):
                        ins = nc.tensor.matmul(
                            pban, B["xTg"][:, tt, half * 8 + k, :],
                            B["winb"][:, k, 0:nw], start=(half == 0 and k == 0),
                            stop=False, skip_group_check=True)
                    if half == 1:
                        ins = nc.tensor.matmul(
                            pban, e0[:], B["winb"][:, 8, 0:nw],
                            start=False, stop=True, skip_group_check=True)
                pe.inc(ins)
            pe.wait(nc.vector)
            for tt in range(2):
                pban = p4[:, tt % 2, 0:nw]
                if oc < 8:
                    ins = nc.vector.tensor_copy(
                        B["zg"][:, tt, oc * 512:(oc + 1) * 512], pban)
                    dve.inc(ins)
                else:
                    nbc = 256 if oc == 16 else 512
                    dve.inc(nc.vector.tensor_copy(B["bstage"][:, 0:nbc],
                                                  pban[:, 0:nbc]))
                    if oc == 16:
                        dve.inc(nc.vector.tensor_copy(B["dtg"][:, tt, :],
                                                      pban[:, 256:320]))
                    nsub = nbc // 128
                    dve.wait(nc.tensor)
                    for s4 in range(nsub):
                        ins = nc.tensor.transpose(
                            pt[:, tt % 2, s4, :],
                            B["bstage"][:, s4 * 128:(s4 + 1) * 128], ident[:])
                    pe.inc(ins)
                    pe.wait(nc.vector)
                    for s4 in range(nsub):
                        ch = (oc - 8) * 4 + s4
                        ins = nc.vector.tensor_copy(
                            B["xBCT"][:, ch, 3 + tt * 128:3 + (tt + 1) * 128],
                            pt[:, tt % 2, s4, :])
                    dve.inc(ins)
        if g == 0 and "dbgm_zg" in B:
            dbg = E["dbg"]
            dve.wait(nc.sync); act.wait(nc.sync); pe.wait(nc.sync)
            for nm, src in [("dbgm_zg", B["zg"][:, :, :]),
                            ("dbgm_xBCT", B["xBCT"][:, :, :]),
                            ("dbgm_dtg", B["dtg"][:, :, :]),
                            ("dbgm_xTg", B["xTg"][:, :, :, :])]:
                dbg.inc(nc.sync.dma_start(out=B[nm], in_=src), 16)
            for eng in [nc.vector, nc.scalar, nc.tensor]:
                dbg.wait(eng)
        for cc in range(2):
            P.maybe_chunk()
            t0 = cc * 128
            for hc in range(2):
                act.wait(nc.vector)
                for ch17 in range(17):
                    ch = hc * 17 + ch17
                    nc.vector.tensor_scalar(
                        B["xBCf"][:, ch17, :], B["xBCT"][:, ch, t0:t0 + 128],
                        B["cw"][:, ch, 0:1], None, ALU.mult)
                for k in range(1, 4):
                    for ch17 in range(17):
                        ch = hc * 17 + ch17
                        ins = nc.vector.scalar_tensor_tensor(
                            B["xBCf"][:, ch17, :],
                            B["xBCT"][:, ch, t0 + k:t0 + k + 128],
                            B["cw"][:, ch, k:k + 1], B["xBCf"][:, ch17, :],
                            ALU.mult, ALU.add)
                dve.inc(ins)
                dve.wait(nc.scalar)
                if hc == 0:
                    pe.wait(nc.scalar)
                act.inc(nc.scalar.activation(
                    B["xBCc"][:, hc * 17:hc * 17 + 17, :]
                    .rearrange("p a b -> p (a b)"),
                    B["xBCf"][:, :, :].rearrange("p a b -> p (a b)"), AF.Silu))
            act.wait(nc.tensor)
            for half4 in range(4):
                dve.wait(nc.tensor)
                for k in range(8):
                    ch = half4 * 8 + k
                    ins = nc.tensor.transpose(pt[:, half4 % 2, k, :],
                                              B["xBCc"][:, ch, :], ident[:])
                pe.inc(ins)
                pe.wait(nc.vector)
                dve.inc(nc.vector.tensor_copy(
                    B["xs"][:, half4 * 8:half4 * 8 + 8, :],
                    pt[:, half4 % 2, 0:8, :]))
            dve.wait(nc.tensor)
            ins = nc.tensor.matmul(p4[:, 2, 0:128], B["xBCc"][:, 32, :],
                                   B["xBCc"][:, 33, :], start=True, stop=True)
            pe.inc(ins)
            ins = nc.tensor.transpose(pt[:, 0, 0, :], B["xBCc"][:, 32, :],
                                      ident[:])
            pe.inc(ins)
            pe.wait(nc.vector)
            dve.inc(nc.vector.tensor_copy(B["CBT"][:, :], p4[:, 2, 0:128]))
            dve.inc(nc.vector.tensor_copy(B["Btm"][:, :], pt[:, 0, 0, :]))
            dve.wait(nc.scalar)
            act.inc(nc.scalar.activation(B["diff"][:, 0:64],
                                         B["dtg"][:, cc, :], AF.Abs))
            act.wait(nc.scalar)
            act.inc(nc.scalar.activation(B["diff"][:, 64:128],
                                         B["diff"][:, 0:64], AF.Exp,
                                         scale=-1.0))
            act.wait(nc.scalar)
            act.inc(nc.scalar.activation(B["diff"][:, 0:64],
                                         B["diff"][:, 64:128], AF.Ln,
                                         bias=1.0))
            act.wait(nc.vector)
            dve.inc(nc.vector.tensor_scalar_max(B["dt"][:, :],
                                                B["dtg"][:, cc, :], 0.0))
            dve.wait(nc.vector)
            dve.inc(nc.vector.tensor_add(B["dt"][:, :], B["dt"][:, :],
                                         B["diff"][:, 0:64]))
            dve.wait(nc.vector)
            dve.inc(nc.vector.tensor_mul(B["av"][:, :], B["dt"][:, :],
                                         B["abc"][:, :]))
            dve.wait(nc.tensor)
            act.wait(nc.tensor)
            ins = nc.tensor.matmul(p4[:, 2, 0:64], C["triu"][:, :],
                                   B["av"][:, :], start=True, stop=True)
            pe.inc(ins)
            pe.wait(nc.vector)
            dve.inc(nc.vector.tensor_copy(B["cum"][:, :], p4[:, 2, 0:64]))
            dve.wait(nc.vector)
            dve.inc(nc.vector.tensor_scalar_mul(B["ncum"][:, :],
                                                B["cum"][:, :], -1.0))
            dve.wait(nc.scalar)
            act.inc(nc.scalar.activation(B["ecum"][:, :], B["cum"][:, :],
                                         AF.Exp))
            dve.wait(nc.tensor)
            act.wait(nc.tensor)
            ins = nc.tensor.matmul(p4[:, 2, 0:64], C["e127f"][:, :],
                                   B["cum"][:, :], start=True, stop=True)
            pe.inc(ins)
            pe.wait(nc.scalar)
            act.inc(nc.scalar.activation(B["decB"][:, :], p4[:, 2, 0:64],
                                         AF.Exp))
            act.wait(nc.vector)
            if g == 0 and cc == 0 and "dbgm_zg" in B:
                dbg = E["dbg"]
                dve.wait(nc.sync); act.wait(nc.sync); pe.wait(nc.sync)
                for nm, src in [("dbgm_xBCc", B["xBCc"][:, :, :]),
                                ("dbgm_xs", B["xs"][:, :, :]),
                                ("dbgm_CBT", B["CBT"][:, :]),
                                ("dbgm_Btm", B["Btm"][:, :]),
                                ("dbgm_dt", B["dt"][:, :]),
                                ("dbgm_cum", B["cum"][:, :]),
                                ("dbgm_ecum", B["ecum"][:, :]),
                                ("dbgm_decB", B["decB"][:, :])]:
                    dbg.inc(nc.sync.dma_start(out=B[nm], in_=src), 16)
                for eng in [nc.vector, nc.scalar, nc.tensor]:
                    dbg.wait(eng)
            for h in range(64):
                bb = h % 2
                dve.inc(nc.vector.tensor_scalar(
                    B["bb1"][:, :], C["onesf"][:, :], B["cum"][:, h:h + 1],
                    None, ALU.mult))
                dve.wait(nc.tensor)
                act.wait(nc.tensor)
                ins = nc.tensor.matmul(p4[:, 2 + bb, 0:128], B["bb1"][:, :],
                                       C["identf"][:, :], start=True, stop=True)
                pe.inc(ins)
                pe.wait(nc.vector)
                dve.inc(nc.vector.scalar_tensor_tensor(
                    B["diff"][:, :], p4[:, 2 + bb, 0:128],
                    B["ncum"][:, h:h + 1], C["maskneg"][:, :],
                    ALU.add, ALU.add))
                dve.wait(nc.scalar)
                act.inc(nc.scalar.activation(B["Lh"][:, :], B["diff"][:, :],
                                             AF.Exp))
                act.wait(nc.vector)
                dve.inc(nc.vector.scalar_tensor_tensor(
                    B["MT"][:, :], B["Lh"][:, :], B["dt"][:, h:h + 1],
                    B["CBT"][:, :], ALU.mult, ALU.mult))
                if g == 0 and cc == 0 and h == 0 and "dbgm_zg" in B:
                    dbg = E["dbg"]
                    dve.wait(nc.sync); act.wait(nc.sync); pe.wait(nc.sync)
                    dbg.inc(nc.sync.dma_start(out=B["dbgm_Lh"],
                                              in_=B["Lh"][:, :]), 16)
                    dbg.inc(nc.sync.dma_start(out=B["dbgm_MT"],
                                              in_=B["MT"][:, :]), 16)
                    for eng in [nc.vector, nc.scalar, nc.tensor]:
                        dbg.wait(eng)
                dve.wait(nc.vector)
                dve.inc(nc.vector.tensor_scalar(
                    B["wdc"][:, 0:1], B["Lh"][:, 127:128], B["dt"][:, h:h + 1],
                    None, ALU.mult))
                dve.wait(nc.vector)
                dve.inc(nc.vector.tensor_scalar(
                    B["xw"][:, :], B["xs"][:, :, :]
                    .rearrange("p a b -> p (a b)")[:, h * 64:(h + 1) * 64],
                    B["wdc"][:, 0:1], None, ALU.mult))
                dve.inc(nc.vector.tensor_copy(B["shbf"][:, :], B["S"][:, h, :]))
                dve.wait(nc.tensor)
                act.wait(nc.tensor)
                ins = nc.tensor.matmul(
                    p4[:, bb, 0:64], B["MT"][:, :],
                    B["xs"][:, :, :].rearrange("p a b -> p (a b)")
                    [:, h * 64:(h + 1) * 64], start=True, stop=True)
                pe.inc(ins)
                ins = nc.tensor.matmul(p4[:, 2 + bb, 0:64],
                                       B["xBCc"][:, 33, :], B["shbf"][:, :],
                                       start=True, stop=True)
                pe.inc(ins)
                ins = nc.tensor.matmul(p4[:, 2 + bb, 64:128], B["Btm"][:, :],
                                       B["xw"][:, :], start=True, stop=True)
                pe.inc(ins)
                pe.wait(nc.vector)
                dve.inc(nc.vector.tensor_copy(B["ys"][:, :], p4[:, bb, 0:64]))
                dve.wait(nc.vector)
                dve.inc(nc.vector.scalar_tensor_tensor(
                    B["ys"][:, :], p4[:, 2 + bb, 0:64], B["ecum"][:, h:h + 1],
                    B["ys"][:, :], ALU.mult, ALU.add))
                dve.wait(nc.vector)
                dve.inc(nc.vector.scalar_tensor_tensor(
                    B["yc"][:, h * 64:(h + 1) * 64],
                    B["xs"][:, :, :].rearrange("p a b -> p (a b)")
                    [:, h * 64:(h + 1) * 64], B["db"][:, h:h + 1],
                    B["ys"][:, :], ALU.mult, ALU.add))
                dve.inc(nc.vector.scalar_tensor_tensor(
                    B["S"][:, h, :], B["S"][:, h, :], B["decB"][:, h:h + 1],
                    p4[:, 2 + bb, 64:128], ALU.mult, ALU.add))
            if g == 0 and cc == 0 and "dbgm_zg" in B:
                dbg = E["dbg"]
                dve.wait(nc.sync); act.wait(nc.sync); pe.wait(nc.sync)
                dbg.inc(nc.sync.dma_start(out=B["dbgm_yc"],
                                          in_=B["yc"][:, :]), 16)
                dbg.inc(nc.sync.dma_start(out=B["dbgm_S"],
                                          in_=B["S"][:, :, :]), 16)
                for eng in [nc.vector, nc.scalar, nc.tensor]:
                    dbg.wait(eng)
            dve.wait(nc.scalar)
            pe.wait(nc.scalar)
            act.inc(nc.scalar.activation(
                B["zs"][:, :], B["zg"][:, cc, :], AF.Silu))
            act.wait(nc.vector)
            dve.inc(nc.vector.tensor_mul(B["yc"][:, :], B["yc"][:, :],
                                         B["zs"][:, :]))
            dve.wait(nc.scalar)
            act.inc(nc.scalar.activation(B["mjunk"][:, 0:HID], B["yc"][:, :],
                                         AF.Square, accum_out=lnt["sq"][:]))
            act.wait(nc.scalar)
            act.inc(nc.scalar.activation(lnt["sd"][:], lnt["sq"][:], AF.Sqrt,
                                         scale=1.0 / HID, bias=1e-5))
            act.wait(nc.vector)
            dve.inc(nc.vector.reciprocal(lnt["rstd"][:], lnt["sd"][:]))
            dve.wait(nc.vector)
            dve.inc(nc.vector.tensor_scalar(B["yc"][:, :], B["yc"][:, :],
                                            lnt["rstd"][:], None, ALU.mult))
            for q8 in range(4):
                dve.wait(nc.tensor)
                for k in range(8):
                    kk = q8 * 8 + k
                    ins = nc.tensor.transpose(
                        pt[:, q8 % 2, k, :],
                        B["yc"][:, kk * 128:(kk + 1) * 128], ident[:])
                pe.inc(ins)
                pe.wait(nc.vector)
                for k in range(8):
                    kk = q8 * 8 + k
                    ins = nc.vector.tensor_scalar(
                        B["yznT"][:, kk, cc, :], pt[:, q8 % 2, k, :],
                        B["nw"][:, kk:kk + 1], None, ALU.mult)
                dve.inc(ins)
        if g == 0 and "dbgm_zg" in B:
            dbg = E["dbg"]
            dve.wait(nc.sync); act.wait(nc.sync); pe.wait(nc.sync)
            dbg.inc(nc.sync.dma_start(out=B["dbgm_yznT"],
                                      in_=B["yznT"][:, :, :, :]), 16)
            for eng in [nc.vector, nc.scalar, nc.tensor]:
                dbg.wait(eng)
        for oc in range(4):
            P.maybe_chunk()
            for q in range(4):
                pe.wait(nc.sync)
                pe.wait(nc.gpsimd)
                act.wait(nc.sync)
                act.wait(nc.gpsimd)
                lds.inc(nc.sync.dma_start(
                    out=B["winb"][:, 0:4, :],
                    in_=wout_s[oc][:, q * 8:q * 8 + 4, :]), 16)
                gd.inc(nc.gpsimd.dma_start(
                    out=B["winb"][:, 4:8, :],
                    in_=wout_s[oc][:, q * 8 + 4:q * 8 + 8, :]), 16)
                lds.wait(nc.tensor)
                gd.wait(nc.tensor)
                if q == 0:
                    act.wait(nc.tensor)
                    dve.wait(nc.tensor)
                for cc in range(2):
                    pban = p4[:, cc, :]
                    for k in range(8):
                        ins = nc.tensor.matmul(
                            pban, B["yznT"][:, q * 8 + k, cc, :],
                            B["winb"][:, k, :], start=(q == 0 and k == 0),
                            stop=(q == 3 and k == 7), skip_group_check=True)
                pe.inc(ins)
            pe.wait(nc.vector)
            for cc in range(2):
                xap = xsb[:, g * 2 + cc, oc * 512:(oc + 1) * 512]
                dve.inc(nc.vector.scalar_tensor_tensor(
                    xap, p4[:, cc, :], 1.0, xap, ALU.mult, ALU.add))
        dve.inc(nc.vector.tensor_copy(B["xBCT"][:, :, 0:3],
                                      B["xBCT"][:, :, 256:259]))

    if io is not None:
        for t in range(NT):
            lds.wait(nc.vector)
            dve.inc(nc.vector.tensor_copy(B["xstage"][:, 0:D], xsb[:, t, :]))
            dve.wait(nc.sync)
            lds.inc(nc.sync.dma_start(out=io[1][t], in_=B["xstage"][:, 0:D]), 16)
        lds.wait(nc.sync)


def alloc_mamba_buffers(nc, B, al=None, compact=False):
    if al is None:
        al = lambda n, sh, dt: nc.alloc_sbuf_tensor(n, sh, dt).ap()
    a = lambda n, sh, dt: al(n, sh, dt)
    B["zg"] = a("zg", [128, 2, 4096], BF16)
    B["xBCT"] = a("xBCT", [128, 34, 259], BF16)
    B["dtg"] = a("dtg", [128, 2, 64], F32)
    B["winb"] = a("winb", [128, 9, 512], BF16)
    B["xBCc"] = a("xBCc", [128, 34, 128], BF16)
    B["xBCf"] = a("xBCf", [128, 17, 128], F32)
    B["xs"] = a("xs", [128, 32, 128], BF16)
    B["yc"] = a("yc", [128, 4096], BF16)
    B["S"] = a("S", [128, 64, 64], F32)
    B["yznT"] = a("yznT", [128, 32, 2, 128], BF16)
    if compact:
        B["mjunk"] = nc.alloc_sbuf_tensor_at(
            "mjunk", [128, 4096], F32,
            offset=nc.lookup_mloc(B["winb"].tensor).addr).ap()
    else:
        B["mjunk"] = B["xstage"]
    B["xTg"] = nc.alloc_sbuf_tensor_at(
        "xTg", [128, 2, 16, 128], BF16,
        offset=nc.lookup_mloc(B["yznT"].tensor).addr).ap()
    B["zs"] = nc.alloc_sbuf_tensor_at(
        "zs", [128, 4096], BF16,
        offset=nc.lookup_mloc(B["xBCf"].tensor).addr).ap()
    B["bstage"] = nc.alloc_sbuf_tensor_at(
        "bstage", [128, 512], BF16,
        offset=nc.lookup_mloc(B["xn"].tensor).addr).ap()
    for nm, sh, dt in [("CBT", [128, 128], BF16), ("Btm", [128, 128], BF16),
                       ("Lh", [128, 128], BF16), ("MT", [128, 128], BF16),
                       ("diff", [128, 128], F32), ("bb1", [128, 128], F32),
                       ("dt", [128, 64], F32), ("av", [128, 64], F32),
                       ("cum", [128, 64], F32), ("ncum", [128, 64], F32),
                       ("ecum", [128, 64], F32), ("decB", [128, 64], F32),
                       ("wdc", [128, 1], F32), ("xw", [128, 64], BF16),
                       ("ys", [128, 64], F32), ("shbf", [128, 64], BF16),
                       ("nw", [128, 32], F32), ("cw", [128, 34, 4], F32),
                       ("ad", [1, 128], F32), ("abc", [128, 64], F32),
                       ("db", [128, 64], F32)]:
        B[nm] = a("mb_" + nm, sh, dt)


def make_mamba_scratch(nc):
    return (make_wscratch(nc, "win_s", 2176, 8512),
            make_wscratch(nc, "wout_s", 4096, 2048))


def build_mamba_standalone(nseq=1):
    nc = bass.Bass("TRN2", target_bir_lowering=False)
    P = Prog(nc)
    x = nc.dram_tensor("x", [nseq, L, D], F32, kind="ExternalInput").ap()
    win_f = nc.dram_tensor("win_f", [2176, 8512], F32,
                           kind="ExternalInput").ap()
    wout_f = nc.dram_tensor("wout_f", [4096, 2048], F32,
                            kind="ExternalInput").ap()
    conv_w = nc.dram_tensor("conv_w", [4352, 4], F32,
                            kind="ExternalInput").ap()
    a_log = nc.dram_tensor("a_log", [64], F32, kind="ExternalInput").ap()
    d_in = nc.dram_tensor("d_in", [64], F32, kind="ExternalInput").ap()
    norm_w = nc.dram_tensor("norm_w", [4096], F32, kind="ExternalInput").ap()
    y = nc.dram_tensor("y", [nseq, L, D], F32, kind="ExternalOutput").ap()
    win_s, wout_s = make_mamba_scratch(nc)

    C = {}
    C["ident"] = nc.alloc_sbuf_tensor("ident", [128, 128], BF16).ap()
    C["identf"] = nc.alloc_sbuf_tensor("identf", [128, 128], F32).ap()
    C["onesf"] = nc.alloc_sbuf_tensor("onesf", [128, 128], F32).ap()
    C["triu"] = nc.alloc_sbuf_tensor("triu", [128, 128], F32).ap()
    C["maskneg"] = nc.alloc_sbuf_tensor("maskneg", [128, 128], F32).ap()
    C["e127f"] = nc.alloc_sbuf_tensor("e127f", [128, 128], F32).ap()
    C["e0mat"] = nc.alloc_sbuf_tensor("e0mat", [128, 128], BF16).ap()
    C["ones1"] = nc.alloc_sbuf_tensor("ones1", [1, 128], BF16).ap()
    B = {}
    B["xsb"] = nc.alloc_sbuf_tensor("xsb", [128, NT, D], BF16).ap()
    B["xn"] = nc.alloc_sbuf_tensor("xn", [128, D], BF16).ap()
    B["xstage"] = nc.alloc_sbuf_tensor("xstage", [128, 4096], F32).ap()
    lnt = {}
    for nm in ["sum", "mean", "m2", "sq", "var", "sd", "rstd"]:
        lnt[nm] = nc.alloc_sbuf_tensor("ln_" + nm, [128, 1], F32).ap()
    lnt["junk"] = B["xstage"]
    B["lnt"] = lnt
    alloc_mamba_buffers(nc, B)
    pcb = nc.alloc_sbuf_tensor_at(
        "pcbuf", [128, 2, 1024], BF16,
        offset=nc.lookup_mloc(B["xstage"].tensor).addr + 0).ap()

    PS = {}
    PS["p4"] = nc.alloc_psum_tensor("p4", [128, 4, 512], F32).ap()
    PS["pt"] = nc.alloc_psum_tensor("pt", [128, 2, 8, 128], BF16).ap()

    E = {k: P.sem(k) for k in ["pe", "dve", "act", "lds", "pl_ld", "pl_st",
                               "gps", "gd"]}
    reg_consts(nc, [1e-5, 1.0])
    emit_constants(P, C)
    E["gps"].inc(nc.gpsimd.memset(C["ones1"][:], 1.0))
    for eng in [nc.tensor, nc.vector, nc.scalar, nc.sync]:
        E["gps"].wait(eng)
    import os
    if os.environ.get("K_MB_DEBUG"):
        E["dbg"] = P.sem("dbg")
        for nm, sh, dt in [("dbgm_zg", [128, 2, 4096], BF16),
                           ("dbgm_xBCT", [128, 34, 259], BF16),
                           ("dbgm_dtg", [128, 2, 64], F32),
                           ("dbgm_xTg", [128, 2, 16, 128], BF16),
                           ("dbgm_xBCc", [128, 34, 128], BF16),
                           ("dbgm_xs", [128, 32, 128], BF16),
                           ("dbgm_CBT", [128, 128], BF16),
                           ("dbgm_Btm", [128, 128], BF16),
                           ("dbgm_dt", [128, 64], F32),
                           ("dbgm_cum", [128, 64], F32),
                           ("dbgm_ecum", [128, 64], F32),
                           ("dbgm_decB", [128, 64], F32),
                           ("dbgm_Lh", [128, 128], BF16),
                           ("dbgm_MT", [128, 128], BF16),
                           ("dbgm_yc", [128, 4096], BF16),
                           ("dbgm_S", [128, 64, 64], F32),
                           ("dbgm_yznT", [128, 32, 2, 128], BF16)]:
            ap = nc.dram_tensor(nm, sh, dt, kind="ExternalOutput").ap()
            B[nm] = ap[tuple(slice(None) for _ in sh)]
    emit_precast(P, E, [(win_f, win_s), (wout_f, wout_s)], pcb)
    emit_mamba_setup(P, E, C, B, PS, conv_w, a_log, d_in, norm_w)
    x4 = x.rearrange("s (t p) d -> s t p d", p=128)
    y4 = y.rearrange("s (t p) d -> s t p d", p=128)
    for sq in range(nseq):
        emit_mamba_seq(P, E, C, B, PS, win_s, wout_s, B["xsb"],
                       io=(x4[sq], y4[sq]))
    E["lds"].wait(nc.sync)
    return nc


def alloc_l0ffn_buffers(nc, B=None, al=None, compact=False):
    if al is None:
        al = lambda n, sh, dt: nc.alloc_sbuf_tensor(n, sh, dt).ap()
    if B is None:
        B = {}
        B["xsb"] = nc.alloc_sbuf_tensor("xsb", [128, NT, D], BF16).ap()
        B["xn"] = nc.alloc_sbuf_tensor("xn", [128, D], BF16).ap()
        B["xstage"] = nc.alloc_sbuf_tensor("xstage", [128, D], F32).ap()
        lnt = {}
        for nm in ["sum", "mean", "m2", "sq", "var", "sd", "rstd"]:
            lnt[nm] = nc.alloc_sbuf_tensor("ln_" + nm, [128, 1], F32).ap()
        lnt["junk"] = B["xstage"]
        B["lnt"] = lnt
    B["xT"] = al("xT", [128, GT, KD, 128], BF16)
    B["hsT"] = al("hsT", [128, GT, 64, 128], BF16)
    B["wg"] = al("f_wgb", [128, 16, 512], BF16)
    B["wu"] = al("f_wub", [128, 16, 512], BF16)
    if compact:
        B["wd"] = nc.alloc_sbuf_tensor_at(
            "f_wdb", [128, 8, 512], BF16,
            offset=nc.lookup_mloc(B["wg"].tensor).addr).ap()
    else:
        B["wd"] = al("f_wdb", [128, 8, 512], BF16)
    B["sg"] = al("sg", [128, 1, 512], F32)
    B["hst"] = al("hst", [128, 2, 512], BF16)
    B["hstf"] = al("hstf", [128, 1, 512], F32)
    B["habs"] = al("habs", [128, 1, 512], F32)
    B["hc"] = al("hc", [128, 6, 128], BF16)
    B["hr"] = al("hr", [128, 4, 128], BF16)
    return B


def build_l0ffn_standalone(nseq=2):
    nc = bass.Bass("TRN2", target_bir_lowering=False)
    P = Prog(nc)
    x = nc.dram_tensor("x", [nseq, L, D], F32, kind="ExternalInput").ap()
    wg_f = nc.dram_tensor("w_gate", [D, 8192], F32, kind="ExternalInput").ap()
    wu_f = nc.dram_tensor("w_up", [D, 8192], F32, kind="ExternalInput").ap()
    wd_f = nc.dram_tensor("w_down", [8192, D], F32, kind="ExternalInput").ap()
    y = nc.dram_tensor("y", [nseq, L, D], F32, kind="ExternalOutput").ap()
    wg_s = make_wscratch(nc, "wg_s", D, 8192)
    wu_s = make_wscratch(nc, "wu_s", D, 8192)
    wd_s = make_wscratch(nc, "wd_s", 8192, D)

    C = {}
    C["ident"] = nc.alloc_sbuf_tensor("ident", [128, 128], BF16).ap()
    C["ones1"] = nc.alloc_sbuf_tensor("ones1", [1, 128], BF16).ap()
    B = alloc_l0ffn_buffers(nc)
    xst_mloc = nc.lookup_mloc(B["xstage"].tensor)
    pcb = nc.alloc_sbuf_tensor_at("pcbuf", [128, 2, 1024], BF16,
                                  offset=xst_mloc.addr).ap()

    PS = {}
    PS["p4"] = nc.alloc_psum_tensor("p4", [128, 4, 512], F32).ap()
    PS["pt"] = nc.alloc_psum_tensor("pt", [128, 2, 8, 128], BF16).ap()

    E = {k: P.sem(k) for k in ["pe", "dve", "act", "lds", "pl_ld", "pl_st",
                               "gps", "gd"]}
    reg_consts(nc, [1e-5])
    emit_constants(P, C)
    E["gps"].inc(nc.gpsimd.memset(C["ones1"][:], 1.0))
    for eng in [nc.tensor, nc.vector, nc.scalar, nc.sync]:
        E["gps"].wait(eng)
    emit_precast(P, E, [(wg_f, wg_s), (wu_f, wu_s), (wd_f, wd_s)], pcb)

    import os
    if os.environ.get("K_DEBUG"):
        dbg_wg = nc.dram_tensor("dbg_wg", [128, 16, 512], BF16, kind="ExternalOutput").ap()
        dbg_xT = nc.dram_tensor("dbg_xT", [128, 16, 128], BF16, kind="ExternalOutput").ap()
        B["dbg_wg"] = dbg_wg[:, :, :]
        B["dbg_xT"] = dbg_xT[:, :, :]
        dbg_hstf = nc.dram_tensor("dbg_hstf", [128, 512], F32, kind="ExternalOutput").ap()
        dbg_hst = nc.dram_tensor("dbg_hst", [128, 512], BF16, kind="ExternalOutput").ap()
        B["dbg_hstf"] = dbg_hstf[:, :]
        B["dbg_hst"] = dbg_hst[:, :]
    x4 = x.rearrange("s (t p) d -> s t p d", p=128)
    y4 = y.rearrange("s (t p) d -> s t p d", p=128)
    for seq in range(nseq):
        emit_l0ffn_seq(P, E, C, B, PS, wg_s, wu_s, wd_s, B["xsb"],
                       io=(x4[seq], y4[seq]))
    E["lds"].wait(nc.sync)
    return nc


def emit_xsb_load(P, E, B, x4):
    nc = P.nc
    dve, lds = E["dve"], E["lds"]
    for t in range(NT):
        dve.wait(nc.sync)
        lds.inc(nc.sync.dma_start(out=B["xstage"][:, 0:D], in_=x4[t]), 16)
        lds.wait(nc.vector)
        dve.inc(nc.vector.tensor_copy(B["xsb"][:, t, :], B["xstage"][:, 0:D]))


def emit_xsb_store(P, E, B, y4):
    nc = P.nc
    dve, lds = E["dve"], E["lds"]
    for t in range(NT):
        lds.wait(nc.vector)
        dve.inc(nc.vector.tensor_copy(B["xstage"][:, 0:D], B["xsb"][:, t, :]))
        dve.wait(nc.sync)
        lds.inc(nc.sync.dma_start(out=y4[t], in_=B["xstage"][:, 0:D]), 16)
    lds.wait(nc.sync)


def build_fused(nseq=2):
    nc = bass.Bass("TRN2", target_bir_lowering=False)
    P = Prog(nc)
    x = nc.dram_tensor("x", [nseq, L, D], F32, kind="ExternalInput").ap()
    y = nc.dram_tensor("y", [nseq, L, D], F32, kind="ExternalOutput").ap()
    wg_f = nc.dram_tensor("w_gate", [D, 8192], F32, kind="ExternalInput").ap()
    wu_f = nc.dram_tensor("w_up", [D, 8192], F32, kind="ExternalInput").ap()
    wd_f = nc.dram_tensor("w_down", [8192, D], F32, kind="ExternalInput").ap()
    win_f = nc.dram_tensor("win_f", [2176, 8512], F32,
                           kind="ExternalInput").ap()
    wout_f = nc.dram_tensor("wout_f", [4096, 2048], F32,
                            kind="ExternalInput").ap()
    conv_w = nc.dram_tensor("conv_w", [4352, 4], F32,
                            kind="ExternalInput").ap()
    a_log = nc.dram_tensor("a_log", [64], F32, kind="ExternalInput").ap()
    d_in = nc.dram_tensor("d_in", [64], F32, kind="ExternalInput").ap()
    norm_w = nc.dram_tensor("norm_w", [4096], F32, kind="ExternalInput").ap()
    wq_f = nc.dram_tensor("wqT", [D, D], F32, kind="ExternalInput").ap()
    wkv_f = nc.dram_tensor("wkvT", [D, 256], F32, kind="ExternalInput").ap()
    wo_f = nc.dram_tensor("woT", [D, D], F32, kind="ExternalInput").ap()
    nsc_f = nc.dram_tensor("nsc", [128, 16], F32, kind="ExternalInput").ap()
    ssc_f = nc.dram_tensor("ssc", [128, 16], F32, kind="ExternalInput").ap()
    mnames = ["wg_f", "wup_f", "wsw_f", "wdn_f", "wsu_f", "wsa_f", "wsd_f"]
    mshapes = [(17 * 128, 16), (17 * 128, 6144), (9 * 128, 6144),
               (8 * 128, 12288), (17 * 128, 2048), (9 * 128, 1024),
               (17 * 128, 2048)]
    fins = {nm: nc.dram_tensor(nm, list(sh), F32, kind="ExternalInput").ap()
            for nm, sh in zip(mnames, mshapes)}
    bdn_f = nc.dram_tensor("bdn_f", [8, D], F32, kind="ExternalInput").ap()

    wg_s = make_wscratch(nc, "fwg_s", D, 8192)
    wu_s = make_wscratch(nc, "fwu_s", D, 8192)
    wd_s = make_wscratch(nc, "fwd_s", 8192, D)
    win_s, wout_s = make_mamba_scratch(nc)
    wq_s = make_wscratch(nc, "wq_s", D, D)
    wkv_s = make_wscratch(nc, "wkv_s", D, 256, oc=256)
    wo_s = make_wscratch(nc, "wo_s", D, D)
    W = make_moe_scratch(nc)

    C = {}
    C["ident"] = nc.alloc_sbuf_tensor("ident", [128, 128], BF16).ap()
    C["ones1"] = nc.alloc_sbuf_tensor("ones1", [1, 128], BF16).ap()
    C["e0mat"] = nc.alloc_sbuf_tensor("e0mat", [128, 128], BF16).ap()
    C["lmask8"] = nc.alloc_sbuf_tensor("lmask8", [128, 8, 8], F32).ap()
    B = {}
    B["xsb"] = nc.alloc_sbuf_tensor("xsb", [128, NT, D], BF16).ap()
    B["xn"] = nc.alloc_sbuf_tensor("xn", [128, D], BF16).ap()
    B["xstage"] = nc.alloc_sbuf_tensor("xstage", [128, D], F32).ap()
    lnt = {}
    for nm in ["sum", "mean", "m2", "sq", "var", "sd", "rstd"]:
        lnt[nm] = nc.alloc_sbuf_tensor("ln_" + nm, [128, 1], F32).ap()
    lnt["junk"] = B["xstage"]
    B["lnt"] = lnt
    pcb = nc.alloc_sbuf_tensor_at(
        "pcbuf", [128, 2, 1024], BF16,
        offset=nc.lookup_mloc(B["xstage"].tensor).addr).ap()
    E = {k: P.sem(k) for k in ["pe", "dve", "act", "lds", "pl_ld", "pl_st",
                               "gps", "gd"]}
    reg_consts(nc, [1e-5, 1.0])

    ar = None
    for kb in range(133, 100, -1):
        try:
            ar = Arena(nc, kb * 1024)
            break
        except Exception:
            continue
    assert ar is not None, "no room for arena"

    Bf = dict(B)
    alloc_l0ffn_buffers(nc, Bf, ar.scope("A"), compact=True)
    Ba = dict(B)
    alloc_attn_buffers(nc, Ba, ar.scope("B"))
    Bm = dict(B)
    alloc_moe_buffers(nc, Bm, ar.scope("C"))
    Bs = dict(B)
    alloc_mamba_buffers(nc, Bs, ar.scope("D"), compact=True)
    Cm = dict(C)
    for nm in ["identf", "onesf", "triu", "maskneg", "e127f"]:
        Cm[nm] = ar("c_" + nm, [128, 128], F32)

    PS = {}
    PS["p4"] = nc.alloc_psum_tensor("p4", [128, 4, 512], F32).ap()
    PS["pt"] = nc.alloc_psum_tensor("pt", [128, 2, 8, 128], BF16).ap()
    PS["pav"] = nc.alloc_psum_tensor("pav", [128, 512], F32).ap()

    emit_constants(P, C)
    E["gps"].inc(nc.gpsimd.memset(C["ones1"][:], 1.0))
    for eng in [nc.tensor, nc.vector, nc.scalar, nc.sync]:
        E["gps"].wait(eng)
    emit_precast(P, E, [(wg_f, wg_s), (wu_f, wu_s), (wd_f, wd_s),
                        (win_f, win_s), (wout_f, wout_s),
                        (wq_f, wq_s), (wkv_f, wkv_s), (wo_f, wo_s)] +
                 [(fins[nm], W[nm.split("_")[0]]) for nm in mnames], pcb)

    def mamba_pre():
        sub = {k: Cm[k] for k in ["identf", "onesf", "triu", "maskneg",
                                  "e127f"]}
        emit_constants(P, sub)
        E["gps"].inc(nc.gpsimd.memset(Cm["onesf"][0:1, 0:1], 1.0))
        for eng in [nc.tensor, nc.vector, nc.scalar]:
            E["gps"].wait(eng)
        emit_mamba_setup(P, E, Cm, Bs, PS, conv_w, a_log, d_in, norm_w)

    def moe_pre():
        E["pl_ld"].inc(nc.gpsimd.dma_start(out=Bm["bdn_sb"][:],
                                           in_=bdn_f[:, :]), 16)
        E["pl_ld"].wait(nc.tensor)
        E["pl_ld"].wait(nc.vector)

    def attn_pre():
        E["lds"].inc(nc.sync.dma_start(out=Ba["nsc"][:], in_=nsc_f[:, :]), 16)
        E["lds"].inc(nc.sync.dma_start(out=Ba["ssc"][:], in_=ssc_f[:, :]), 16)
        for eng in [nc.tensor, nc.vector, nc.scalar]:
            E["lds"].wait(eng)

    x4 = x.rearrange("s (t p) d -> s t p d", p=128)
    y4 = y.rearrange("s (t p) d -> s t p d", p=128)
    for sq in range(nseq):
        emit_xsb_load(P, E, B, x4[sq])
        emit_sublayer_barrier(P, E)
        emit_l0ffn_seq(P, E, C, Bf, PS, wg_s, wu_s, wd_s, B["xsb"])
        emit_sublayer_barrier(P, E)
        mamba_pre()
        emit_mamba_seq(P, E, Cm, Bs, PS, win_s, wout_s, B["xsb"])
        emit_sublayer_barrier(P, E)
        emit_l0ffn_seq(P, E, C, Bf, PS, wg_s, wu_s, wd_s, B["xsb"])
        emit_sublayer_barrier(P, E)
        moe_pre()
        emit_moe_seq(P, E, C, Bm, PS, W, B["xsb"])
        emit_sublayer_barrier(P, E)
        attn_pre()
        emit_attn_seq(P, E, C, Ba, PS, wq_s, wkv_s, wo_s, B["xsb"])
        emit_sublayer_barrier(P, E)
        moe_pre()
        emit_moe_seq(P, E, C, Bm, PS, W, B["xsb"])
        emit_sublayer_barrier(P, E)
        emit_xsb_store(P, E, B, y4[sq])
    E["lds"].wait(nc.sync)
    return nc


def kernel(**inputs):
    x = np.ascontiguousarray(inputs["L0_attn_x"], dtype=np.float32)

    w_in = inputs["L0_attn_W_in"]
    bias_row = np.zeros((1, 8512), np.float32)
    bias_row[0, 8448:8512] = inputs["L0_attn_dt_bias"]
    win_f = np.concatenate([np.ascontiguousarray(w_in.T), bias_row,
                            np.zeros((127, 8512), np.float32)], axis=0)
    wout_T = np.ascontiguousarray(inputs["L0_attn_W_out"].T)
    mamba_in = {"win_f": win_f, "wout_f": wout_T,
                "conv_w": np.ascontiguousarray(inputs["L0_attn_conv_w"]),
                "a_log": np.ascontiguousarray(inputs["L0_attn_A_log"]),
                "d_in": np.ascontiguousarray(inputs["L0_attn_D"]),
                "norm_w": np.ascontiguousarray(inputs["L0_attn_norm_w"])}
    ffn_in = {"w_gate": np.ascontiguousarray(inputs["L0_ffn_w_gate"]),
              "w_up": np.ascontiguousarray(inputs["L0_ffn_w_up"]),
              "w_down": np.ascontiguousarray(inputs["L0_ffn_w_down"])}
    attn_in = {
        "wqT": np.ascontiguousarray(inputs["L1_attn_Wq"].T),
        "wkvT": np.ascontiguousarray(np.concatenate(
            [inputs["L1_attn_Wk"].T, inputs["L1_attn_Wv"].T], axis=1)),
        "woT": np.ascontiguousarray(inputs["L1_attn_Wo"].T),
        "nsc": np.ascontiguousarray(
            inputs["L1_attn_norm_scale"].reshape(16, 128).T),
        "ssc": np.ascontiguousarray(
            inputs["L1_attn_split_scale"].reshape(16, 128).T)}
    hw = moe_host_weights(
        inputs["L1_ffn_Wg"], inputs["L1_ffn_bg"], inputs["L1_ffn_Wup"],
        inputs["L1_ffn_bup"], inputs["L1_ffn_Wsw"], inputs["L1_ffn_bsw"],
        inputs["L1_ffn_Wdn"], inputs["L1_ffn_bdn"], inputs["L1_ffn_Wsu"],
        inputs["L1_ffn_bsu"], inputs["L1_ffn_Wsa"], inputs["L1_ffn_bsa"],
        inputs["L1_ffn_Wsd"], inputs["L1_ffn_bsd"])
    moe_in = {nm: hw[nm] for nm in ["wg_f", "wup_f", "wsw_f", "wdn_f",
                                    "wsu_f", "wsa_f", "wsd_f", "bdn_f"]}

    prog = build_fused(2)
    wall = {}
    wall.update(ffn_in)
    wall.update(mamba_in)
    wall.update(attn_in)
    wall.update(moe_in)
    per_core = []
    for c in range(NCORES):
        d = {"x": x[2 * c:2 * c + 2]}
        d.update(wall)
        per_core.append(d)
    res = run_bass_kernel_spmd(prog, per_core, core_ids=list(range(NCORES)))
    out = np.concatenate(
        [np.asarray(res.results[c]["y"], dtype=np.float32)
         for c in range(NCORES)], axis=0)
    return out.reshape(16, L, D)
